# version 13
# baseline (speedup 1.0000x reference)
import contextlib
import math

import numpy as np
import ml_dtypes

import concourse.bass as bass
import concourse.mybir as mybir
from concourse.bass_utils import run_bass_kernel_spmd

F32 = mybir.dt.float32
BF16 = mybir.dt.bfloat16
I32 = mybir.dt.int32
AF = mybir.ActivationFunctionType
ALU = mybir.AluOpType
AX = mybir.AxisListType.X
bf = ml_dtypes.bfloat16

HS = 1024
H, D = 8, 128
CHUNK, CTX = 12, 24
PP = CTX + 1
SOFTCAP = 50.0
Q_SCALE = D ** (-0.5) / math.log(2)
K_SCALE = math.log(1.0 + math.e) / math.log(2)
QK = Q_SCALE * K_SCALE
EPS = 1e-5
FF = 4096

NBLK = 171
TQ = NBLK * CHUNK
HALO = 12
SHARD = 2176
NCHUNK = 17

SUBB = 9
SUBT = SUBB * CHUNK
NSUB = NBLK // SUBB
KTW = SUBT + 12
SUPER = [(0, 4), (4, 4), (8, 4), (12, 4), (16, 3)]

CSTRIDE = SUBT * PP
BSTRIDE = SUBB * (CHUNK * KTW + CHUNK)
SUB_C = H * CSTRIDE
SUB_B = H * BSTRIDE

BTILES = [(k * 128, min(128, TQ - k * 128)) for k in range((TQ + 127) // 128)]
NT = len(BTILES)
DEBUG_TAPS = False


def build_nc():
    nc = bass.Bass("TRN2", target_bir_lowering=False, debug=False)

    d_x = nc.dram_tensor("xsh", [SHARD, HS], F32, kind="ExternalInput").ap()
    d_wq = nc.dram_tensor("wq", [HS, HS], F32, kind="ExternalInput").ap()
    d_wk = nc.dram_tensor("wk", [HS, HS], F32, kind="ExternalInput").ap()
    d_wv = nc.dram_tensor("wv", [HS, HS], F32, kind="ExternalInput").ap()
    d_wp = nc.dram_tensor("wpost", [HS, HS], F32, kind="ExternalInput").ap()
    d_wr = nc.dram_tensor("wrel", [HS, HS], F32, kind="ExternalInput").ap()
    d_pt = nc.dram_tensor("posT", [HS, PP], F32, kind="ExternalInput").ap()
    d_pds = nc.dram_tensor("pds", [D], F32, kind="ExternalInput").ap()
    d_w1t = nc.dram_tensor("w1T", [HS, FF], F32, kind="ExternalInput").ap()
    d_w2t = nc.dram_tensor("w2T", [FF, HS], F32, kind="ExternalInput").ap()
    d_b1 = nc.dram_tensor("b1", [FF], F32, kind="ExternalInput").ap()
    d_b2 = nc.dram_tensor("b2", [HS], F32, kind="ExternalInput").ap()
    d_cst = nc.dram_tensor("cst", [128, 8], F32, kind="ExternalInput").ap()
    d_idb = nc.dram_tensor("identb", [128, 128], BF16, kind="ExternalInput").ap()
    d_oc_b = nc.dram_tensor("onescol_b", [1, 128], BF16, kind="ExternalInput").ap()
    d_oc_f = nc.dram_tensor("onescol_f", [1, 128], F32, kind="ExternalInput").ap()
    d_o128f = nc.dram_tensor("ones128_f", [128, 1], F32, kind="ExternalInput").ap()
    d_o120b = nc.dram_tensor("ones120_b", [128, 1], BF16, kind="ExternalInput").ap()
    d_mask = nc.dram_tensor("mask8", [KTW, H * SUBT], BF16, kind="ExternalInput").ap()
    d_out = nc.dram_tensor("out", [TQ, HS], F32, kind="ExternalOutput").ap()

    d_xp = nc.dram_tensor("xp_scr", [TQ * HS], BF16, kind="Internal").ap()
    d_bdc = nc.dram_tensor("bdc_scr", [2 * SUB_C], F32, kind="Internal").ap()
    d_bdb = nc.dram_tensor("bdz", [2 * SUB_B], F32, kind="ExternalInput").ap()
    d_wq1 = nc.dram_tensor("wq1_scr", [HS * FF], BF16, kind="Internal").ap()
    d_wq2 = nc.dram_tensor("wq2_scr", [FF * HS], BF16, kind="Internal").ap()
    d_hq = nc.dram_tensor("hq_scr", [NT * 128 * FF], BF16, kind="Internal").ap()
    d_wscr = nc.dram_tensor("wscr_scr", [512], F32, kind="Internal").ap()
    d_txq = (nc.dram_tensor("tapxq_scr", [NT * 128 * HS], BF16,
                            kind="Internal").ap() if DEBUG_TAPS else None)
    d_tyh = (nc.dram_tensor("tapyh_scr", [NT * 128 * FF], F32,
                            kind="Internal").ap() if DEBUG_TAPS else None)

    es = contextlib.ExitStack()
    with es:
        es.enter_context(nc.allow_low_precision(
            reason="bf16 matmul operands by design; fp32 accumulation kept"))
        SEMS = {}
        for _k in ["pe", "dve", "act"]:
            SEMS[_k] = es.enter_context(nc.semaphore("s_" + _k))
        for _q in ["gp", "hw", "ad"]:
            for _i in range(4):
                SEMS[_q + str(_i)] = es.enter_context(
                    nc.semaphore("s_%s%d" % (_q, _i)))
        SEMS["pl"] = es.enter_context(nc.semaphore("s_pl"))
        C = {k: 0 for k in SEMS}
        DMAQ = {"gp": {"idx": 0, "last": None}, "hw": {"idx": 0, "last": None},
                "ad": {"idx": 0, "last": None}}

        class _DmaQ:
            def __init__(self, eng, q):
                self._e = eng
                self._q = q

            def dma_start(self, *a, **kw):
                q = DMAQ[self._q]
                k = self._q + str(q["idx"] % 4)
                q["idx"] += 1
                self._e.wait_ge(SEMS[k], C[k])
                ins = self._e.dma_start(*a, **kw)
                ins.then_inc(SEMS[k], 16)
                C[k] += 16
                q["last"] = (k, C[k])
                return ins

            def __getattr__(self, name):
                return getattr(self._e, name)

        def GPS(f):
            blk.gpsimd(lambda g: f(_DmaQ(g, "gp")))

        def SYN(f):
            blk.sync(lambda y: f(_DmaQ(y, "hw")))

        def SCD(f):
            blk.scalar(lambda s2: f(_DmaQ(s2, "ad")))

        def fin(ins, k):
            if k in ("dve", "act", "pl"):
                return (k, C[k])
            if k in ("gp", "hw", "ad"):
                return DMAQ[k]["last"]
            ins.then_inc(SEMS[k], 1)
            C[k] += 1
            return (k, C[k])

        def w8(eng, *toks):
            best = {}
            for t in toks:
                if t is None:
                    continue
                k, v = t
                if v > 0 and v > best.get(k, 0):
                    best[k] = v
            for k, v in best.items():
                eng.wait_ge(SEMS[k], v)

        def now():
            return [(k, C[k]) for k in C]

        _CHAIN_OPS = {
            "activation", "tensor_copy", "tensor_tensor", "tensor_add",
            "tensor_sub", "tensor_mul", "tensor_max", "tensor_scalar",
            "tensor_single_scalar", "tensor_scalar_add", "tensor_scalar_sub",
            "tensor_scalar_mul", "tensor_scalar_max", "tensor_scalar_min",
            "tensor_relu", "tensor_reduce", "reduce_max", "reduce_sum",
            "scalar_tensor_tensor", "reciprocal", "memset", "memzero",
            "select", "copy_predicated", "sign", "square", "sqrt", "mul",
            "add", "copy",
        }

        class _Chain:
            def __init__(self, eng, key):
                self._e = eng
                self._k = key

            def __getattr__(self, name):
                f = getattr(self._e, name)
                if name in _CHAIN_OPS:
                    k = self._k

                    def g(*a, **kw):
                        self._e.wait_ge(SEMS[k], C[k])
                        ins = f(*a, **kw)
                        ins.then_inc(SEMS[k], 1)
                        C[k] += 1
                        return ins
                    return g
                return f

        def VEC(f):
            blk.vector(lambda v: f(_Chain(v, "dve")))

        def SCL(f):
            blk.scalar(lambda s: f(_Chain(s, "act")))

        def POOL(f):
            blk.gpsimd(lambda g: f(_Chain(_DmaQ(g, "gp"), "pl")))

        cst = es.enter_context(nc.sbuf_tensor("cst_s", [128, 8], F32))
        idb = es.enter_context(nc.sbuf_tensor("idb_s", [128, 128], BF16))
        oc_b = es.enter_context(nc.sbuf_tensor("ocb_s", [1, 128], BF16))
        oc_f = es.enter_context(nc.sbuf_tensor("ocf_s", [1, 128], F32))
        o128f = es.enter_context(nc.sbuf_tensor("o128f_s", [128, 1], F32))
        o120b = es.enter_context(nc.sbuf_tensor("o120b_s", [128, 1], BF16))
        maskb = es.enter_context(nc.sbuf_tensor("mask_s", [KTW, H * SUBT], BF16))
        b1r = es.enter_context(nc.sbuf_tensor("b1r_s", [1, FF], BF16))
        b2r = es.enter_context(nc.sbuf_tensor("b2r_s", [1, HS], BF16))
        qsc = es.enter_context(nc.sbuf_tensor("qsc_s", [128, 1], F32))
        st = es.enter_context(nc.sbuf_tensor("st_s", [128, 64], F32))
        absx = es.enter_context(nc.sbuf_tensor("absx_s", [128, NT], F32))
        absh = es.enter_context(nc.sbuf_tensor("absh_s", [128, NT], F32))
        mu1b = es.enter_context(nc.sbuf_tensor("mu1b_s", [128, 1], F32))
        mu2b = es.enter_context(nc.sbuf_tensor("mu2b_s", [128, 1], F32))

        PSA = es.enter_context(nc.psum_tensor("psA", [128, 512], F32))
        PSB = es.enter_context(nc.psum_tensor("psB", [128, 512], F32))
        PSC = es.enter_context(nc.psum_tensor("psC", [128, 512], F32))
        PSD = es.enter_context(nc.psum_tensor("psD", [128, 512], F32))
        PT1 = es.enter_context(nc.psum_tensor("psT1", [128, 1024], BF16))
        PT2 = es.enter_context(nc.psum_tensor("psT2", [128, 1024], BF16))
        PT3 = es.enter_context(nc.psum_tensor("psT3", [128, 1024], BF16))
        PT4 = es.enter_context(nc.psum_tensor("psT4", [128, 1024], BF16))
        PTS = [PT1, PT2, PT3, PT4]

        blk = es.enter_context(nc.Block())

        AP_HALF = cst[:, 0:1]
        AP_NHALF = cst[:, 1:2]
        AP_ONE = cst[:, 2:3]
        AP_EPS = cst[:, 3:4]

        def pingpong(n, slots, pe_emit, ev_emit, ev_eng, pre_deps, prev_users):
            toks = []
            ns = len(slots)
            for w0 in range(0, n, ns):
                wN = min(n, w0 + ns)

                def _t(t, w0=w0, wN=wN):
                    for i in range(w0, wN):
                        if i < ns:
                            w8(t, *pre_deps)
                            w8(t, *prev_users)
                        else:
                            w8(t, toks[i - ns][1])
                        last = pe_emit(t, i, slots[i % ns])
                        toks.append([fin(last, "pe"), None])
                blk.tensor(_t)

                def _e(e, w0=w0, wN=wN):
                    for i in range(w0, wN):
                        w8(e, toks[i][0])
                        toks[i][1] = fin(ev_emit(e, i, slots[i % ns]), ev_eng)
                (SCL if ev_eng == "act" else VEC)(_e)
            return toks

        tok_const = {}

        def _(g):
            tok_const["cst"] = fin(g.dma_start(cst[:], d_cst), "gp")
            tok_const["idb"] = fin(g.dma_start(idb[:], d_idb), "gp")
            tok_const["oc_b"] = fin(g.dma_start(oc_b[:], d_oc_b), "gp")
            tok_const["oc_f"] = fin(g.dma_start(oc_f[:], d_oc_f), "gp")
            tok_const["o128f"] = fin(g.dma_start(o128f[:], d_o128f), "gp")
            tok_const["o120b"] = fin(g.dma_start(o120b[:], d_o120b), "gp")
            tok_const["mask"] = fin(g.dma_start(maskb[:], d_mask), "gp")
            tok_const["pds"] = fin(g.dma_start(
                qsc[:], bass.AP(d_pds.tensor, 0, [[1, 128], [1, 1]])), "gp")
            tok_const["b1"] = fin(g.dma_start(b1r[:], d_b1.unsqueeze(0)), "gp")
            tok_const["b2"] = fin(g.dma_start(b2r[:], d_b2.unsqueeze(0)), "gp")
        GPS(_)

        def emit_ln_sq(X, cnt, scratch, deps, co=0, c_extra=False):
            W = X.shape[-1]
            ta, td = [], []

            def sl(a, b=None):
                return st[0:cnt, co + a:co + (b if b is not None else a + 1)]

            def _(sc):
                w8(sc, *deps)
                ta.append(fin(sc.activation(scratch[0:cnt, 0:W], X, AF.Square,
                                            accum_out=sl(1)), "act"))
            SCL(_)

            def _(v):
                w8(v, *deps)
                v.reduce_sum(sl(0), X, axis=AX)
                v.tensor_scalar_mul(sl(2), sl(0), 1.0 / W)
                v.tensor_mul(sl(3), sl(2), sl(2))
                w8(v, ta[0])
                v.tensor_scalar(sl(4), sl(1), 1.0 / W, None, ALU.mult)
                last = v.tensor_sub(sl(4), sl(4), sl(3))
                if c_extra:
                    v.tensor_scalar_add(sl(9), sl(4), EPS)
                    v.reciprocal(sl(10), sl(9))
                    last = v.tensor_mul(sl(11), sl(4), sl(10))
                td.append(fin(last, "dve"))
            VEC(_)
            return ta[0], td[0]

        def emit_ln_fin(X, cnt, d_tok, co=0, c_extra=False, apply_out=None,
                        deps=()):
            ta = []

            def sl(a, b=None):
                return st[0:cnt, co + a:co + (b if b is not None else a + 1)]

            def _(sc):
                w8(sc, d_tok, *deps)
                sc.activation(sl(5), sl(4), AF.Ln, bias=cst[0:cnt, 3:4])
                sc.activation(sl(6), sl(5), AF.Exp, scale=-0.5)
                sc.activation(sl(7), sl(6), AF.Identity, scale=-1.0)
                last = sc.activation(sl(8), sl(2), AF.Identity, scale=sl(7))
                if c_extra:
                    sc.activation(sl(12), sl(11), AF.Ln, bias=cst[0:cnt, 3:4])
                    sc.activation(sl(13), sl(12), AF.Exp, scale=-0.5)
                    sc.activation(sl(15), sl(6), AF.Identity, scale=sl(13))
                    sc.activation(sl(16), sl(15), AF.Identity, scale=-1.0)
                    last = sc.activation(sl(14), sl(2), AF.Identity,
                                         scale=sl(16))
                if apply_out is not None:
                    if c_extra:
                        last = sc.activation(apply_out, X, AF.Identity,
                                             scale=sl(15), bias=sl(14))
                    else:
                        last = sc.activation(apply_out, X, AF.Identity,
                                             scale=sl(6), bias=sl(8))
                ta.append(fin(last, "act"))
            SCL(_)
            return ta[0]

        def emit_ln_stats(X, cnt, scratch, deps, c_extra=False, co=0,
                          apply_out=None):
            sq_tok, d_tok = emit_ln_sq(X, cnt, scratch, deps, co=co,
                                       c_extra=c_extra)
            a_tok = emit_ln_fin(X, cnt, d_tok, co=co, c_extra=c_extra,
                                apply_out=apply_out)
            return sq_tok, a_tok, d_tok

        def emit_round(Y, Q0I, OUT, deps):
            tk = {}

            def _(v):
                w8(v, *deps)
                v.tensor_copy(Q0I, Y)
                tk["q"] = fin(v.tensor_copy(OUT, Q0I), "dve")
            VEC(_)
            return tk["q"]

        with contextlib.ExitStack() as esA:
            wq_s = esA.enter_context(nc.sbuf_tensor("wq_s", [128, 8, HS], BF16))
            wk_s = esA.enter_context(nc.sbuf_tensor("wk_s", [128, 8, HS], BF16))
            wv_s = esA.enter_context(nc.sbuf_tensor("wv_s", [128, 8, HS], BF16))
            wp_s = esA.enter_context(nc.sbuf_tensor("wp_s", [128, 8, HS], BF16))
            rel_s = esA.enter_context(nc.sbuf_tensor("rel_s", [128, 8, PP], BF16))
            tok_bdb_zero = None

            def _(g):
                for nm, dst, srct in [("wq", wq_s, d_wq), ("wk", wk_s, d_wk),
                                      ("wv", wv_s, d_wv), ("wp", wp_s, d_wp)]:
                    tok_const[nm] = fin(g.dma_start(
                        bass.AP(dst, 0, [[8 * HS, 128], [HS, 8], [1, HS]]),
                        bass.AP(srct.tensor, 0,
                                [[HS, 128], [128 * HS, 8], [1, HS]])), "gp")
            GPS(_)

            t_qs = []

            def _(sc):
                w8(sc, tok_const["pds"], tok_const["cst"])
                sc.activation(st[:, 16:17], qsc[:], AF.Abs)
                sc.activation(st[:, 17:18], st[:, 16:17], AF.Exp, scale=-1.0)
                t_qs.append(fin(sc.activation(st[:, 18:19], st[:, 17:18], AF.Ln,
                                              bias=AP_ONE), "act"))
            SCL(_)

            def _(v):
                w8(v, t_qs[0])
                v.tensor_scalar_max(st[:, 19:20], qsc[:], 0.0)
                v.tensor_add(st[:, 20:21], st[:, 19:20], st[:, 18:19])
                t_qs.append(fin(v.tensor_scalar_mul(qsc[:], st[:, 20:21], QK), "dve"))
            VEC(_)


            with contextlib.ExitStack() as esR:
                wr_s = esR.enter_context(
                    nc.sbuf_tensor("wr_s", [128, 8, HS], BF16))
                pt_s = esR.enter_context(
                    nc.sbuf_tensor("pt_s", [128, 8, PP], BF16))

                def _(g):
                    tok_const["wr"] = fin(g.dma_start(
                        bass.AP(wr_s, 0, [[8 * HS, 128], [HS, 8], [1, HS]]),
                        bass.AP(d_wr.tensor, 0,
                                [[HS, 128], [128 * HS, 8], [1, HS]])), "gp")
                    tok_const["pt"] = fin(g.dma_start(
                        bass.AP(pt_s, 0, [[8 * PP, 128], [PP, 8], [1, PP]]),
                        bass.AP(d_pt.tensor, 0,
                                [[PP, 128], [128 * PP, 8], [1, PP]])), "gp")
                GPS(_)

                def _pe_rel(t, i, ps):
                    for dc in range(8):
                        last = t.matmul(ps[0:128, 0:PP],
                                        wr_s[:, dc, i * 128:(i + 1) * 128],
                                        pt_s[:, dc, :], start=(dc == 0),
                                        stop=(dc == 7))
                    return last

                def _ev_rel(sc, i, ps):
                    return sc.activation(rel_s[:, i, :], ps[0:128, 0:PP],
                                         AF.Identity, scale=1.0 / K_SCALE)

                t_rel = pingpong(H, [PSA, PSB, PSC, PSD], _pe_rel, _ev_rel,
                                 "act", [tok_const["wr"], tok_const["pt"]], [])
                tok_rel = t_rel[H - 1][1]

            wcr = [esA.enter_context(
                nc.sbuf_tensor("wcr%d_s" % i, [128, 512], F32))
                for i in range(4)]
            sg1w = [esA.enter_context(
                nc.sbuf_tensor("sg1%d_s" % i, [128, 512], BF16))
                for i in range(2)]
            sg2w = [esA.enter_context(
                nc.sbuf_tensor("sg2%d_s" % i, [128, 512], BF16))
                for i in range(2)]
            wqow = [esA.enter_context(
                nc.sbuf_tensor("wqo%d_s" % i, [128, 512], BF16))
                for i in range(2)]
            asum = esA.enter_context(nc.sbuf_tensor("asum_s", [128, 66], F32))
            thbw = esA.enter_context(nc.sbuf_tensor("thb_s", [128, 8], F32))
            thrw = esA.enter_context(nc.sbuf_tensor("thr_s", [1, 8], F32))
            prow = esA.enter_context(nc.sbuf_tensor("prow_s", [1, 256], F32))

            def _w_src(wi, u):
                if wi == 0:
                    c2, q2 = divmod(u, 8)
                    return bass.AP(d_w1t.tensor, c2 * 128 * FF + q2 * 512,
                                   [[FF, 128], [1, 512]])
                c2, q2 = divmod(u, 2)
                return bass.AP(d_w2t.tensor, c2 * 128 * HS + q2 * 512,
                               [[HS, 128], [1, 512]])

            def _w_dst(wi, u):
                if wi == 0:
                    c2, q2 = divmod(u, 8)
                    return bass.AP(d_wq1.tensor, c2 * 128 * FF + q2 * 512,
                                   [[FF, 128], [1, 512]])
                c2, q2 = divmod(u, 2)
                return bass.AP(d_wq2.tensor, c2 * 128 * HS + q2 * 512,
                               [[HS, 128], [1, 512]])

            WUNITS = ([("scan", 0, u) for u in range(64)] + [("fin", 0, 0)] +
                      [("sign", 0, u) for u in range(64)] +
                      [("scan", 1, u) for u in range(64)] + [("fin", 1, 0)] +
                      [("sign", 1, u) for u in range(64)])
            WLOADS = [un for un in WUNITS if un[0] != "fin"]
            WLIDX = {un: j for j, un in enumerate(WLOADS)}
            w_load_tok, w_comp_tok, w_fin_tok = {}, {}, {}
            w_asum_rd, w_prow_rd = {}, {}
            w_done = {0: [], 1: []}
            w_sg_pool = [None, None]
            w_wqo_st = [None, None]
            w_li = [0]
            w_ci = [0]

            def _w_emit_loads(upto):
                lo, hi = w_li[0], min(upto, len(WLOADS))
                if lo >= hi:
                    return

                def _(sd):
                    for j in range(lo, hi):
                        un = WLOADS[j]
                        if j >= 4:
                            w8(sd, w_comp_tok.get(WLOADS[j - 4]))
                        else:
                            w8(sd, tok_rel)
                        w_load_tok[un] = fin(sd.dma_start(
                            wcr[j % 4][:], _w_src(un[1], un[2])), "ad")
                SCD(_)
                w_li[0] = hi

            def _w_emit_fin(wi):
                base = 4 * wi
                po = 128 * wi
                so = 256 * wi
                toks = []

                def _(v):
                    w8(v, *[w_comp_tok[("scan", wi, u)] for u in range(64)])
                    w8(v, w_asum_rd.get(1 - wi))
                    toks.append(fin(v.tensor_reduce(
                        asum[:, 64:65], asum[:, 0:64], AX, ALU.add), "dve"))
                VEC(_)

                def _(g):
                    w8(g, toks[0])
                    toks.append(fin(g.dma_start(
                        bass.AP(d_wscr.tensor, so, [[1, 128]]),
                        asum[:, 64:65]), "gp"))
                GPS(_)
                w_asum_rd[wi] = toks[1]

                def _(g):
                    w8(g, toks[1], w_prow_rd.get(1 - wi))
                    toks.append(fin(g.dma_start(
                        prow[0:1, po:po + 128],
                        bass.AP(d_wscr.tensor, so, [[1, 128]])), "gp"))
                GPS(_)

                def _(v):
                    w8(v, toks[2])
                    v.tensor_reduce(thrw[0:1, base:base + 1],
                                    prow[0:1, po:po + 128], AX, ALU.add)
                    v.tensor_scalar_mul(thrw[0:1, base:base + 1],
                                        thrw[0:1, base:base + 1],
                                        1.0 / (HS * FF))
                    v.tensor_scalar_max(thrw[0:1, base:base + 1],
                                        thrw[0:1, base:base + 1], 1e-5)
                    v.tensor_scalar_mul(thrw[0:1, base + 1:base + 2],
                                        thrw[0:1, base:base + 1], 0.5)
                    toks.append(fin(v.tensor_scalar_mul(
                        thrw[0:1, base + 2:base + 3],
                        thrw[0:1, base + 1:base + 2], -1.0), "dve"))
                VEC(_)
                w_prow_rd[wi] = toks[3]

                def _(g):
                    w8(g, toks[3])
                    toks.append(fin(g.dma_start(
                        bass.AP(d_wscr.tensor, so + 128, [[1, 3]]),
                        thrw[0:1, base:base + 3]), "gp"))
                GPS(_)

                def _(g):
                    w8(g, toks[4])
                    t_thb = fin(g.dma_start(
                        thbw[:, 3 * wi:3 * wi + 3],
                        bass.AP(d_wscr.tensor, so + 128,
                                [[0, 128], [1, 3]])), "gp")
                    mub = mu1b if wi == 0 else mu2b
                    t_mub = fin(g.dma_start(
                        mub[:, 0:1],
                        bass.AP(d_wscr.tensor, so + 128,
                                [[0, 128], [1, 1]])), "gp")
                    toks.append((t_thb, t_mub))
                GPS(_)
                w_fin_tok[wi] = toks[5]

            def wpump(k):
                if w_ci[0] >= len(WUNITS):
                    return
                batch = []
                while len(batch) < k and w_ci[0] < len(WUNITS):
                    un = WUNITS[w_ci[0]]
                    if un[0] != "fin" and WLIDX[un] >= w_li[0]:
                        break
                    batch.append(un)
                    w_ci[0] += 1
                    if un[0] == "fin":
                        break
                scans = [u2 for u2 in batch if u2[0] == "scan"]
                signs = [u2 for u2 in batch if u2[0] == "sign"]
                fins = [u2 for u2 in batch if u2[0] == "fin"]
                if scans:
                    def _(v, scans=scans):
                        for un in scans:
                            j = WLIDX[un]
                            w8(v, w_load_tok[un])
                            w_comp_tok[un] = fin(v.tensor_reduce(
                                asum[:, un[2]:un[2] + 1], wcr[j % 4][:],
                                AX, ALU.add, apply_absolute_value=True),
                                "dve")
                    VEC(_)
                for pi in range(0, len(signs), 2):
                    pair = signs[pi:pi + 2]

                    def _(sc, pair=pair):
                        for pj, un in enumerate(pair):
                            wi, u = un[1], un[2]
                            j = WLIDX[un]
                            w8(sc, w_load_tok[un], *w_fin_tok[wi])
                            w8(sc, w_sg_pool[pj])
                            sc.activation(sg1w[pj][:], wcr[j % 4][:], AF.Sign,
                                          bias=thbw[:, 3 * wi + 2:3 * wi + 3])
                            w_comp_tok[un] = fin(sc.activation(
                                sg2w[pj][:], wcr[j % 4][:], AF.Sign,
                                bias=thbw[:, 3 * wi + 1:3 * wi + 2]), "act")
                    SCL(_)

                    def _(g, pair=pair):
                        for pj, un in enumerate(pair):
                            wi, u = un[1], un[2]
                            w8(g, w_comp_tok[un], w_wqo_st[pj])
                            t_add = fin(g.tensor_add(
                                wqow[pj][:], sg1w[pj][:], sg2w[pj][:]), "pl")
                            w_sg_pool[pj] = t_add
                            w8(g, t_add)
                            tst = fin(g.dma_start(_w_dst(wi, u),
                                                  wqow[pj][:]), "gp")
                            w_wqo_st[pj] = tst
                            w_done[wi].append(tst)
                    POOL(_)
                for un in fins:
                    _w_emit_fin(un[1])
                ncl = sum(1 for x2 in WUNITS[:w_ci[0]] if x2[0] != "fin")
                _w_emit_loads(ncl + 4)

            _w_emit_loads(4)


            xn = esA.enter_context(nc.sbuf_tensor("xn_s", [128, 8, SHARD], BF16))
            xt0 = esA.enter_context(nc.sbuf_tensor("xt0_s", [128, HS], F32))
            xt1 = esA.enter_context(nc.sbuf_tensor("xt1_s", [128, HS], F32))
            xt2 = esA.enter_context(nc.sbuf_tensor("xt2_s", [128, HS], F32))
            xnt = esA.enter_context(nc.sbuf_tensor("xnt_s", [128, HS], BF16))
            q_st = esA.enter_context(nc.sbuf_tensor("q_st", [128, 8, 432], BF16))
            k_st = esA.enter_context(nc.sbuf_tensor("k_st", [128, 8, 444], BF16))
            a_st = esA.enter_context(nc.sbuf_tensor("a_st", [128, 8, 432], BF16))
            v_s0 = esA.enter_context(nc.sbuf_tensor("v_s0", [128, HS], BF16))
            v_s1 = esA.enter_context(nc.sbuf_tensor("v_s1", [128, HS], BF16))
            bdst2 = [esA.enter_context(
                nc.sbuf_tensor("bdst%d_s" % _i, [128, H, PP], F32))
                for _i in range(2)]
            scst = esA.enter_context(nc.sbuf_tensor("scst_s", [128, H, KTW], F32))
            bdbs2 = [esA.enter_context(
                nc.sbuf_tensor("bdbs%d_s" % _i, [128, H, KTW], F32))
                for _i in range(2)]
            s1st = esA.enter_context(nc.sbuf_tensor("s1st_s", [128, H, KTW], F32))
            ths = esA.enter_context(nc.sbuf_tensor("ths_s", [128, H, KTW], BF16))
            emst = esA.enter_context(nc.sbuf_tensor("emst_s", [KTW, H, SUBT], BF16))
            rcr = esA.enter_context(nc.sbuf_tensor("rcr_s", [1, H * SUBT], BF16))
            rbst = esA.enter_context(nc.sbuf_tensor("rbst_s", [128, H * SUBT], F32))
            y_fm = esA.enter_context(nc.sbuf_tensor("yfm_s", [128, 8, 432], BF16))
            y_tm = esA.enter_context(nc.sbuf_tensor("ytm_s", [128, HS], BF16))
            lny = esA.enter_context(nc.sbuf_tensor("lny_s", [128, HS], F32))
            xpt = esA.enter_context(nc.sbuf_tensor("xpt_s", [128, HS], BF16))

            xtiles = [xt0, xt1, xt2]
            t_xld = [None] * NCHUNK
            prev_tr = []
            t_a1_last = [None] * NCHUNK

            def _(g):
                for c2 in range(3):
                    t_xld[c2] = fin(g.dma_start(
                        xtiles[c2][:], d_x[c2 * 128:(c2 + 1) * 128, :]), "gp")
            GPS(_)

            prev_xnt_pe = None
            A1sq = [None] * NCHUNK
            A1sq[0] = emit_ln_sq(xtiles[0][:], 128, xnt,
                                 [t_xld[0], tok_const["cst"]], co=0)
            for c in range(NCHUNK):
                wpump(4)
                X = xtiles[c % 3]
                co = 32 * (c % 2)
                if c + 1 < NCHUNK:
                    A1sq[c + 1] = emit_ln_sq(
                        xtiles[(c + 1) % 3][:], 128, xnt,
                        [t_xld[c + 1], prev_xnt_pe], co=32 * ((c + 1) % 2))
                sq_tok, d_tok = A1sq[c]
                a_tok = emit_ln_fin(X[:], 128, d_tok, co=co,
                                    apply_out=xnt[:], deps=(prev_xnt_pe,))
                t_ap = [a_tok]

                def _pe_tr(t, i, pt, c=c, t_ap=t_ap):
                    if i == 0:
                        w8(t, t_ap[0], tok_const["idb"])
                    return t.transpose(pt[0:128, 0:128],
                                       xnt[:, i * 128:(i + 1) * 128], idb[:])

                def _ev_tr(v, i, pt, c=c):
                    return v.tensor_copy(xn[:, i, c * 128:(c + 1) * 128],
                                         pt[0:128, 0:128])

                t_tr = pingpong(8, PTS, _pe_tr, _ev_tr, "dve",
                                [], list(prev_tr))
                prev_tr = [t_tr[4][1], t_tr[5][1], t_tr[6][1], t_tr[7][1]]
                t_a1_last[c] = t_tr[7][1]
                prev_xnt_pe = t_tr[7][0]

                if c + 3 < NCHUNK:
                    def _(g, c=c, sq_tok=sq_tok, d_tok=d_tok, t_ap=t_ap):
                        w8(g, t_ap[0], sq_tok, d_tok)
                        t_xld[c + 3] = fin(g.dma_start(
                            xtiles[c % 3][:],
                            d_x[(c + 3) * 128:(c + 4) * 128, :]), "gp")
                    GPS(_)

            tok_xn = now()

            last_super = now()
            prev_ep = list(tok_xn)
            prev_pv_evac = []
            prev_rb_acts = []
            prev_v_evac = []
            t_pv = []
            cd_users = []
            t_bdw2 = [None, None]
            t_shr = [None, None]
            t_rdb2 = [None, None]
            pv_s1 = [None, None]
            pv_tanh = [None]
            t_bd_h = [None, None]
            pv_qk_pe = [None]
            pv_wp_ev = []
            for si, (sub0, nsub) in enumerate(SUPER):
                S = sub0 * SUBT
                T = nsub * SUBT
                sh = S + HALO

                def _pe_q(t, i, ps, S=S, T=T, sh=sh):
                    if i == 0:
                        w8(t, *tok_xn)
                        w8(t, tok_const["wq"], tok_const["wk"], t_qs[-1],
                           pv_qk_pe[0])
                    wgt, width, off = (wq_s, T, sh) if i < 8 else \
                        (wk_s, T + 12, sh - 12)
                    hc = i % 8
                    for dc in range(8):
                        last = t.matmul(ps[0:128, 0:width],
                                        wgt[:, dc, hc * 128:(hc + 1) * 128],
                                        xn[:, dc, off:off + width],
                                        start=(dc == 0), stop=(dc == 7))
                    return last

                def _ev_q(sc, i, ps, T=T):
                    hc = i % 8
                    if i < 8:
                        return sc.activation(q_st[:, hc, 0:T], ps[0:128, 0:T],
                                             AF.Identity, scale=qsc[:, 0:1])
                    return sc.activation(k_st[:, hc, 0:T + 12],
                                         ps[0:128, 0:T + 12], AF.Identity)

                t_proj = pingpong(16, [PSA, PSB, PSC, PSD], _pe_q, _ev_q, "act",
                                  [], list(prev_rb_acts) + list(prev_pv_evac)
                                  + list(pv_wp_ev))

                cd_users[:] = [t_proj[14][1], t_proj[15][1]]
                prev_pv_evac = []

                def emit_bd(su):
                    p = su % 2
                    s0 = su * SUBT

                    def _pe_bd(t, i, ps, s0=s0, p=p):
                        if i == 0:
                            w8(t, t_proj[7][1], tok_rel, t_bdw2[p])
                        return t.matmul(ps[0:SUBT, 0:PP],
                                        q_st[:, i, s0:s0 + SUBT],
                                        rel_s[:, i, :], start=True, stop=True)

                    def _ev_bd(sc, i, ps, p=p):
                        return sc.activation(bdst2[p][0:SUBT, i, :],
                                             ps[0:SUBT, 0:PP], AF.Identity)

                    t_bd = pingpong(H, [PSC, PSD], _pe_bd, _ev_bd, "act",
                                    [], list(cd_users))
                    cd_users[:] = [t_bd[H - 2][1], t_bd[H - 1][1]]
                    t_bd_h[p] = t_bd

                    def _(y, p=p):
                        w8(y, t_bd[H - 1][1], t_shr[p])
                        t_bdw2[p] = fin(y.dma_start(
                            bass.AP(d_bdc.tensor, p * SUB_C,
                                    [[PP, SUBT], [CSTRIDE, H], [1, PP]]),
                            bass.AP(bdst2[p], 0,
                                    [[H * PP, SUBT], [PP, H], [1, PP]])), "hw")
                        w8(y, t_bdw2[p], t_rdb2[p])
                        t_shr[p] = fin(y.dma_start(
                            bass.AP(d_bdb.tensor, p * SUB_B,
                                    [[CHUNK * KTW + CHUNK, H * SUBB],
                                     [KTW, CHUNK], [1, CTX]]),
                            bass.AP(d_bdc.tensor, p * SUB_C,
                                    [[CHUNK * PP, H * SUBB],
                                     [CTX, CHUNK], [1, CTX]])), "hw")
                        w8(y, t_shr[p], pv_s1[p])
                        t_rdb2[p] = fin(y.dma_start(
                            bass.AP(bdbs2[p], 0,
                                    [[H * KTW, SUBT], [KTW, H], [1, KTW]]),
                            bass.AP(d_bdb.tensor, p * SUB_B,
                                    [[KTW, SUBT], [BSTRIDE, H], [1, KTW]])),
                            "hw")
                    SYN(_)

                emit_bd(0)
                for su in range(nsub):
                    wpump(4)
                    p_su = su % 2
                    s0 = su * SUBT
                    w0 = sh + s0 - 12
                    if su + 1 < nsub:
                        emit_bd(su + 1)
                    t_bd = t_bd_h[p_su]

                    vbuf = v_s0 if su % 2 == 0 else v_s1

                    def _pe_v(t, i, ps, w0=w0):
                        if i == 0:
                            w8(t, tok_const["wv"], t_proj[12][1], t_proj[13][1])
                            w8(t, *prev_v_evac)
                        for dc in range(8):
                            last = t.matmul(ps[0:KTW, 0:512],
                                            xn[:, dc, w0:w0 + KTW],
                                            wv_s[:, dc, i * 512:(i + 1) * 512],
                                            start=(dc == 0), stop=(dc == 7))
                        return last

                    def _ev_v(v, i, ps, vbuf=vbuf):
                        return v.tensor_copy(
                            vbuf[0:KTW, i * 512:(i + 1) * 512], ps[0:KTW, 0:512])

                    t_vp = pingpong(2, [PSA, PSB], _pe_v, _ev_v, "dve",
                                    [], [t_proj[12][1], t_proj[13][1]]
                                    + list(prev_rb_acts))
                    prev_v_evac = [t_vp[0][1], t_vp[1][1]]

                    def _pe_s(t, i, ps, s0=s0, t_bd=t_bd):
                        if i < 2:
                            w8(t, t_bd[H - 2 + i][1])
                        return t.matmul(ps[0:SUBT, 0:KTW],
                                        q_st[:, i, s0:s0 + SUBT],
                                        k_st[:, i, s0:s0 + KTW],
                                        start=True, stop=True)

                    def _ev_s(v, i, ps):
                        return v.tensor_copy(scst[0:SUBT, i, :],
                                             ps[0:SUBT, 0:KTW])

                    t_sc = pingpong(H, [PSC, PSD], _pe_s, _ev_s, "dve",
                                    [], list(cd_users))
                    cd_users[:] = [t_sc[H - 2][1], t_sc[H - 1][1]]
                    pv_qk_pe[0] = t_sc[H - 1][0]

                    t_s1 = []

                    def _(v, p_su=p_su):
                        w8(v, t_sc[H - 1][1], t_rdb2[p_su], pv_tanh[0])
                        t_s1.append(fin(v.tensor_add(
                            s1st[0:SUBT, :, :].rearrange("p a b -> p (a b)"),
                            scst[0:SUBT, :, :].rearrange("p a b -> p (a b)"),
                            bdbs2[p_su][0:SUBT, :, :].rearrange(
                                "p a b -> p (a b)")), "dve"))
                    VEC(_)
                    pv_s1[p_su] = t_s1[0]

                    t_tanh = []

                    def _(sc):
                        w8(sc, t_s1[0])
                        t_tanh.append(fin(sc.activation(
                            ths[0:SUBT, :, :].rearrange("p a b -> p (a b)"),
                            s1st[0:SUBT, :, :].rearrange("p a b -> p (a b)"),
                            AF.Tanh, scale=1.0 / SOFTCAP), "act"))
                    SCL(_)
                    pv_tanh[0] = t_tanh[0]

                    def _pe_te(t, i, pt):
                        if i == 0:
                            w8(t, t_tanh[0])
                        return t.transpose(pt[0:KTW, 0:SUBT],
                                           ths[0:SUBT, i, :],
                                           idb[0:SUBT, 0:SUBT])

                    def _ev_te(sc, i, pt):
                        return sc.activation(emst[:, i, :], pt[0:KTW, 0:SUBT],
                                             AF.Exp, scale=SOFTCAP)

                    t_te = pingpong(H, PTS, _pe_te, _ev_te, "act",
                                    [], [t_tanh[0]])

                    t_mask = []

                    def _(v):
                        w8(v, t_te[H - 1][1], tok_const["mask"])
                        t_mask.append(fin(v.tensor_mul(
                            emst[:].rearrange("p a b -> p (a b)"),
                            emst[:].rearrange("p a b -> p (a b)"),
                            maskb[:]), "dve"))
                    VEC(_)

                    t_den = []

                    def _(t):
                        w8(t, t_mask[0], tok_const["o120b"])
                        w8(t, *cd_users)
                        t.matmul(PSC[0:1, 0:432], o120b[0:KTW, 0:1],
                                 emst[:].rearrange("p a b -> p (a b)")[:, 0:432],
                                 start=True, stop=True)
                        _m2 = t.matmul(
                            PSD[0:1, 0:432], o120b[0:KTW, 0:1],
                            emst[:].rearrange("p a b -> p (a b)")[:, 432:864],
                            start=True, stop=True)
                        t_den.append(fin(_m2, "pe"))
                    blk.tensor(_)

                    def _(v):
                        w8(v, t_den[0])
                        v.reciprocal(rcr[0:1, 0:432], PSC[0:1, 0:432])
                        t_den.append(fin(v.reciprocal(rcr[0:1, 432:864],
                                                      PSD[0:1, 0:432]), "dve"))
                    VEC(_)
                    cd_users[:] = [t_den[1]]

                    def _pe_rb(t, i, ps):
                        if i == 0:
                            w8(t, t_den[1], tok_const["oc_b"])
                        return t.matmul(ps[0:128, 0:432], oc_b[:],
                                        rcr[0:1, i * 432:(i + 1) * 432],
                                        start=True, stop=True)

                    def _ev_rb(sc, i, ps):
                        return sc.activation(rbst[:, i * 432:(i + 1) * 432],
                                             ps[0:128, 0:432], AF.Identity)

                    wpump(4)
                    t_rb = pingpong(2, [PSA, PSB], _pe_rb, _ev_rb, "act",
                                    [], [t_vp[0][1], t_vp[1][1]])
                    prev_rb_acts = [t_rb[0][1], t_rb[1][1]]

                    def _pe_pv(t, i, ps, vbuf=vbuf):
                        if i == 0:
                            w8(t, t_rb[1][1])
                        return t.matmul(ps[0:128, 0:SUBT],
                                        vbuf[0:KTW, i * 128:(i + 1) * 128],
                                        emst[:, i, :], start=True, stop=True)

                    def _ev_pv(v, i, ps, s0=s0):
                        return v.tensor_mul(
                            a_st[:, i, s0:s0 + SUBT], ps[0:128, 0:SUBT],
                            rbst[:, i * SUBT:(i + 1) * SUBT])

                    t_pv = pingpong(H, [PSC, PSD], _pe_pv, _ev_pv, "dve",
                                    [], list(cd_users))
                    cd_users[:] = [t_pv[H - 2][1], t_pv[H - 1][1]]
                    prev_pv_evac = [t_pv[H - 2][1], t_pv[H - 1][1]]

                def _pe_wp(t, i, ps, T=T):
                    if i == 0:
                        w8(t, t_pv[H - 1][1], tok_const["wp"])
                        w8(t, *prev_rb_acts)
                    for dci in range(8):
                        last = t.matmul(ps[0:128, 0:T],
                                        wp_s[:, dci, i * 128:(i + 1) * 128],
                                        a_st[:, dci, 0:T], start=(dci == 0),
                                        stop=(dci == 7))
                    return last

                def _ev_wp(sc, i, ps, T=T):
                    return sc.activation(y_fm[:, i, 0:T], ps[0:128, 0:T],
                                         AF.Identity)

                t_wp = pingpong(8, [PSA, PSB], _pe_wp, _ev_wp, "act",
                                [], list(prev_rb_acts))
                prev_rb_acts = []
                pv_wp_ev[:] = [t_wp[6][1], t_wp[7][1]]

                a = 0
                while a < T:
                    wpump(4)
                    cnt = min(128, T - a)

                    def _pe_yt(t, i, pt, a=a, cnt=cnt, prev_ep=list(prev_ep)):
                        if i == 0:
                            w8(t, t_wp[7][1], *prev_ep)
                        return t.transpose(pt[0:cnt, 0:128],
                                           y_fm[:, i, a:a + cnt], idb[:])

                    def _ev_yt(v, i, pt, cnt=cnt):
                        return v.tensor_copy(
                            y_tm[0:cnt, i * 128:(i + 1) * 128], pt[0:cnt, 0:128])

                    t_yt = pingpong(8, PTS, _pe_yt, _ev_yt, "dve",
                                    [], list(prev_tr))
                    prev_tr = [t_yt[4][1], t_yt[5][1], t_yt[6][1], t_yt[7][1]]

                    t_xr = []

                    def _(g, S=S, a=a, cnt=cnt, prev_ep=list(prev_ep)):
                        w8(g, *prev_ep)
                        t_xr.append(fin(g.dma_start(
                            xt0[0:cnt, :],
                            d_x[HALO + S + a:HALO + S + a + cnt, :]), "gp"))
                    GPS(_)

                    sq_tok, a_tok, d_tok = emit_ln_stats(
                        y_tm[0:cnt, :], cnt, xnt, [t_yt[7][1]],
                        apply_out=lny[0:cnt, :])
                    t_ep = [a_tok]

                    def _(v, cnt=cnt):
                        w8(v, t_ep[0], t_xr[0])
                        t_ep.append(fin(v.tensor_add(
                            xpt[0:cnt, :], lny[0:cnt, :], xt0[0:cnt, :]), "dve"))
                    VEC(_)

                    def _(g, S=S, a=a, cnt=cnt):
                        w8(g, t_ep[1])
                        t_ep.append(fin(g.dma_start(
                            bass.AP(d_xp.tensor, (S + a) * HS,
                                    [[HS, cnt], [1, HS]]), xpt[0:cnt, :]), "gp"))
                    GPS(_)
                    prev_ep = [t_ep[1], t_ep[2], sq_tok]
                    a += cnt

                last_super = now()

            while w_ci[0] < len(WUNITS):
                wpump(6)
            tok_A = now()

        tok_W = now()

        with contextlib.ExitStack() as esB:
            w1q = esB.enter_context(nc.sbuf_tensor("w1q_s", [128, 8, FF], BF16))
            xpb, xnrb, hscb, xqtmb, xqfmb, svrb, svcb = \
                [], [], [], [], [], [], []
            htmb, hqtmb, hqfmb = [], [], []
            for p in range(2):
                xpb.append(esB.enter_context(
                    nc.sbuf_tensor("xp%d_s" % p, [128, HS], F32)))
                xnrb.append(esB.enter_context(
                    nc.sbuf_tensor("xnr%d_s" % p, [128, HS], F32)))
                hscb.append(esB.enter_context(
                    nc.sbuf_tensor("hsc%d_s" % p, [128, 16], F32)))
                xqtmb.append(esB.enter_context(
                    nc.sbuf_tensor("xqtm%d_s" % p, [128, HS], BF16)))
                xqfmb.append(esB.enter_context(
                    nc.sbuf_tensor("xqfm%d_s" % p, [128, 8, 128], BF16)))
                svrb.append(esB.enter_context(
                    nc.sbuf_tensor("svr%d_s" % p, [1, 128], BF16)))
                svcb.append(esB.enter_context(
                    nc.sbuf_tensor("svc%d_s" % p, [128, 1], BF16)))
                htmb.append(esB.enter_context(
                    nc.sbuf_tensor("htm%d_s" % p, [128, FF], F32)))
                hqtmb.append(esB.enter_context(
                    nc.sbuf_tensor("hqtm%d_s" % p, [128, FF], BF16)))
                hqfmb.append(esB.enter_context(
                    nc.sbuf_tensor("hqfm%d_s" % p, [128, 32, 128], BF16)))
            yxs = esB.enter_context(nc.sbuf_tensor("yxs_s", [128, HS], F32))
            q0xs = esB.enter_context(nc.sbuf_tensor("q0xs_s", [128, HS], I32))
            yhfs = esB.enter_context(nc.sbuf_tensor("yhfs_s", [128, FF], F32))
            q0hs = esB.enter_context(nc.sbuf_tensor("q0hs_s", [128, FF], I32))

            t_w1l = []

            def _(g):
                w8(g, *w_done[0])
                t_w1l.append(fin(g.dma_start(
                    bass.AP(w1q, 0, [[8 * FF, 128], [FF, 8], [1, FF]]),
                    bass.AP(d_wq1.tensor, 0, [[FF, 128], [128 * FF, 8], [1, FF]])),
                    "gp"))
            GPS(_)

            t_xpl = [None] * NT
            t_store = []

            def _(g):
                w8(g, *tok_W)
                for c2 in range(2):
                    t0, cnt0 = BTILES[c2]
                    t_xpl[c2] = fin(g.dma_start(
                        xpb[c2][0:cnt0, :],
                        bass.AP(d_xp.tensor, t0 * HS, [[HS, cnt0], [1, HS]])),
                        "gp")
            GPS(_)

            prev_tr = []
            t_tapyh = [None]
            pv_stats_free = [[], []]
            pv_hfree = [[], []]
            pv_xtr_pe = [None, None]
            pv_htr_pe = [None, None]
            pv_w1_tail = []
            pv_sv_pe = [None, None]
            pv_xnr_dve = [[], []]
            pv_rnd_x = [None]
            pv_rnd_h = [None]
            pv_w1_pe = [None, None]
            pv_hq_store = [None, None]
            S_tq = [None] * NT
            S_vec = [None] * NT
            X_sv = [None] * NT
            X_tr = [None] * NT
            W_ev = [None] * NT
            H_tq = [None] * NT

            def emit_S(k):
                p = k % 2
                co = 32 * p
                t0, cnt = BTILES[k]
                sq_tok, a_tok, d_tok = emit_ln_stats(
                    xpb[p][0:cnt, :], cnt, xnrb[p],
                    [t_xpl[k]] + list(pv_xnr_dve[p]), c_extra=True, co=co,
                    apply_out=xnrb[p][0:cnt, :])
                pv_stats_free[p] = [sq_tok, d_tok, a_tok]
                t_x1 = [a_tok]

                def _(v, cnt=cnt, k=k, p=p, co=co):
                    w8(v, t_x1[0], *tok_W)
                    w8(v, pv_sv_pe[p])
                    v.reduce_max(st[0:cnt, co + 20:co + 21], xnrb[p][0:cnt, :],
                                 axis=AX, apply_absolute_value=True)
                    v.tensor_scalar_max(absx[0:cnt, k:k + 1],
                                        st[0:cnt, co + 20:co + 21], 1e-5)
                    v.reciprocal(st[0:cnt, co + 21:co + 22],
                                 absx[0:cnt, k:k + 1])
                    v.tensor_mul(st[0:cnt, co + 22:co + 23],
                                 absx[0:cnt, k:k + 1],
                                 st[0:cnt, co + 21:co + 22])
                    v.tensor_scalar(st[0:cnt, co + 22:co + 23],
                                    st[0:cnt, co + 22:co + 23], -1.0, 2.0,
                                    ALU.mult, ALU.add)
                    v.tensor_mul(st[0:cnt, co + 21:co + 22],
                                 st[0:cnt, co + 21:co + 22],
                                 st[0:cnt, co + 22:co + 23])
                    v.tensor_scalar_mul(st[0:cnt, co + 23:co + 24],
                                        st[0:cnt, co + 21:co + 22], 127.0)
                    v.reciprocal(st[0:cnt, co + 24:co + 25], mu1b[0:cnt, 0:1])
                    v.tensor_mul(st[0:cnt, co + 25:co + 26],
                                 st[0:cnt, co + 21:co + 22],
                                 st[0:cnt, co + 24:co + 25])
                    v.tensor_scalar_mul(st[0:cnt, co + 25:co + 26],
                                        st[0:cnt, co + 25:co + 26], 254.0)
                    v.tensor_copy(svcb[p][0:cnt, 0:1],
                                  st[0:cnt, co + 25:co + 26])
                    v.tensor_mul(st[0:cnt, co + 26:co + 27],
                                 absx[0:cnt, k:k + 1], mu1b[0:cnt, 0:1])
                    t_x1.append(fin(v.tensor_scalar_mul(
                        st[0:cnt, co + 26:co + 27],
                        st[0:cnt, co + 26:co + 27], 1.0 / 254.0), "dve"))
                VEC(_)
                pv_xnr_dve[p] = [t_x1[1]]

                def _(sc, cnt=cnt, p=p, co=co):
                    w8(sc, t_x1[1], pv_rnd_x[0])
                    t_x1.append(fin(sc.activation(
                        yxs[0:cnt, :], xnrb[p][0:cnt, :], AF.Identity,
                        scale=st[0:cnt, co + 23:co + 24]), "act"))
                SCL(_)

                t_q = emit_round(yxs[0:cnt, :], q0xs[0:cnt, :],
                                 xqtmb[p][0:cnt, :], [t_x1[2], pv_xtr_pe[p]])
                pv_rnd_x[0] = t_q
                S_tq[k] = t_q
                S_vec[k] = t_x1[1]

                if k + 2 < NT:
                    def _(g, k=k, p=p):
                        w8(g, *pv_stats_free[p])
                        t2, cnt2 = BTILES[k + 2]
                        t_xpl[k + 2] = fin(g.dma_start(
                            xpb[p][0:cnt2, :],
                            bass.AP(d_xp.tensor, t2 * HS,
                                    [[HS, cnt2], [1, HS]])), "gp")
                    GPS(_)

            def emit_X(i):
                nonlocal prev_tr
                p = i % 2
                t0, cnt = BTILES[i]

                def _pe_xt(t, j, pt, cnt=cnt, i=i, p=p):
                    if j == 0:
                        w8(t, S_tq[i])
                    return t.transpose(pt[0:128, 0:cnt],
                                       xqtmb[p][0:cnt, j * 128:(j + 1) * 128],
                                       idb[0:cnt, 0:cnt])

                def _ev_xt(v, j, pt, cnt=cnt, p=p):
                    return v.tensor_copy(xqfmb[p][:, j, 0:cnt],
                                         pt[0:128, 0:cnt])

                t_xtr = pingpong(8, PTS, _pe_xt, _ev_xt, "dve",
                                 [pv_w1_pe[p]], list(prev_tr))
                prev_tr = [t_xtr[j][1] for j in range(4, 8)]
                pv_xtr_pe[p] = t_xtr[7][0]
                X_tr[i] = t_xtr

                t_sv = []

                def _(t, cnt=cnt, i=i, p=p):
                    w8(t, S_vec[i], t_xtr[6][1], t_xtr[7][1])
                    _m = t.transpose(PT1[0:1, 0:cnt], svcb[p][0:cnt, 0:1],
                                     idb[0:cnt, 0:cnt])
                    t_sv.append(fin(_m, "pe"))
                blk.tensor(_)
                pv_sv_pe[p] = t_sv[0]

                def _(v, cnt=cnt, p=p):
                    w8(v, t_sv[0], pv_w1_pe[p])
                    t_sv.append(fin(v.tensor_copy(svrb[p][0:1, 0:cnt],
                                                  PT1[0:1, 0:cnt]), "dve"))
                VEC(_)
                prev_tr = prev_tr[1:] + [t_sv[1]]
                X_sv[i] = t_sv[1]

            def _mk_w1(i, off):
                p = i % 2
                co = 32 * p
                t0, cnt = BTILES[i]

                def _pe(t, j, ps, cnt=cnt, i=i, p=p, off=off):
                    fs = off + j
                    if j == 0 and off == 0:
                        w8(t, X_tr[i][7][1], X_sv[i], t_w1l[0],
                           tok_const["b1"])
                    for dc in range(8):
                        t.matmul(ps[0:cnt, 0:512], xqfmb[p][:, dc, 0:cnt],
                                 w1q[:, dc, fs * 512:(fs + 1) * 512],
                                 start=(dc == 0), stop=False)
                    return t.matmul(ps[0:cnt, 0:512], svrb[p][0:1, 0:cnt],
                                    b1r[0:1, fs * 512:(fs + 1) * 512],
                                    start=False, stop=True)

                def _ev(sc, j, ps, cnt=cnt, p=p, co=co, off=off):
                    fs = off + j
                    return sc.activation(
                        htmb[p][0:cnt, fs * 512:(fs + 1) * 512],
                        ps[0:cnt, 0:512], AF.Gelu,
                        scale=st[0:cnt, co + 26:co + 27])
                return _pe, _ev

            def emit_W1(i):
                p = i % 2
                pe, ev = _mk_w1(i, 0)
                t_h = pingpong(4, [PSA, PSB, PSC, PSD], pe, ev, "act",
                               [S_vec[i]] + list(pv_hfree[p]),
                               list(pv_w1_tail))
                W_ev[i] = [t_h[j][1] for j in range(4)]

            def emit_W2(i):
                p = i % 2
                pe, ev = _mk_w1(i, 4)
                t_h = pingpong(4, [PSA, PSB, PSC, PSD], pe, ev, "act",
                               [], list(W_ev[i]))
                W_ev[i] = W_ev[i] + [t_h[j][1] for j in range(4)]
                pv_w1_tail[:] = [t_h[j][1] for j in range(4)]
                pv_w1_pe[p] = t_h[3][0]

            def emit_H(i):
                p = i % 2
                co = 32 * p
                t0, cnt = BTILES[i]
                t_hst = []

                def _(sc, cnt=cnt, i=i, p=p, co=co):
                    w8(sc, *W_ev[i])
                    w8(sc, t_tapyh[0])
                    for fs in range(8):
                        sc.activation(xnrb[p][0:cnt, 0:512],
                                      htmb[p][0:cnt, fs * 512:(fs + 1) * 512],
                                      AF.Square,
                                      accum_out=hscb[p][0:cnt, fs:fs + 1])
                    t_hst.append(fin(sc.activation(
                        xnrb[p][0:cnt, 0:1], hscb[p][0:cnt, 0:1],
                        AF.Identity), "act"))
                SCL(_)

                def _(v, cnt=cnt, p=p, co=co):
                    w8(v, t_hst[0])
                    v.reduce_sum(st[0:cnt, co + 1:co + 2],
                                 hscb[p][0:cnt, 0:8], axis=AX)
                    v.reduce_max(st[0:cnt, co + 20:co + 21],
                                 htmb[p][0:cnt, :], axis=AX,
                                 apply_absolute_value=True)
                    t_hst.append(fin(v.tensor_scalar_mul(
                        st[0:cnt, co + 4:co + 5], st[0:cnt, co + 1:co + 2],
                        1.0 / FF), "dve"))
                VEC(_)

                def _(sc, cnt=cnt, co=co):
                    w8(sc, t_hst[1])
                    sc.activation(st[0:cnt, co + 5:co + 6],
                                  st[0:cnt, co + 4:co + 5], AF.Ln,
                                  bias=cst[0:cnt, 3:4])
                    t_hst.append(fin(sc.activation(
                        st[0:cnt, co + 6:co + 7], st[0:cnt, co + 5:co + 6],
                        AF.Exp, scale=-0.5), "act"))
                SCL(_)

                def _(v, cnt=cnt, i=i, p=p, co=co):
                    w8(v, t_hst[2])
                    v.tensor_mul(st[0:cnt, co + 9:co + 10],
                                 st[0:cnt, co + 6:co + 7],
                                 st[0:cnt, co + 20:co + 21])
                    v.tensor_scalar_max(absh[0:cnt, i:i + 1],
                                        st[0:cnt, co + 9:co + 10], 1e-5)
                    v.reciprocal(st[0:cnt, co + 21:co + 22],
                                 absh[0:cnt, i:i + 1])
                    v.tensor_mul(st[0:cnt, co + 22:co + 23],
                                 absh[0:cnt, i:i + 1],
                                 st[0:cnt, co + 21:co + 22])
                    v.tensor_scalar(st[0:cnt, co + 22:co + 23],
                                    st[0:cnt, co + 22:co + 23], -1.0, 2.0,
                                    ALU.mult, ALU.add)
                    v.tensor_mul(st[0:cnt, co + 21:co + 22],
                                 st[0:cnt, co + 21:co + 22],
                                 st[0:cnt, co + 22:co + 23])
                    v.tensor_scalar_mul(st[0:cnt, co + 23:co + 24],
                                        st[0:cnt, co + 21:co + 22], 127.0)
                    t_hst.append(fin(v.tensor_mul(
                        st[0:cnt, co + 27:co + 28], st[0:cnt, co + 6:co + 7],
                        st[0:cnt, co + 23:co + 24]), "dve"))
                VEC(_)

                t_yh = []

                def _(sc, cnt=cnt, p=p, co=co):
                    w8(sc, t_hst[3], pv_htr_pe[p], t_tapyh[0], pv_rnd_h[0])
                    t_yh.append(fin(sc.activation(
                        yhfs[0:cnt, :], htmb[p][0:cnt, :], AF.Identity,
                        scale=st[0:cnt, co + 27:co + 28]), "act"))
                SCL(_)
                if DEBUG_TAPS:
                    def _(g, i=i, cnt=cnt, t_yh=t_yh):
                        w8(g, t_yh[0])
                        t_tapyh[0] = fin(g.dma_start(
                            bass.AP(d_tyh.tensor, i * 128 * FF,
                                    [[FF, cnt], [1, FF]]),
                            yhfs[0:cnt, :]), "gp")
                    GPS(_)
                t_hq_last = emit_round(yhfs[0:cnt, :], q0hs[0:cnt, :],
                                       hqtmb[p][0:cnt, :], [t_yh[0]])
                pv_rnd_h[0] = t_hq_last
                pv_hfree[p] = [t_hst[1], t_hq_last]
                H_tq[i] = t_hq_last

            def emit_T(i):
                nonlocal prev_tr
                p = i % 2
                t0, cnt = BTILES[i]

                def _pe_ht(t, j, pt, cnt=cnt, i=i, p=p):
                    if j == 0:
                        w8(t, H_tq[i])
                    return t.transpose(pt[0:128, 0:cnt],
                                       hqtmb[p][0:cnt, j * 128:(j + 1) * 128],
                                       idb[0:cnt, 0:cnt])

                def _ev_ht(v, j, pt, cnt=cnt, p=p):
                    return v.tensor_copy(hqfmb[p][:, j, 0:cnt],
                                         pt[0:128, 0:cnt])

                t_htr = pingpong(32, PTS, _pe_ht, _ev_ht, "dve",
                                 [pv_hq_store[p]], list(prev_tr))
                prev_tr = [t_htr[j][1] for j in range(28, 32)]
                pv_htr_pe[p] = t_htr[31][0]

                def _(g, i=i, p=p):
                    w8(g, t_htr[31][1])
                    t_store.append(fin(g.dma_start(
                        bass.AP(d_hq.tensor, i * 128 * FF,
                                [[FF, 128], [1, FF]]),
                        hqfmb[p][:].rearrange("p a b -> p (a b)")), "gp"))
                    pv_hq_store[p] = t_store[-1]
                GPS(_)

            emit_S(0)
            for i in range(NT):
                emit_X(i)
                emit_W1(i)
                if i + 1 < NT:
                    emit_S(i + 1)
                emit_W2(i)
                if i >= 1:
                    emit_T(i - 1)
                emit_H(i)
            emit_T(NT - 1)

            tok_B1 = now()

        with contextlib.ExitStack() as esB2:
            w2q = esB2.enter_context(nc.sbuf_tensor("w2q_s", [128, 32, HS], BF16))
            hqb, xprb, x2tb, ln2b, outtb, sv2rb, sv2cb = \
                [], [], [], [], [], [], []
            for p in range(2):
                hqb.append(esB2.enter_context(
                    nc.sbuf_tensor("hq%d_s" % p, [128, 32, 128], BF16)))
                xprb.append(esB2.enter_context(
                    nc.sbuf_tensor("xpr%d_s" % p, [128, HS], F32)))
                x2tb.append(esB2.enter_context(
                    nc.sbuf_tensor("x2t%d_s" % p, [128, HS], F32)))
                ln2b.append(esB2.enter_context(
                    nc.sbuf_tensor("ln2%d_s" % p, [128, HS], F32)))
                outtb.append(esB2.enter_context(
                    nc.sbuf_tensor("outt%d_s" % p, [128, HS], F32)))
                sv2rb.append(esB2.enter_context(
                    nc.sbuf_tensor("sv2r%d_s" % p, [1, 128], BF16)))
                sv2cb.append(esB2.enter_context(
                    nc.sbuf_tensor("sv2c%d_s" % p, [128, 1], BF16)))

            t_w2l = []

            def _(g):
                w8(g, *w_done[1])
                w8(g, *tok_B1)
                t_w2l.append(fin(g.dma_start(
                    bass.AP(w2q, 0, [[32 * HS, 128], [HS, 32], [1, HS]]),
                    bass.AP(d_wq2.tensor, 0,
                            [[HS, 128], [128 * HS, 32], [1, HS]])), "gp"))
            GPS(_)

            t_hql = [None] * NT
            t_xpl2 = [None] * NT
            t_out = []
            prev_tr2 = []
            pv_sv2_pe = [None, None]
            pv_x2_free = [[], []]
            pv_out_dve = [None, None]
            pv_w2_tail = []
            pv_w2_pe = [None, None]
            pv_out_store = [None, None]

            def _(g):
                w8(g, *tok_B1)
                for c2 in range(2):
                    t0, cnt0 = BTILES[c2]
                    t_hql[c2] = fin(g.dma_start(
                        hqb[c2][:].rearrange("p a b -> p (a b)"),
                        bass.AP(d_hq.tensor, c2 * 128 * FF,
                                [[FF, 128], [1, FF]])), "gp")
                    t_xpl2[c2] = fin(g.dma_start(
                        xprb[c2][0:cnt0, :],
                        bass.AP(d_xp.tensor, t0 * HS, [[HS, cnt0], [1, HS]])),
                        "gp")
            GPS(_)

            S2_tok = [None] * NT

            def emit_P2(k):
                nonlocal prev_tr2
                p = k % 2
                co = 32 * p
                t0, cnt = BTILES[k]
                t_s2 = []

                def _(v, cnt=cnt, k=k, p=p, co=co):
                    w8(v, *tok_B1)
                    w8(v, pv_sv2_pe[p], pv_out_dve[p])
                    v.reciprocal(st[0:cnt, co + 21:co + 22],
                                 absh[0:cnt, k:k + 1])
                    v.tensor_mul(st[0:cnt, co + 22:co + 23],
                                 absh[0:cnt, k:k + 1],
                                 st[0:cnt, co + 21:co + 22])
                    v.tensor_scalar(st[0:cnt, co + 22:co + 23],
                                    st[0:cnt, co + 22:co + 23], -1.0, 2.0,
                                    ALU.mult, ALU.add)
                    v.tensor_mul(st[0:cnt, co + 21:co + 22],
                                 st[0:cnt, co + 21:co + 22],
                                 st[0:cnt, co + 22:co + 23])
                    v.reciprocal(st[0:cnt, co + 24:co + 25], mu2b[0:cnt, 0:1])
                    v.tensor_mul(st[0:cnt, co + 25:co + 26],
                                 st[0:cnt, co + 21:co + 22],
                                 st[0:cnt, co + 24:co + 25])
                    v.tensor_scalar_mul(st[0:cnt, co + 25:co + 26],
                                        st[0:cnt, co + 25:co + 26], 254.0)
                    v.tensor_copy(sv2cb[p][0:cnt, 0:1],
                                  st[0:cnt, co + 25:co + 26])
                    v.tensor_mul(st[0:cnt, co + 26:co + 27],
                                 absh[0:cnt, k:k + 1], mu2b[0:cnt, 0:1])
                    t_s2.append(fin(v.tensor_scalar_mul(
                        st[0:cnt, co + 26:co + 27],
                        st[0:cnt, co + 26:co + 27], 1.0 / 254.0), "dve"))
                VEC(_)

                def _(t, cnt=cnt, p=p):
                    w8(t, t_s2[0], *prev_tr2)
                    _m = t.transpose(PT1[0:1, 0:cnt], sv2cb[p][0:cnt, 0:1],
                                     idb[0:cnt, 0:cnt])
                    t_s2.append(fin(_m, "pe"))
                blk.tensor(_)
                pv_sv2_pe[p] = t_s2[1]

                def _(v, cnt=cnt, p=p):
                    w8(v, t_s2[1], pv_w2_pe[p])
                    t_s2.append(fin(v.tensor_copy(sv2rb[p][0:1, 0:cnt],
                                                  PT1[0:1, 0:cnt]), "dve"))
                VEC(_)
                prev_tr2 = [t_s2[2]]
                S2_tok[k] = t_s2

            T_x2 = [None] * NT

            def emit_M2(i):
                p = i % 2
                co = 32 * p
                t0, cnt = BTILES[i]

                def _pe_w2(t, j, ps, cnt=cnt, i=i, p=p):
                    if j == 0:
                        w8(t, t_hql[i], t_w2l[0], S2_tok[i][2],
                           tok_const["b2"])
                    for fc in range(32):
                        t.matmul(ps[0:cnt, 0:512], hqb[p][:, fc, 0:cnt],
                                 w2q[:, fc, j * 512:(j + 1) * 512],
                                 start=(fc == 0), stop=False)
                    return t.matmul(ps[0:cnt, 0:512], sv2rb[p][0:1, 0:cnt],
                                    b2r[0:1, j * 512:(j + 1) * 512],
                                    start=False, stop=True)

                def _ev_w2(sc, j, ps, cnt=cnt, p=p, co=co):
                    return sc.activation(
                        x2tb[p][0:cnt, j * 512:(j + 1) * 512],
                        ps[0:cnt, 0:512], AF.Identity,
                        scale=st[0:cnt, co + 26:co + 27])

                t_x2 = pingpong(2, [PSA, PSB], _pe_w2, _ev_w2, "act",
                                [S2_tok[i][0]] + list(pv_x2_free[p]),
                                list(pv_w2_tail))
                pv_w2_tail[:] = [t_x2[0][1], t_x2[1][1]]
                pv_w2_pe[p] = t_x2[1][0]
                T_x2[i] = t_x2

            def emit_Q2(i):
                p = i % 2
                co = 32 * p
                t0, cnt = BTILES[i]
                t_x2 = T_x2[i]
                sq_tok, a_tok, d_tok = emit_ln_stats(
                    x2tb[p][0:cnt, :], cnt, ln2b[p],
                    [t_x2[0][1], t_x2[1][1], pv_out_dve[p]], co=co,
                    apply_out=ln2b[p][0:cnt, :])
                t_fin = [a_tok]

                def _(v, cnt=cnt, i=i, p=p):
                    w8(v, t_fin[0], t_xpl2[i], pv_out_store[p])
                    t_fin.append(fin(v.tensor_add(
                        outtb[p][0:cnt, :], ln2b[p][0:cnt, :],
                        xprb[p][0:cnt, :]), "dve"))
                VEC(_)
                pv_x2_free[p] = [sq_tok, d_tok, t_fin[0]]
                pv_out_dve[p] = t_fin[1]

                def _(g, t0=t0, cnt=cnt, p=p):
                    w8(g, t_fin[1])
                    t_out.append(fin(g.dma_start(
                        d_out[t0:t0 + cnt, :], outtb[p][0:cnt, :]), "gp"))
                    pv_out_store[p] = t_out[-1]
                GPS(_)

                if i + 2 < NT:
                    def _(g, i=i, sq_tok=sq_tok, t_fin=t_fin, t_x2=t_x2, p=p):
                        w8(g, t_x2[1][1], sq_tok, t_fin[1])
                        t2, cnt2 = BTILES[i + 2]
                        t_hql[i + 2] = fin(g.dma_start(
                            hqb[p][:].rearrange("p a b -> p (a b)"),
                            bass.AP(d_hq.tensor, (i + 2) * 128 * FF,
                                    [[FF, 128], [1, FF]])), "gp")
                        t_xpl2[i + 2] = fin(g.dma_start(
                            xprb[p][0:cnt2, :],
                            bass.AP(d_xp.tensor, t2 * HS,
                                    [[HS, cnt2], [1, HS]])), "gp")
                    GPS(_)

            emit_P2(0)
            for i in range(NT):
                emit_M2(i)
                if i + 1 < NT:
                    emit_P2(i + 1)
                emit_Q2(i)

            def _(g):
                for _q in ("gp", "hw"):
                    for _i in range(4):
                        _k = _q + str(_i)
                        g.wait_ge(SEMS[_k], C[_k])
            GPS(_)

    return nc


def _host_inputs(L0_attn_x, L0_attn_pos_emb, L0_attn_Wq, L0_attn_Wk, L0_attn_Wv,
                 L0_attn_Wpost, L0_attn_Wrel, L0_attn_per_dim_scale,
                 L0_ffn_w1, L0_ffn_b1, L0_ffn_w2, L0_ffn_b2):
    f = lambda a: np.ascontiguousarray(np.asarray(a), dtype=np.float32)
    x = f(L0_attn_x)
    B, S, _ = x.shape
    shards = []
    for b in range(B):
        for half in range(2):
            s = half * TQ
            sh = np.zeros((SHARD, HS), np.float32)
            lo = max(0, s - HALO)
            hi = min(S, s + SHARD - HALO)
            sh[lo - (s - HALO):hi - (s - HALO)] = x[b, lo:hi]
            shards.append(sh)

    mask = np.zeros((KTW, H, SUBT), np.float32)
    for j in range(SUBB):
        for qq in range(CHUNK):
            mask[12 * j:12 * j + CTX, :, 12 * j + qq] = 1.0
    mask8 = np.ascontiguousarray(mask.reshape(KTW, H * SUBT)).astype(bf)

    cstv = np.zeros((128, 8), np.float32)
    cstv[:, 0] = 0.5
    cstv[:, 1] = -0.5
    cstv[:, 2] = 1.0
    cstv[:, 3] = EPS

    common = dict(
        wq=f(L0_attn_Wq), wk=f(L0_attn_Wk), wv=f(L0_attn_Wv),
        wpost=f(L0_attn_Wpost), wrel=f(L0_attn_Wrel),
        posT=f(np.asarray(L0_attn_pos_emb).T),
        pds=f(L0_attn_per_dim_scale),
        w1T=f(np.asarray(L0_ffn_w1).T), w2T=f(np.asarray(L0_ffn_w2).T),
        b1=f(L0_ffn_b1), b2=f(L0_ffn_b2),
        cst=cstv,
        identb=np.eye(128, dtype=np.float32).astype(bf),
        onescol_b=np.ones((1, 128), np.float32).astype(bf),
        onescol_f=np.ones((1, 128), np.float32),
        ones128_f=np.ones((128, 1), np.float32),
        ones120_b=np.ones((128, 1), np.float32).astype(bf),
        mask8=mask8,
        bdz=np.zeros(2 * SUB_B, np.float32),
    )
    return [dict(common, xsh=shards[c]) for c in range(8)]


def kernel(**inputs):
    in_maps = _host_inputs(**inputs)
    nc = build_nc()
    res = run_bass_kernel_spmd(nc, in_maps, core_ids=list(range(8)))
    x = np.asarray(inputs["L0_attn_x"])
    B, S, _ = x.shape
    out = np.empty((B, S, HS), np.float32)
    for b in range(B):
        for half in range(2):
            s = half * TQ
            own = min(TQ, S - s)
            out[b, s:s + own] = np.asarray(
                res.results[2 * b + half]["out"])[0:own]
    return out


# version 14
# speedup vs baseline: 1.0057x; 1.0057x over previous
import contextlib
import math

import numpy as np
import ml_dtypes

import concourse.bass as bass
import concourse.mybir as mybir
from concourse.bass_utils import run_bass_kernel_spmd

F32 = mybir.dt.float32
BF16 = mybir.dt.bfloat16
I32 = mybir.dt.int32
AF = mybir.ActivationFunctionType
ALU = mybir.AluOpType
AX = mybir.AxisListType.X
bf = ml_dtypes.bfloat16

HS = 1024
H, D = 8, 128
CHUNK, CTX = 12, 24
PP = CTX + 1
SOFTCAP = 50.0
Q_SCALE = D ** (-0.5) / math.log(2)
K_SCALE = math.log(1.0 + math.e) / math.log(2)
QK = Q_SCALE * K_SCALE
EPS = 1e-5
FF = 4096

NBLK = 171
TQ = NBLK * CHUNK
HALO = 12
SHARD = 2176
NCHUNK = 17

SUBB = 9
SUBT = SUBB * CHUNK
NSUB = NBLK // SUBB
KTW = SUBT + 12
SUPER = [(0, 4), (4, 4), (8, 4), (12, 4), (16, 3)]

CSTRIDE = SUBT * PP
BSTRIDE = SUBB * (CHUNK * KTW + CHUNK)
SUB_C = H * CSTRIDE
SUB_B = H * BSTRIDE

BTILES = [(k * 128, min(128, TQ - k * 128)) for k in range((TQ + 127) // 128)]
NT = len(BTILES)
DEBUG_TAPS = False


def build_nc():
    nc = bass.Bass("TRN2", target_bir_lowering=False, debug=False)

    d_x = nc.dram_tensor("xsh", [SHARD, HS], F32, kind="ExternalInput").ap()
    d_wq = nc.dram_tensor("wq", [HS, HS], F32, kind="ExternalInput").ap()
    d_wk = nc.dram_tensor("wk", [HS, HS], F32, kind="ExternalInput").ap()
    d_wv = nc.dram_tensor("wv", [HS, HS], F32, kind="ExternalInput").ap()
    d_wp = nc.dram_tensor("wpost", [HS, HS], F32, kind="ExternalInput").ap()
    d_wr = nc.dram_tensor("wrel", [HS, HS], F32, kind="ExternalInput").ap()
    d_pt = nc.dram_tensor("posT", [HS, PP], F32, kind="ExternalInput").ap()
    d_pds = nc.dram_tensor("pds", [D], F32, kind="ExternalInput").ap()
    d_w1t = nc.dram_tensor("w1T", [HS, FF], F32, kind="ExternalInput").ap()
    d_w2t = nc.dram_tensor("w2T", [FF, HS], F32, kind="ExternalInput").ap()
    d_b1 = nc.dram_tensor("b1", [FF], F32, kind="ExternalInput").ap()
    d_b2 = nc.dram_tensor("b2", [HS], F32, kind="ExternalInput").ap()
    d_cst = nc.dram_tensor("cst", [128, 8], F32, kind="ExternalInput").ap()
    d_idb = nc.dram_tensor("identb", [128, 128], BF16, kind="ExternalInput").ap()
    d_oc_b = nc.dram_tensor("onescol_b", [1, 128], BF16, kind="ExternalInput").ap()
    d_oc_f = nc.dram_tensor("onescol_f", [1, 128], F32, kind="ExternalInput").ap()
    d_o128f = nc.dram_tensor("ones128_f", [128, 1], F32, kind="ExternalInput").ap()
    d_o120b = nc.dram_tensor("ones120_b", [128, 1], BF16, kind="ExternalInput").ap()
    d_mask = nc.dram_tensor("mask8", [KTW, H * SUBT], BF16, kind="ExternalInput").ap()
    d_out = nc.dram_tensor("out", [TQ, HS], F32, kind="ExternalOutput").ap()

    d_xp = nc.dram_tensor("xp_scr", [TQ * HS], BF16, kind="Internal").ap()
    d_bdc = nc.dram_tensor("bdc_scr", [2 * SUB_C], F32, kind="Internal").ap()
    d_bdb = nc.dram_tensor("bdz", [2 * SUB_B], F32, kind="ExternalInput").ap()
    d_wq1 = nc.dram_tensor("wq1_scr", [HS * FF], BF16, kind="Internal").ap()
    d_wq2 = nc.dram_tensor("wq2_scr", [FF * HS], BF16, kind="Internal").ap()
    d_hq = nc.dram_tensor("hq_scr", [NT * 128 * FF], BF16, kind="Internal").ap()
    d_wscr = nc.dram_tensor("wscr_scr", [512], F32, kind="Internal").ap()
    d_txq = (nc.dram_tensor("tapxq_scr", [NT * 128 * HS], BF16,
                            kind="Internal").ap() if DEBUG_TAPS else None)
    d_tyh = (nc.dram_tensor("tapyh_scr", [NT * 128 * FF], F32,
                            kind="Internal").ap() if DEBUG_TAPS else None)

    es = contextlib.ExitStack()
    with es:
        es.enter_context(nc.allow_low_precision(
            reason="bf16 matmul operands by design; fp32 accumulation kept"))
        SEMS = {}
        for _k in ["pe", "dve", "act"]:
            SEMS[_k] = es.enter_context(nc.semaphore("s_" + _k))
        for _q in ["gp", "hw", "ad"]:
            for _i in range(4):
                SEMS[_q + str(_i)] = es.enter_context(
                    nc.semaphore("s_%s%d" % (_q, _i)))
        SEMS["pl"] = es.enter_context(nc.semaphore("s_pl"))
        C = {k: 0 for k in SEMS}
        DMAQ = {"gp": {"idx": 0, "last": None}, "hw": {"idx": 0, "last": None},
                "ad": {"idx": 0, "last": None}}

        class _DmaQ:
            def __init__(self, eng, q):
                self._e = eng
                self._q = q

            def dma_start(self, *a, **kw):
                q = DMAQ[self._q]
                k = self._q + str(q["idx"] % 4)
                q["idx"] += 1
                self._e.wait_ge(SEMS[k], C[k])
                ins = self._e.dma_start(*a, **kw)
                ins.then_inc(SEMS[k], 16)
                C[k] += 16
                q["last"] = (k, C[k])
                return ins

            def __getattr__(self, name):
                return getattr(self._e, name)

        def GPS(f):
            blk.gpsimd(lambda g: f(_DmaQ(g, "gp")))

        def SYN(f):
            blk.sync(lambda y: f(_DmaQ(y, "hw")))

        def SCD(f):
            blk.scalar(lambda s2: f(_DmaQ(s2, "ad")))

        def fin(ins, k):
            if k in ("dve", "act", "pl"):
                return (k, C[k])
            if k in ("gp", "hw", "ad"):
                return DMAQ[k]["last"]
            ins.then_inc(SEMS[k], 1)
            C[k] += 1
            return (k, C[k])

        def w8(eng, *toks):
            best = {}
            for t in toks:
                if t is None:
                    continue
                k, v = t
                if v > 0 and v > best.get(k, 0):
                    best[k] = v
            for k, v in best.items():
                eng.wait_ge(SEMS[k], v)

        def now():
            return [(k, C[k]) for k in C]

        _CHAIN_OPS = {
            "activation", "tensor_copy", "tensor_tensor", "tensor_add",
            "tensor_sub", "tensor_mul", "tensor_max", "tensor_scalar",
            "tensor_single_scalar", "tensor_scalar_add", "tensor_scalar_sub",
            "tensor_scalar_mul", "tensor_scalar_max", "tensor_scalar_min",
            "tensor_relu", "tensor_reduce", "reduce_max", "reduce_sum",
            "scalar_tensor_tensor", "reciprocal", "memset", "memzero",
            "select", "copy_predicated", "sign", "square", "sqrt", "mul",
            "add", "copy",
        }

        class _Chain:
            def __init__(self, eng, key):
                self._e = eng
                self._k = key

            def __getattr__(self, name):
                f = getattr(self._e, name)
                if name in _CHAIN_OPS:
                    k = self._k

                    def g(*a, **kw):
                        self._e.wait_ge(SEMS[k], C[k])
                        ins = f(*a, **kw)
                        ins.then_inc(SEMS[k], 1)
                        C[k] += 1
                        return ins
                    return g
                return f

        def VEC(f):
            blk.vector(lambda v: f(_Chain(v, "dve")))

        def SCL(f):
            blk.scalar(lambda s: f(_Chain(s, "act")))

        def POOL(f):
            blk.gpsimd(lambda g: f(_Chain(_DmaQ(g, "gp"), "pl")))

        cst = es.enter_context(nc.sbuf_tensor("cst_s", [128, 8], F32))
        idb = es.enter_context(nc.sbuf_tensor("idb_s", [128, 128], BF16))
        oc_b = es.enter_context(nc.sbuf_tensor("ocb_s", [1, 128], BF16))
        oc_f = es.enter_context(nc.sbuf_tensor("ocf_s", [1, 128], F32))
        o128f = es.enter_context(nc.sbuf_tensor("o128f_s", [128, 1], F32))
        o120b = es.enter_context(nc.sbuf_tensor("o120b_s", [128, 1], BF16))
        maskb = es.enter_context(nc.sbuf_tensor("mask_s", [KTW, H * SUBT], BF16))
        b1r = es.enter_context(nc.sbuf_tensor("b1r_s", [1, FF], BF16))
        b2r = es.enter_context(nc.sbuf_tensor("b2r_s", [1, HS], BF16))
        qsc = es.enter_context(nc.sbuf_tensor("qsc_s", [128, 1], F32))
        st = es.enter_context(nc.sbuf_tensor("st_s", [128, 64], F32))
        absx = es.enter_context(nc.sbuf_tensor("absx_s", [128, NT], F32))
        absh = es.enter_context(nc.sbuf_tensor("absh_s", [128, NT], F32))
        mu1b = es.enter_context(nc.sbuf_tensor("mu1b_s", [128, 1], F32))
        mu2b = es.enter_context(nc.sbuf_tensor("mu2b_s", [128, 1], F32))

        PSA = es.enter_context(nc.psum_tensor("psA", [128, 512], F32))
        PSB = es.enter_context(nc.psum_tensor("psB", [128, 512], F32))
        PSC = es.enter_context(nc.psum_tensor("psC", [128, 512], F32))
        PSD = es.enter_context(nc.psum_tensor("psD", [128, 512], F32))
        PT1 = es.enter_context(nc.psum_tensor("psT1", [128, 1024], BF16))
        PT2 = es.enter_context(nc.psum_tensor("psT2", [128, 1024], BF16))
        PT3 = es.enter_context(nc.psum_tensor("psT3", [128, 1024], BF16))
        PT4 = es.enter_context(nc.psum_tensor("psT4", [128, 1024], BF16))
        PTS = [PT1, PT2, PT3, PT4]

        blk = es.enter_context(nc.Block())

        AP_HALF = cst[:, 0:1]
        AP_NHALF = cst[:, 1:2]
        AP_ONE = cst[:, 2:3]
        AP_EPS = cst[:, 3:4]

        def pingpong(n, slots, pe_emit, ev_emit, ev_eng, pre_deps, prev_users):
            toks = []
            ns = len(slots)
            for w0 in range(0, n, ns):
                wN = min(n, w0 + ns)

                def _t(t, w0=w0, wN=wN):
                    for i in range(w0, wN):
                        if i < ns:
                            w8(t, *pre_deps)
                            w8(t, *prev_users)
                        else:
                            w8(t, toks[i - ns][1])
                        last = pe_emit(t, i, slots[i % ns])
                        toks.append([fin(last, "pe"), None])
                blk.tensor(_t)

                def _e(e, w0=w0, wN=wN):
                    for i in range(w0, wN):
                        w8(e, toks[i][0])
                        toks[i][1] = fin(ev_emit(e, i, slots[i % ns]), ev_eng)
                (SCL if ev_eng == "act" else VEC)(_e)
            return toks

        tok_const = {}

        def _(g):
            tok_const["cst"] = fin(g.dma_start(cst[:], d_cst), "gp")
            tok_const["idb"] = fin(g.dma_start(idb[:], d_idb), "gp")
            tok_const["oc_b"] = fin(g.dma_start(oc_b[:], d_oc_b), "gp")
            tok_const["oc_f"] = fin(g.dma_start(oc_f[:], d_oc_f), "gp")
            tok_const["o128f"] = fin(g.dma_start(o128f[:], d_o128f), "gp")
            tok_const["o120b"] = fin(g.dma_start(o120b[:], d_o120b), "gp")
            tok_const["mask"] = fin(g.dma_start(maskb[:], d_mask), "gp")
            tok_const["pds"] = fin(g.dma_start(
                qsc[:], bass.AP(d_pds.tensor, 0, [[1, 128], [1, 1]])), "gp")
            tok_const["b1"] = fin(g.dma_start(b1r[:], d_b1.unsqueeze(0)), "gp")
            tok_const["b2"] = fin(g.dma_start(b2r[:], d_b2.unsqueeze(0)), "gp")
        GPS(_)

        def emit_ln_sq(X, cnt, scratch, deps, co=0, c_extra=False,
                       minmax=False):
            W = X.shape[-1]
            ta, td = [], []

            def sl(a, b=None):
                return st[0:cnt, co + a:co + (b if b is not None else a + 1)]

            def _(sc):
                w8(sc, *deps)
                ta.append(fin(sc.activation(scratch[0:cnt, 0:W], X, AF.Square,
                                            accum_out=sl(1)), "act"))
            SCL(_)

            def _(v):
                w8(v, *deps)
                if minmax:
                    v.tensor_reduce(sl(17), X, AX, ALU.max)
                    v.tensor_reduce(sl(18), X, AX, ALU.min)
                v.reduce_sum(sl(0), X, axis=AX)
                v.tensor_scalar_mul(sl(2), sl(0), 1.0 / W)
                v.tensor_mul(sl(3), sl(2), sl(2))
                w8(v, ta[0])
                v.tensor_scalar(sl(4), sl(1), 1.0 / W, None, ALU.mult)
                last = v.tensor_sub(sl(4), sl(4), sl(3))
                if c_extra:
                    v.tensor_scalar_add(sl(9), sl(4), EPS)
                    v.reciprocal(sl(10), sl(9))
                    last = v.tensor_mul(sl(11), sl(4), sl(10))
                td.append(fin(last, "dve"))
            VEC(_)
            return ta[0], td[0]

        def emit_ln_fin(X, cnt, d_tok, co=0, c_extra=False, apply_out=None,
                        deps=()):
            ta = []

            def sl(a, b=None):
                return st[0:cnt, co + a:co + (b if b is not None else a + 1)]

            def _(sc):
                w8(sc, d_tok, *deps)
                sc.activation(sl(5), sl(4), AF.Ln, bias=cst[0:cnt, 3:4])
                sc.activation(sl(6), sl(5), AF.Exp, scale=-0.5)
                sc.activation(sl(7), sl(6), AF.Identity, scale=-1.0)
                last = sc.activation(sl(8), sl(2), AF.Identity, scale=sl(7))
                if c_extra:
                    sc.activation(sl(12), sl(11), AF.Ln, bias=cst[0:cnt, 3:4])
                    sc.activation(sl(13), sl(12), AF.Exp, scale=-0.5)
                    sc.activation(sl(15), sl(6), AF.Identity, scale=sl(13))
                    sc.activation(sl(16), sl(15), AF.Identity, scale=-1.0)
                    last = sc.activation(sl(14), sl(2), AF.Identity,
                                         scale=sl(16))
                if apply_out is not None:
                    if c_extra:
                        last = sc.activation(apply_out, X, AF.Identity,
                                             scale=sl(15), bias=sl(14))
                    else:
                        last = sc.activation(apply_out, X, AF.Identity,
                                             scale=sl(6), bias=sl(8))
                ta.append(fin(last, "act"))
            SCL(_)
            return ta[0]

        def emit_ln_stats(X, cnt, scratch, deps, c_extra=False, co=0,
                          apply_out=None, minmax=False):
            sq_tok, d_tok = emit_ln_sq(X, cnt, scratch, deps, co=co,
                                       c_extra=c_extra, minmax=minmax)
            a_tok = emit_ln_fin(X, cnt, d_tok, co=co, c_extra=c_extra,
                                apply_out=apply_out)
            return sq_tok, a_tok, d_tok

        def emit_round(Y, Q0I, OUT, deps):
            tk = {}

            def _(v):
                w8(v, *deps)
                v.tensor_copy(Q0I, Y)
                tk["q"] = fin(v.tensor_copy(OUT, Q0I), "dve")
            VEC(_)
            return tk["q"]

        with contextlib.ExitStack() as esA:
            wq_s = esA.enter_context(nc.sbuf_tensor("wq_s", [128, 8, HS], BF16))
            wk_s = esA.enter_context(nc.sbuf_tensor("wk_s", [128, 8, HS], BF16))
            wv_s = esA.enter_context(nc.sbuf_tensor("wv_s", [128, 8, HS], BF16))
            wp_s = esA.enter_context(nc.sbuf_tensor("wp_s", [128, 8, HS], BF16))
            rel_s = esA.enter_context(nc.sbuf_tensor("rel_s", [128, 8, PP], BF16))
            tok_bdb_zero = None

            def _(g):
                for nm, dst, srct in [("wq", wq_s, d_wq), ("wk", wk_s, d_wk),
                                      ("wv", wv_s, d_wv), ("wp", wp_s, d_wp)]:
                    tok_const[nm] = fin(g.dma_start(
                        bass.AP(dst, 0, [[8 * HS, 128], [HS, 8], [1, HS]]),
                        bass.AP(srct.tensor, 0,
                                [[HS, 128], [128 * HS, 8], [1, HS]])), "gp")
            GPS(_)

            t_qs = []

            def _(sc):
                w8(sc, tok_const["pds"], tok_const["cst"])
                sc.activation(st[:, 16:17], qsc[:], AF.Abs)
                sc.activation(st[:, 17:18], st[:, 16:17], AF.Exp, scale=-1.0)
                t_qs.append(fin(sc.activation(st[:, 18:19], st[:, 17:18], AF.Ln,
                                              bias=AP_ONE), "act"))
            SCL(_)

            def _(v):
                w8(v, t_qs[0])
                v.tensor_scalar_max(st[:, 19:20], qsc[:], 0.0)
                v.tensor_add(st[:, 20:21], st[:, 19:20], st[:, 18:19])
                t_qs.append(fin(v.tensor_scalar_mul(qsc[:], st[:, 20:21], QK), "dve"))
            VEC(_)


            with contextlib.ExitStack() as esR:
                wr_s = esR.enter_context(
                    nc.sbuf_tensor("wr_s", [128, 8, HS], BF16))
                pt_s = esR.enter_context(
                    nc.sbuf_tensor("pt_s", [128, 8, PP], BF16))

                def _(g):
                    tok_const["wr"] = fin(g.dma_start(
                        bass.AP(wr_s, 0, [[8 * HS, 128], [HS, 8], [1, HS]]),
                        bass.AP(d_wr.tensor, 0,
                                [[HS, 128], [128 * HS, 8], [1, HS]])), "gp")
                    tok_const["pt"] = fin(g.dma_start(
                        bass.AP(pt_s, 0, [[8 * PP, 128], [PP, 8], [1, PP]]),
                        bass.AP(d_pt.tensor, 0,
                                [[PP, 128], [128 * PP, 8], [1, PP]])), "gp")
                GPS(_)

                def _pe_rel(t, i, ps):
                    for dc in range(8):
                        last = t.matmul(ps[0:128, 0:PP],
                                        wr_s[:, dc, i * 128:(i + 1) * 128],
                                        pt_s[:, dc, :], start=(dc == 0),
                                        stop=(dc == 7))
                    return last

                def _ev_rel(sc, i, ps):
                    return sc.activation(rel_s[:, i, :], ps[0:128, 0:PP],
                                         AF.Identity, scale=1.0 / K_SCALE)

                t_rel = pingpong(H, [PSA, PSB, PSC, PSD], _pe_rel, _ev_rel,
                                 "act", [tok_const["wr"], tok_const["pt"]], [])
                tok_rel = t_rel[H - 1][1]

            wcr = [esA.enter_context(
                nc.sbuf_tensor("wcr%d_s" % i, [128, 512], F32))
                for i in range(4)]
            sg1w = [esA.enter_context(
                nc.sbuf_tensor("sg1%d_s" % i, [128, 512], BF16))
                for i in range(2)]
            sg2w = [esA.enter_context(
                nc.sbuf_tensor("sg2%d_s" % i, [128, 512], BF16))
                for i in range(2)]
            wqow = [esA.enter_context(
                nc.sbuf_tensor("wqo%d_s" % i, [128, 512], BF16))
                for i in range(2)]
            asum = esA.enter_context(nc.sbuf_tensor("asum_s", [128, 66], F32))
            thbw = esA.enter_context(nc.sbuf_tensor("thb_s", [128, 8], F32))
            thrw = esA.enter_context(nc.sbuf_tensor("thr_s", [1, 8], F32))
            prow = esA.enter_context(nc.sbuf_tensor("prow_s", [1, 256], F32))

            def _w_src(wi, u):
                if wi == 0:
                    c2, q2 = divmod(u, 8)
                    return bass.AP(d_w1t.tensor, c2 * 128 * FF + q2 * 512,
                                   [[FF, 128], [1, 512]])
                c2, q2 = divmod(u, 2)
                return bass.AP(d_w2t.tensor, c2 * 128 * HS + q2 * 512,
                               [[HS, 128], [1, 512]])

            def _w_dst(wi, u):
                if wi == 0:
                    c2, q2 = divmod(u, 8)
                    return bass.AP(d_wq1.tensor, c2 * 128 * FF + q2 * 512,
                                   [[FF, 128], [1, 512]])
                c2, q2 = divmod(u, 2)
                return bass.AP(d_wq2.tensor, c2 * 128 * HS + q2 * 512,
                               [[HS, 128], [1, 512]])

            WUNITS = ([("scan", 0, u) for u in range(64)] + [("fin", 0, 0)] +
                      [("sign", 0, u) for u in range(64)] +
                      [("scan", 1, u) for u in range(64)] + [("fin", 1, 0)] +
                      [("sign", 1, u) for u in range(64)])
            WLOADS = [un for un in WUNITS if un[0] != "fin"]
            WLIDX = {un: j for j, un in enumerate(WLOADS)}
            w_load_tok, w_comp_tok, w_fin_tok = {}, {}, {}
            w_asum_rd, w_prow_rd = {}, {}
            w_done = {0: [], 1: []}
            w_sg_pool = [None, None]
            w_wqo_st = [None, None]
            w_li = [0]
            w_ci = [0]

            def _w_emit_loads(upto):
                lo, hi = w_li[0], min(upto, len(WLOADS))
                if lo >= hi:
                    return

                def _(sd):
                    for j in range(lo, hi):
                        un = WLOADS[j]
                        if j >= 4:
                            w8(sd, w_comp_tok.get(WLOADS[j - 4]))
                        else:
                            w8(sd, tok_rel)
                        w_load_tok[un] = fin(sd.dma_start(
                            wcr[j % 4][:], _w_src(un[1], un[2])), "ad")
                SCD(_)
                w_li[0] = hi

            def _w_emit_fin(wi):
                base = 4 * wi
                po = 128 * wi
                so = 256 * wi
                toks = []

                def _(v):
                    w8(v, *[w_comp_tok[("scan", wi, u)] for u in range(64)])
                    w8(v, w_asum_rd.get(1 - wi))
                    toks.append(fin(v.tensor_reduce(
                        asum[:, 64:65], asum[:, 0:64], AX, ALU.add), "dve"))
                VEC(_)

                def _(g):
                    w8(g, toks[0])
                    toks.append(fin(g.dma_start(
                        bass.AP(d_wscr.tensor, so, [[1, 128]]),
                        asum[:, 64:65]), "gp"))
                GPS(_)
                w_asum_rd[wi] = toks[1]

                def _(g):
                    w8(g, toks[1], w_prow_rd.get(1 - wi))
                    toks.append(fin(g.dma_start(
                        prow[0:1, po:po + 128],
                        bass.AP(d_wscr.tensor, so, [[1, 128]])), "gp"))
                GPS(_)

                def _(v):
                    w8(v, toks[2])
                    v.tensor_reduce(thrw[0:1, base:base + 1],
                                    prow[0:1, po:po + 128], AX, ALU.add)
                    v.tensor_scalar_mul(thrw[0:1, base:base + 1],
                                        thrw[0:1, base:base + 1],
                                        1.0 / (HS * FF))
                    v.tensor_scalar_max(thrw[0:1, base:base + 1],
                                        thrw[0:1, base:base + 1], 1e-5)
                    v.tensor_scalar_mul(thrw[0:1, base + 1:base + 2],
                                        thrw[0:1, base:base + 1], 0.5)
                    toks.append(fin(v.tensor_scalar_mul(
                        thrw[0:1, base + 2:base + 3],
                        thrw[0:1, base + 1:base + 2], -1.0), "dve"))
                VEC(_)
                w_prow_rd[wi] = toks[3]

                def _(g):
                    w8(g, toks[3])
                    toks.append(fin(g.dma_start(
                        bass.AP(d_wscr.tensor, so + 128, [[1, 3]]),
                        thrw[0:1, base:base + 3]), "gp"))
                GPS(_)

                def _(g):
                    w8(g, toks[4])
                    t_thb = fin(g.dma_start(
                        thbw[:, 3 * wi:3 * wi + 3],
                        bass.AP(d_wscr.tensor, so + 128,
                                [[0, 128], [1, 3]])), "gp")
                    mub = mu1b if wi == 0 else mu2b
                    t_mub = fin(g.dma_start(
                        mub[:, 0:1],
                        bass.AP(d_wscr.tensor, so + 128,
                                [[0, 128], [1, 1]])), "gp")
                    toks.append((t_thb, t_mub))
                GPS(_)
                w_fin_tok[wi] = toks[5]

            def wpump(k):
                if w_ci[0] >= len(WUNITS):
                    return
                batch = []
                while len(batch) < k and w_ci[0] < len(WUNITS):
                    un = WUNITS[w_ci[0]]
                    if un[0] != "fin" and WLIDX[un] >= w_li[0]:
                        break
                    batch.append(un)
                    w_ci[0] += 1
                    if un[0] == "fin":
                        break
                scans = [u2 for u2 in batch if u2[0] == "scan"]
                signs = [u2 for u2 in batch if u2[0] == "sign"]
                fins = [u2 for u2 in batch if u2[0] == "fin"]
                if scans:
                    def _(v, scans=scans):
                        for un in scans:
                            j = WLIDX[un]
                            w8(v, w_load_tok[un])
                            w_comp_tok[un] = fin(v.tensor_reduce(
                                asum[:, un[2]:un[2] + 1], wcr[j % 4][:],
                                AX, ALU.add, apply_absolute_value=True),
                                "dve")
                    VEC(_)
                for pi in range(0, len(signs), 2):
                    pair = signs[pi:pi + 2]

                    def _(sc, pair=pair):
                        for pj, un in enumerate(pair):
                            wi, u = un[1], un[2]
                            j = WLIDX[un]
                            w8(sc, w_load_tok[un], *w_fin_tok[wi])
                            w8(sc, w_sg_pool[pj])
                            sc.activation(sg1w[pj][:], wcr[j % 4][:], AF.Sign,
                                          bias=thbw[:, 3 * wi + 2:3 * wi + 3])
                            w_comp_tok[un] = fin(sc.activation(
                                sg2w[pj][:], wcr[j % 4][:], AF.Sign,
                                bias=thbw[:, 3 * wi + 1:3 * wi + 2]), "act")
                    SCL(_)

                    def _(g, pair=pair):
                        for pj, un in enumerate(pair):
                            wi, u = un[1], un[2]
                            w8(g, w_comp_tok[un], w_wqo_st[pj])
                            t_add = fin(g.tensor_add(
                                wqow[pj][:], sg1w[pj][:], sg2w[pj][:]), "pl")
                            w_sg_pool[pj] = t_add
                            w8(g, t_add)
                            tst = fin(g.dma_start(_w_dst(wi, u),
                                                  wqow[pj][:]), "gp")
                            w_wqo_st[pj] = tst
                            w_done[wi].append(tst)
                    POOL(_)
                for un in fins:
                    _w_emit_fin(un[1])
                ncl = sum(1 for x2 in WUNITS[:w_ci[0]] if x2[0] != "fin")
                _w_emit_loads(ncl + 4)

            _w_emit_loads(4)


            xn = esA.enter_context(nc.sbuf_tensor("xn_s", [128, 8, SHARD], BF16))
            xt0 = esA.enter_context(nc.sbuf_tensor("xt0_s", [128, HS], F32))
            xt1 = esA.enter_context(nc.sbuf_tensor("xt1_s", [128, HS], F32))
            xt2 = esA.enter_context(nc.sbuf_tensor("xt2_s", [128, HS], F32))
            xnt = esA.enter_context(nc.sbuf_tensor("xnt_s", [128, HS], BF16))
            q_st = esA.enter_context(nc.sbuf_tensor("q_st", [128, 8, 432], BF16))
            k_st = esA.enter_context(nc.sbuf_tensor("k_st", [128, 8, 444], BF16))
            a_st = esA.enter_context(nc.sbuf_tensor("a_st", [128, 8, 432], BF16))
            v_s0 = esA.enter_context(nc.sbuf_tensor("v_s0", [128, HS], BF16))
            v_s1 = esA.enter_context(nc.sbuf_tensor("v_s1", [128, HS], BF16))
            bdst2 = [esA.enter_context(
                nc.sbuf_tensor("bdst%d_s" % _i, [128, H, PP], F32))
                for _i in range(2)]
            scst = esA.enter_context(nc.sbuf_tensor("scst_s", [128, H, KTW], F32))
            bdbs2 = [esA.enter_context(
                nc.sbuf_tensor("bdbs%d_s" % _i, [128, H, KTW], F32))
                for _i in range(2)]
            s1st = esA.enter_context(nc.sbuf_tensor("s1st_s", [128, H, KTW], F32))
            ths = esA.enter_context(nc.sbuf_tensor("ths_s", [128, H, KTW], BF16))
            emst = esA.enter_context(nc.sbuf_tensor("emst_s", [KTW, H, SUBT], BF16))
            rcr = esA.enter_context(nc.sbuf_tensor("rcr_s", [1, H * SUBT], BF16))
            rbst = esA.enter_context(nc.sbuf_tensor("rbst_s", [128, H * SUBT], F32))
            y_fm = esA.enter_context(nc.sbuf_tensor("yfm_s", [128, 8, 432], BF16))
            y_tm = esA.enter_context(nc.sbuf_tensor("ytm_s", [128, HS], BF16))
            lny = esA.enter_context(nc.sbuf_tensor("lny_s", [128, HS], F32))
            xpt = esA.enter_context(nc.sbuf_tensor("xpt_s", [128, HS], BF16))

            xtiles = [xt0, xt1, xt2]
            t_xld = [None] * NCHUNK
            prev_tr = []
            t_a1_last = [None] * NCHUNK

            def _(g):
                for c2 in range(3):
                    t_xld[c2] = fin(g.dma_start(
                        xtiles[c2][:], d_x[c2 * 128:(c2 + 1) * 128, :]), "gp")
            GPS(_)

            prev_xnt_pe = None
            A1sq = [None] * NCHUNK
            A1sq[0] = emit_ln_sq(xtiles[0][:], 128, xnt,
                                 [t_xld[0], tok_const["cst"]], co=0)
            for c in range(NCHUNK):
                wpump(4)
                X = xtiles[c % 3]
                co = 32 * (c % 2)
                if c + 1 < NCHUNK:
                    A1sq[c + 1] = emit_ln_sq(
                        xtiles[(c + 1) % 3][:], 128, xnt,
                        [t_xld[c + 1], prev_xnt_pe], co=32 * ((c + 1) % 2))
                sq_tok, d_tok = A1sq[c]
                a_tok = emit_ln_fin(X[:], 128, d_tok, co=co,
                                    apply_out=xnt[:], deps=(prev_xnt_pe,))
                t_ap = [a_tok]

                def _pe_tr(t, i, pt, c=c, t_ap=t_ap):
                    if i == 0:
                        w8(t, t_ap[0], tok_const["idb"])
                    return t.transpose(pt[0:128, 0:128],
                                       xnt[:, i * 128:(i + 1) * 128], idb[:])

                def _ev_tr(v, i, pt, c=c):
                    return v.tensor_copy(xn[:, i, c * 128:(c + 1) * 128],
                                         pt[0:128, 0:128])

                t_tr = pingpong(8, PTS, _pe_tr, _ev_tr, "dve",
                                [], list(prev_tr))
                prev_tr = [t_tr[4][1], t_tr[5][1], t_tr[6][1], t_tr[7][1]]
                t_a1_last[c] = t_tr[7][1]
                prev_xnt_pe = t_tr[7][0]

                if c + 3 < NCHUNK:
                    def _(g, c=c, sq_tok=sq_tok, d_tok=d_tok, t_ap=t_ap):
                        w8(g, t_ap[0], sq_tok, d_tok)
                        t_xld[c + 3] = fin(g.dma_start(
                            xtiles[c % 3][:],
                            d_x[(c + 3) * 128:(c + 4) * 128, :]), "gp")
                    GPS(_)

            tok_xn = now()

            last_super = now()
            prev_ep = list(tok_xn)
            prev_pv_evac = []
            prev_rb_acts = []
            prev_v_evac = []
            t_pv = []
            cd_users = []
            t_bdw2 = [None, None]
            t_shr = [None, None]
            t_rdb2 = [None, None]
            pv_s1 = [None, None]
            pv_tanh = [None]
            t_bd_h = [None, None]
            pv_qk_pe = [None]
            pv_wp_ev = []
            for si, (sub0, nsub) in enumerate(SUPER):
                S = sub0 * SUBT
                T = nsub * SUBT
                sh = S + HALO

                def _pe_q(t, i, ps, S=S, T=T, sh=sh):
                    if i == 0:
                        w8(t, *tok_xn)
                        w8(t, tok_const["wq"], tok_const["wk"], t_qs[-1],
                           pv_qk_pe[0])
                    wgt, width, off = (wq_s, T, sh) if i < 8 else \
                        (wk_s, T + 12, sh - 12)
                    hc = i % 8
                    for dc in range(8):
                        last = t.matmul(ps[0:128, 0:width],
                                        wgt[:, dc, hc * 128:(hc + 1) * 128],
                                        xn[:, dc, off:off + width],
                                        start=(dc == 0), stop=(dc == 7))
                    return last

                def _ev_q(sc, i, ps, T=T):
                    hc = i % 8
                    if i < 8:
                        return sc.activation(q_st[:, hc, 0:T], ps[0:128, 0:T],
                                             AF.Identity, scale=qsc[:, 0:1])
                    return sc.activation(k_st[:, hc, 0:T + 12],
                                         ps[0:128, 0:T + 12], AF.Identity)

                t_proj = pingpong(16, [PSA, PSB, PSC, PSD], _pe_q, _ev_q, "act",
                                  [], list(prev_rb_acts) + list(prev_pv_evac)
                                  + list(pv_wp_ev))

                cd_users[:] = [t_proj[14][1], t_proj[15][1]]
                prev_pv_evac = []

                def emit_bd(su):
                    p = su % 2
                    s0 = su * SUBT

                    def _pe_bd(t, i, ps, s0=s0, p=p):
                        if i == 0:
                            w8(t, t_proj[7][1], tok_rel, t_bdw2[p])
                        return t.matmul(ps[0:SUBT, 0:PP],
                                        q_st[:, i, s0:s0 + SUBT],
                                        rel_s[:, i, :], start=True, stop=True)

                    def _ev_bd(sc, i, ps, p=p):
                        return sc.activation(bdst2[p][0:SUBT, i, :],
                                             ps[0:SUBT, 0:PP], AF.Identity)

                    t_bd = pingpong(H, [PSC, PSD], _pe_bd, _ev_bd, "act",
                                    [], list(cd_users))
                    cd_users[:] = [t_bd[H - 2][1], t_bd[H - 1][1]]
                    t_bd_h[p] = t_bd

                    def _(y, p=p):
                        w8(y, t_bd[H - 1][1], t_shr[p])
                        t_bdw2[p] = fin(y.dma_start(
                            bass.AP(d_bdc.tensor, p * SUB_C,
                                    [[PP, SUBT], [CSTRIDE, H], [1, PP]]),
                            bass.AP(bdst2[p], 0,
                                    [[H * PP, SUBT], [PP, H], [1, PP]])), "hw")
                        w8(y, t_bdw2[p], t_rdb2[p])
                        t_shr[p] = fin(y.dma_start(
                            bass.AP(d_bdb.tensor, p * SUB_B,
                                    [[CHUNK * KTW + CHUNK, H * SUBB],
                                     [KTW, CHUNK], [1, CTX]]),
                            bass.AP(d_bdc.tensor, p * SUB_C,
                                    [[CHUNK * PP, H * SUBB],
                                     [CTX, CHUNK], [1, CTX]])), "hw")
                        w8(y, t_shr[p], pv_s1[p])
                        t_rdb2[p] = fin(y.dma_start(
                            bass.AP(bdbs2[p], 0,
                                    [[H * KTW, SUBT], [KTW, H], [1, KTW]]),
                            bass.AP(d_bdb.tensor, p * SUB_B,
                                    [[KTW, SUBT], [BSTRIDE, H], [1, KTW]])),
                            "hw")
                    SYN(_)

                emit_bd(0)
                for su in range(nsub):
                    wpump(4)
                    p_su = su % 2
                    s0 = su * SUBT
                    w0 = sh + s0 - 12
                    if su + 1 < nsub:
                        emit_bd(su + 1)
                    t_bd = t_bd_h[p_su]

                    vbuf = v_s0 if su % 2 == 0 else v_s1

                    def _pe_v(t, i, ps, w0=w0):
                        if i == 0:
                            w8(t, tok_const["wv"], t_proj[12][1], t_proj[13][1])
                            w8(t, *prev_v_evac)
                        for dc in range(8):
                            last = t.matmul(ps[0:KTW, 0:512],
                                            xn[:, dc, w0:w0 + KTW],
                                            wv_s[:, dc, i * 512:(i + 1) * 512],
                                            start=(dc == 0), stop=(dc == 7))
                        return last

                    def _ev_v(v, i, ps, vbuf=vbuf):
                        return v.tensor_copy(
                            vbuf[0:KTW, i * 512:(i + 1) * 512], ps[0:KTW, 0:512])

                    t_vp = pingpong(2, [PSA, PSB], _pe_v, _ev_v, "dve",
                                    [], [t_proj[12][1], t_proj[13][1]]
                                    + list(prev_rb_acts))
                    prev_v_evac = [t_vp[0][1], t_vp[1][1]]

                    def _pe_s(t, i, ps, s0=s0, t_bd=t_bd):
                        if i < 2:
                            w8(t, t_bd[H - 2 + i][1])
                        return t.matmul(ps[0:SUBT, 0:KTW],
                                        q_st[:, i, s0:s0 + SUBT],
                                        k_st[:, i, s0:s0 + KTW],
                                        start=True, stop=True)

                    def _ev_s(v, i, ps):
                        return v.tensor_copy(scst[0:SUBT, i, :],
                                             ps[0:SUBT, 0:KTW])

                    t_sc = pingpong(H, [PSC, PSD], _pe_s, _ev_s, "dve",
                                    [], list(cd_users))
                    cd_users[:] = [t_sc[H - 2][1], t_sc[H - 1][1]]
                    pv_qk_pe[0] = t_sc[H - 1][0]

                    t_s1 = []

                    def _(v, p_su=p_su):
                        w8(v, t_sc[H - 1][1], t_rdb2[p_su], pv_tanh[0])
                        t_s1.append(fin(v.tensor_add(
                            s1st[0:SUBT, :, :].rearrange("p a b -> p (a b)"),
                            scst[0:SUBT, :, :].rearrange("p a b -> p (a b)"),
                            bdbs2[p_su][0:SUBT, :, :].rearrange(
                                "p a b -> p (a b)")), "dve"))
                    VEC(_)
                    pv_s1[p_su] = t_s1[0]

                    t_tanh = []

                    def _(sc):
                        w8(sc, t_s1[0])
                        t_tanh.append(fin(sc.activation(
                            ths[0:SUBT, :, :].rearrange("p a b -> p (a b)"),
                            s1st[0:SUBT, :, :].rearrange("p a b -> p (a b)"),
                            AF.Tanh, scale=1.0 / SOFTCAP), "act"))
                    SCL(_)
                    pv_tanh[0] = t_tanh[0]

                    def _pe_te(t, i, pt):
                        if i == 0:
                            w8(t, t_tanh[0])
                        return t.transpose(pt[0:KTW, 0:SUBT],
                                           ths[0:SUBT, i, :],
                                           idb[0:SUBT, 0:SUBT])

                    def _ev_te(sc, i, pt):
                        return sc.activation(emst[:, i, :], pt[0:KTW, 0:SUBT],
                                             AF.Exp, scale=SOFTCAP)

                    t_te = pingpong(H, PTS, _pe_te, _ev_te, "act",
                                    [], [t_tanh[0]])

                    t_mask = []

                    def _(v):
                        w8(v, t_te[H - 1][1], tok_const["mask"])
                        t_mask.append(fin(v.tensor_mul(
                            emst[:].rearrange("p a b -> p (a b)"),
                            emst[:].rearrange("p a b -> p (a b)"),
                            maskb[:]), "dve"))
                    VEC(_)

                    t_den = []

                    def _(t):
                        w8(t, t_mask[0], tok_const["o120b"])
                        w8(t, *cd_users)
                        t.matmul(PSC[0:1, 0:432], o120b[0:KTW, 0:1],
                                 emst[:].rearrange("p a b -> p (a b)")[:, 0:432],
                                 start=True, stop=True)
                        _m2 = t.matmul(
                            PSD[0:1, 0:432], o120b[0:KTW, 0:1],
                            emst[:].rearrange("p a b -> p (a b)")[:, 432:864],
                            start=True, stop=True)
                        t_den.append(fin(_m2, "pe"))
                    blk.tensor(_)

                    def _(v):
                        w8(v, t_den[0])
                        v.reciprocal(rcr[0:1, 0:432], PSC[0:1, 0:432])
                        t_den.append(fin(v.reciprocal(rcr[0:1, 432:864],
                                                      PSD[0:1, 0:432]), "dve"))
                    VEC(_)
                    cd_users[:] = [t_den[1]]

                    def _pe_rb(t, i, ps):
                        if i == 0:
                            w8(t, t_den[1], tok_const["oc_b"])
                        return t.matmul(ps[0:128, 0:432], oc_b[:],
                                        rcr[0:1, i * 432:(i + 1) * 432],
                                        start=True, stop=True)

                    def _ev_rb(sc, i, ps):
                        return sc.activation(rbst[:, i * 432:(i + 1) * 432],
                                             ps[0:128, 0:432], AF.Identity)

                    wpump(4)
                    t_rb = pingpong(2, [PSA, PSB], _pe_rb, _ev_rb, "act",
                                    [], [t_vp[0][1], t_vp[1][1]])
                    prev_rb_acts = [t_rb[0][1], t_rb[1][1]]

                    def _pe_pv(t, i, ps, vbuf=vbuf):
                        if i == 0:
                            w8(t, t_rb[1][1])
                        return t.matmul(ps[0:128, 0:SUBT],
                                        vbuf[0:KTW, i * 128:(i + 1) * 128],
                                        emst[:, i, :], start=True, stop=True)

                    def _ev_pv(v, i, ps, s0=s0):
                        return v.tensor_mul(
                            a_st[:, i, s0:s0 + SUBT], ps[0:128, 0:SUBT],
                            rbst[:, i * SUBT:(i + 1) * SUBT])

                    t_pv = pingpong(H, [PSC, PSD], _pe_pv, _ev_pv, "dve",
                                    [], list(cd_users))
                    cd_users[:] = [t_pv[H - 2][1], t_pv[H - 1][1]]
                    prev_pv_evac = [t_pv[H - 2][1], t_pv[H - 1][1]]

                def _pe_wp(t, i, ps, T=T):
                    if i == 0:
                        w8(t, t_pv[H - 1][1], tok_const["wp"])
                        w8(t, *prev_rb_acts)
                    for dci in range(8):
                        last = t.matmul(ps[0:128, 0:T],
                                        wp_s[:, dci, i * 128:(i + 1) * 128],
                                        a_st[:, dci, 0:T], start=(dci == 0),
                                        stop=(dci == 7))
                    return last

                def _ev_wp(sc, i, ps, T=T):
                    return sc.activation(y_fm[:, i, 0:T], ps[0:128, 0:T],
                                         AF.Identity)

                t_wp = pingpong(8, [PSA, PSB], _pe_wp, _ev_wp, "act",
                                [], list(prev_rb_acts))
                prev_rb_acts = []
                pv_wp_ev[:] = [t_wp[6][1], t_wp[7][1]]

                a = 0
                while a < T:
                    wpump(4)
                    cnt = min(128, T - a)

                    def _pe_yt(t, i, pt, a=a, cnt=cnt, prev_ep=list(prev_ep)):
                        if i == 0:
                            w8(t, t_wp[7][1], *prev_ep)
                        return t.transpose(pt[0:cnt, 0:128],
                                           y_fm[:, i, a:a + cnt], idb[:])

                    def _ev_yt(v, i, pt, cnt=cnt):
                        return v.tensor_copy(
                            y_tm[0:cnt, i * 128:(i + 1) * 128], pt[0:cnt, 0:128])

                    t_yt = pingpong(8, PTS, _pe_yt, _ev_yt, "dve",
                                    [], list(prev_tr))
                    prev_tr = [t_yt[4][1], t_yt[5][1], t_yt[6][1], t_yt[7][1]]

                    t_xr = []

                    def _(g, S=S, a=a, cnt=cnt, prev_ep=list(prev_ep)):
                        w8(g, *prev_ep)
                        t_xr.append(fin(g.dma_start(
                            xt0[0:cnt, :],
                            d_x[HALO + S + a:HALO + S + a + cnt, :]), "gp"))
                    GPS(_)

                    sq_tok, a_tok, d_tok = emit_ln_stats(
                        y_tm[0:cnt, :], cnt, xnt, [t_yt[7][1]],
                        apply_out=lny[0:cnt, :])
                    t_ep = [a_tok]

                    def _(v, cnt=cnt):
                        w8(v, t_ep[0], t_xr[0])
                        t_ep.append(fin(v.tensor_add(
                            xpt[0:cnt, :], lny[0:cnt, :], xt0[0:cnt, :]), "dve"))
                    VEC(_)

                    def _(g, S=S, a=a, cnt=cnt):
                        w8(g, t_ep[1])
                        t_ep.append(fin(g.dma_start(
                            bass.AP(d_xp.tensor, (S + a) * HS,
                                    [[HS, cnt], [1, HS]]), xpt[0:cnt, :]), "gp"))
                    GPS(_)
                    prev_ep = [t_ep[1], t_ep[2], sq_tok]
                    a += cnt

                last_super = now()

            while w_ci[0] < len(WUNITS):
                wpump(6)
            tok_A = now()

        tok_W = now()

        with contextlib.ExitStack() as esB:
            w1q = esB.enter_context(nc.sbuf_tensor("w1q_s", [128, 8, FF], BF16))
            xpb, xnrb, hscb, xqtmb, xqfmb, svrb, svcb = \
                [], [], [], [], [], [], []
            htmb, hqtmb, hqfmb = [], [], []
            for p in range(2):
                xpb.append(esB.enter_context(
                    nc.sbuf_tensor("xp%d_s" % p, [128, HS], F32)))
                xnrb.append(esB.enter_context(
                    nc.sbuf_tensor("xnr%d_s" % p, [128, HS], F32)))
                hscb.append(esB.enter_context(
                    nc.sbuf_tensor("hsc%d_s" % p, [128, 16], F32)))
                xqtmb.append(esB.enter_context(
                    nc.sbuf_tensor("xqtm%d_s" % p, [128, HS], BF16)))
                xqfmb.append(esB.enter_context(
                    nc.sbuf_tensor("xqfm%d_s" % p, [128, 8, 128], BF16)))
                svrb.append(esB.enter_context(
                    nc.sbuf_tensor("svr%d_s" % p, [1, 128], BF16)))
                svcb.append(esB.enter_context(
                    nc.sbuf_tensor("svc%d_s" % p, [128, 1], BF16)))
                htmb.append(esB.enter_context(
                    nc.sbuf_tensor("htm%d_s" % p, [128, FF], F32)))
                hqtmb.append(esB.enter_context(
                    nc.sbuf_tensor("hqtm%d_s" % p, [128, FF], BF16)))
                hqfmb.append(esB.enter_context(
                    nc.sbuf_tensor("hqfm%d_s" % p, [128, 32, 128], BF16)))
            yxs = esB.enter_context(nc.sbuf_tensor("yxs_s", [128, HS], F32))
            q0xs = esB.enter_context(nc.sbuf_tensor("q0xs_s", [128, HS], I32))
            yhfs = esB.enter_context(nc.sbuf_tensor("yhfs_s", [128, FF], F32))
            q0hs = esB.enter_context(nc.sbuf_tensor("q0hs_s", [128, FF], I32))

            t_w1l = []

            def _(g):
                w8(g, *w_done[0])
                t_w1l.append(fin(g.dma_start(
                    bass.AP(w1q, 0, [[8 * FF, 128], [FF, 8], [1, FF]]),
                    bass.AP(d_wq1.tensor, 0, [[FF, 128], [128 * FF, 8], [1, FF]])),
                    "gp"))
            GPS(_)

            t_xpl = [None] * NT
            t_store = []

            def _(g):
                w8(g, *tok_W)
                for c2 in range(2):
                    t0, cnt0 = BTILES[c2]
                    t_xpl[c2] = fin(g.dma_start(
                        xpb[c2][0:cnt0, :],
                        bass.AP(d_xp.tensor, t0 * HS, [[HS, cnt0], [1, HS]])),
                        "gp")
            GPS(_)

            prev_tr = []
            t_tapyh = [None]
            pv_stats_free = [[], []]
            pv_hfree = [[], []]
            pv_xtr_pe = [None, None]
            pv_htr_pe = [None, None]
            pv_w1_tail = []
            pv_sv_pe = [None, None]
            pv_xnr_dve = [[], []]
            pv_rnd_x = [None]
            pv_rnd_h = [None]
            pv_w1_pe = [None, None]
            pv_hq_store = [None, None]
            S_tq = [None] * NT
            S_vec = [None] * NT
            X_sv = [None] * NT
            X_tr = [None] * NT
            W_ev = [None] * NT
            H_tq = [None] * NT

            def emit_S(k):
                p = k % 2
                co = 32 * p
                t0, cnt = BTILES[k]
                sq_tok, a_tok, d_tok = emit_ln_stats(
                    xpb[p][0:cnt, :], cnt, xnrb[p],
                    [t_xpl[k]], c_extra=True, co=co, minmax=True)
                t_x1 = [a_tok]

                def _(v, cnt=cnt, k=k, p=p, co=co):
                    w8(v, t_x1[0], *tok_W)
                    w8(v, pv_sv_pe[p])
                    v.tensor_sub(st[0:cnt, co + 19:co + 20],
                                 st[0:cnt, co + 17:co + 18],
                                 st[0:cnt, co + 2:co + 3])
                    v.tensor_sub(st[0:cnt, co + 20:co + 21],
                                 st[0:cnt, co + 2:co + 3],
                                 st[0:cnt, co + 18:co + 19])
                    v.tensor_max(st[0:cnt, co + 20:co + 21],
                                 st[0:cnt, co + 20:co + 21],
                                 st[0:cnt, co + 19:co + 20])
                    v.tensor_mul(st[0:cnt, co + 20:co + 21],
                                 st[0:cnt, co + 20:co + 21],
                                 st[0:cnt, co + 15:co + 16])
                    v.tensor_scalar_max(absx[0:cnt, k:k + 1],
                                        st[0:cnt, co + 20:co + 21], 1e-5)
                    v.reciprocal(st[0:cnt, co + 21:co + 22],
                                 absx[0:cnt, k:k + 1])
                    v.tensor_mul(st[0:cnt, co + 22:co + 23],
                                 absx[0:cnt, k:k + 1],
                                 st[0:cnt, co + 21:co + 22])
                    v.tensor_scalar(st[0:cnt, co + 22:co + 23],
                                    st[0:cnt, co + 22:co + 23], -1.0, 2.0,
                                    ALU.mult, ALU.add)
                    v.tensor_mul(st[0:cnt, co + 21:co + 22],
                                 st[0:cnt, co + 21:co + 22],
                                 st[0:cnt, co + 22:co + 23])
                    v.tensor_scalar_mul(st[0:cnt, co + 23:co + 24],
                                        st[0:cnt, co + 21:co + 22], 127.0)
                    v.tensor_mul(st[0:cnt, co + 28:co + 29],
                                 st[0:cnt, co + 15:co + 16],
                                 st[0:cnt, co + 23:co + 24])
                    v.tensor_mul(st[0:cnt, co + 29:co + 30],
                                 st[0:cnt, co + 14:co + 15],
                                 st[0:cnt, co + 23:co + 24])
                    v.reciprocal(st[0:cnt, co + 24:co + 25], mu1b[0:cnt, 0:1])
                    v.tensor_mul(st[0:cnt, co + 25:co + 26],
                                 st[0:cnt, co + 21:co + 22],
                                 st[0:cnt, co + 24:co + 25])
                    v.tensor_scalar_mul(st[0:cnt, co + 25:co + 26],
                                        st[0:cnt, co + 25:co + 26], 254.0)
                    v.tensor_copy(svcb[p][0:cnt, 0:1],
                                  st[0:cnt, co + 25:co + 26])
                    v.tensor_mul(st[0:cnt, co + 26:co + 27],
                                 absx[0:cnt, k:k + 1], mu1b[0:cnt, 0:1])
                    t_x1.append(fin(v.tensor_scalar_mul(
                        st[0:cnt, co + 26:co + 27],
                        st[0:cnt, co + 26:co + 27], 1.0 / 254.0), "dve"))
                VEC(_)

                def _(sc, cnt=cnt, p=p, co=co):
                    w8(sc, t_x1[1], pv_rnd_x[0])
                    t_x1.append(fin(sc.activation(
                        yxs[0:cnt, :], xpb[p][0:cnt, :], AF.Identity,
                        scale=st[0:cnt, co + 28:co + 29],
                        bias=st[0:cnt, co + 29:co + 30]), "act"))
                SCL(_)
                pv_stats_free[p] = [sq_tok, d_tok, t_x1[1], t_x1[2]]

                t_q = emit_round(yxs[0:cnt, :], q0xs[0:cnt, :],
                                 xqtmb[p][0:cnt, :], [t_x1[2], pv_xtr_pe[p]])
                pv_rnd_x[0] = t_q
                S_tq[k] = t_q
                S_vec[k] = t_x1[1]

                if k + 2 < NT:
                    def _(g, k=k, p=p):
                        w8(g, *pv_stats_free[p])
                        t2, cnt2 = BTILES[k + 2]
                        t_xpl[k + 2] = fin(g.dma_start(
                            xpb[p][0:cnt2, :],
                            bass.AP(d_xp.tensor, t2 * HS,
                                    [[HS, cnt2], [1, HS]])), "gp")
                    GPS(_)

            def emit_X(i):
                nonlocal prev_tr
                p = i % 2
                t0, cnt = BTILES[i]

                def _pe_xt(t, j, pt, cnt=cnt, i=i, p=p):
                    if j == 0:
                        w8(t, S_tq[i])
                    return t.transpose(pt[0:128, 0:cnt],
                                       xqtmb[p][0:cnt, j * 128:(j + 1) * 128],
                                       idb[0:cnt, 0:cnt])

                def _ev_xt(v, j, pt, cnt=cnt, p=p):
                    return v.tensor_copy(xqfmb[p][:, j, 0:cnt],
                                         pt[0:128, 0:cnt])

                t_xtr = pingpong(8, PTS, _pe_xt, _ev_xt, "dve",
                                 [pv_w1_pe[p]], list(prev_tr))
                prev_tr = [t_xtr[j][1] for j in range(4, 8)]
                pv_xtr_pe[p] = t_xtr[7][0]
                X_tr[i] = t_xtr

                t_sv = []

                def _(t, cnt=cnt, i=i, p=p):
                    w8(t, S_vec[i], t_xtr[6][1], t_xtr[7][1])
                    _m = t.transpose(PT1[0:1, 0:cnt], svcb[p][0:cnt, 0:1],
                                     idb[0:cnt, 0:cnt])
                    t_sv.append(fin(_m, "pe"))
                blk.tensor(_)
                pv_sv_pe[p] = t_sv[0]

                def _(v, cnt=cnt, p=p):
                    w8(v, t_sv[0], pv_w1_pe[p])
                    t_sv.append(fin(v.tensor_copy(svrb[p][0:1, 0:cnt],
                                                  PT1[0:1, 0:cnt]), "dve"))
                VEC(_)
                prev_tr = prev_tr[1:] + [t_sv[1]]
                X_sv[i] = t_sv[1]

            def _mk_w1(i, off):
                p = i % 2
                co = 32 * p
                t0, cnt = BTILES[i]

                def _pe(t, j, ps, cnt=cnt, i=i, p=p, off=off):
                    fs = off + j
                    if j == 0 and off == 0:
                        w8(t, X_tr[i][7][1], X_sv[i], t_w1l[0],
                           tok_const["b1"])
                    for dc in range(8):
                        t.matmul(ps[0:cnt, 0:512], xqfmb[p][:, dc, 0:cnt],
                                 w1q[:, dc, fs * 512:(fs + 1) * 512],
                                 start=(dc == 0), stop=False)
                    return t.matmul(ps[0:cnt, 0:512], svrb[p][0:1, 0:cnt],
                                    b1r[0:1, fs * 512:(fs + 1) * 512],
                                    start=False, stop=True)

                def _ev(sc, j, ps, cnt=cnt, p=p, co=co, off=off):
                    fs = off + j
                    return sc.activation(
                        htmb[p][0:cnt, fs * 512:(fs + 1) * 512],
                        ps[0:cnt, 0:512], AF.Gelu,
                        scale=st[0:cnt, co + 26:co + 27])
                return _pe, _ev

            def emit_W1(i):
                p = i % 2
                pe, ev = _mk_w1(i, 0)
                t_h = pingpong(4, [PSA, PSB, PSC, PSD], pe, ev, "act",
                               [S_vec[i]] + list(pv_hfree[p]),
                               list(pv_w1_tail))
                W_ev[i] = [t_h[j][1] for j in range(4)]

            def emit_W2(i):
                p = i % 2
                pe, ev = _mk_w1(i, 4)
                t_h = pingpong(4, [PSA, PSB, PSC, PSD], pe, ev, "act",
                               [], list(W_ev[i]))
                W_ev[i] = W_ev[i] + [t_h[j][1] for j in range(4)]
                pv_w1_tail[:] = [t_h[j][1] for j in range(4)]
                pv_w1_pe[p] = t_h[3][0]

            def emit_H(i):
                p = i % 2
                co = 32 * p
                t0, cnt = BTILES[i]
                t_hst = []

                def _(sc, cnt=cnt, i=i, p=p, co=co):
                    w8(sc, *W_ev[i])
                    w8(sc, t_tapyh[0])
                    for fs in range(8):
                        sc.activation(xnrb[p][0:cnt, 0:512],
                                      htmb[p][0:cnt, fs * 512:(fs + 1) * 512],
                                      AF.Square,
                                      accum_out=hscb[p][0:cnt, fs:fs + 1])
                    t_hst.append(fin(sc.activation(
                        xnrb[p][0:cnt, 0:1], hscb[p][0:cnt, 0:1],
                        AF.Identity), "act"))
                SCL(_)

                def _(v, cnt=cnt, p=p, co=co):
                    w8(v, t_hst[0])
                    v.reduce_sum(st[0:cnt, co + 1:co + 2],
                                 hscb[p][0:cnt, 0:8], axis=AX)
                    v.reduce_max(st[0:cnt, co + 20:co + 21],
                                 htmb[p][0:cnt, :], axis=AX,
                                 apply_absolute_value=True)
                    t_hst.append(fin(v.tensor_scalar_mul(
                        st[0:cnt, co + 4:co + 5], st[0:cnt, co + 1:co + 2],
                        1.0 / FF), "dve"))
                VEC(_)

                def _(sc, cnt=cnt, co=co):
                    w8(sc, t_hst[1])
                    sc.activation(st[0:cnt, co + 5:co + 6],
                                  st[0:cnt, co + 4:co + 5], AF.Ln,
                                  bias=cst[0:cnt, 3:4])
                    t_hst.append(fin(sc.activation(
                        st[0:cnt, co + 6:co + 7], st[0:cnt, co + 5:co + 6],
                        AF.Exp, scale=-0.5), "act"))
                SCL(_)

                def _(v, cnt=cnt, i=i, p=p, co=co):
                    w8(v, t_hst[2])
                    v.tensor_mul(st[0:cnt, co + 9:co + 10],
                                 st[0:cnt, co + 6:co + 7],
                                 st[0:cnt, co + 20:co + 21])
                    v.tensor_scalar_max(absh[0:cnt, i:i + 1],
                                        st[0:cnt, co + 9:co + 10], 1e-5)
                    v.reciprocal(st[0:cnt, co + 21:co + 22],
                                 absh[0:cnt, i:i + 1])
                    v.tensor_mul(st[0:cnt, co + 22:co + 23],
                                 absh[0:cnt, i:i + 1],
                                 st[0:cnt, co + 21:co + 22])
                    v.tensor_scalar(st[0:cnt, co + 22:co + 23],
                                    st[0:cnt, co + 22:co + 23], -1.0, 2.0,
                                    ALU.mult, ALU.add)
                    v.tensor_mul(st[0:cnt, co + 21:co + 22],
                                 st[0:cnt, co + 21:co + 22],
                                 st[0:cnt, co + 22:co + 23])
                    v.tensor_scalar_mul(st[0:cnt, co + 23:co + 24],
                                        st[0:cnt, co + 21:co + 22], 127.0)
                    t_hst.append(fin(v.tensor_mul(
                        st[0:cnt, co + 27:co + 28], st[0:cnt, co + 6:co + 7],
                        st[0:cnt, co + 23:co + 24]), "dve"))
                VEC(_)

                t_yh = []

                def _(sc, cnt=cnt, p=p, co=co):
                    w8(sc, t_hst[3], pv_htr_pe[p], t_tapyh[0], pv_rnd_h[0])
                    t_yh.append(fin(sc.activation(
                        yhfs[0:cnt, :], htmb[p][0:cnt, :], AF.Identity,
                        scale=st[0:cnt, co + 27:co + 28]), "act"))
                SCL(_)
                if DEBUG_TAPS:
                    def _(g, i=i, cnt=cnt, t_yh=t_yh):
                        w8(g, t_yh[0])
                        t_tapyh[0] = fin(g.dma_start(
                            bass.AP(d_tyh.tensor, i * 128 * FF,
                                    [[FF, cnt], [1, FF]]),
                            yhfs[0:cnt, :]), "gp")
                    GPS(_)
                t_hq_last = emit_round(yhfs[0:cnt, :], q0hs[0:cnt, :],
                                       hqtmb[p][0:cnt, :], [t_yh[0]])
                pv_rnd_h[0] = t_hq_last
                pv_hfree[p] = [t_hst[1], t_hq_last]
                H_tq[i] = t_hq_last

            def emit_T(i):
                nonlocal prev_tr
                p = i % 2
                t0, cnt = BTILES[i]

                def _pe_ht(t, j, pt, cnt=cnt, i=i, p=p):
                    if j == 0:
                        w8(t, H_tq[i])
                    return t.transpose(pt[0:128, 0:cnt],
                                       hqtmb[p][0:cnt, j * 128:(j + 1) * 128],
                                       idb[0:cnt, 0:cnt])

                def _ev_ht(v, j, pt, cnt=cnt, p=p):
                    return v.tensor_copy(hqfmb[p][:, j, 0:cnt],
                                         pt[0:128, 0:cnt])

                t_htr = pingpong(32, PTS, _pe_ht, _ev_ht, "dve",
                                 [pv_hq_store[p]], list(prev_tr))
                prev_tr = [t_htr[j][1] for j in range(28, 32)]
                pv_htr_pe[p] = t_htr[31][0]

                def _(g, i=i, p=p):
                    w8(g, t_htr[31][1])
                    t_store.append(fin(g.dma_start(
                        bass.AP(d_hq.tensor, i * 128 * FF,
                                [[FF, 128], [1, FF]]),
                        hqfmb[p][:].rearrange("p a b -> p (a b)")), "gp"))
                    pv_hq_store[p] = t_store[-1]
                GPS(_)

            emit_S(0)
            for i in range(NT):
                emit_X(i)
                emit_W1(i)
                if i + 1 < NT:
                    emit_S(i + 1)
                emit_W2(i)
                if i >= 1:
                    emit_T(i - 1)
                emit_H(i)
            emit_T(NT - 1)

            tok_B1 = now()

        with contextlib.ExitStack() as esB2:
            w2q = esB2.enter_context(nc.sbuf_tensor("w2q_s", [128, 32, HS], BF16))
            hqb, xprb, x2tb, ln2b, outtb, sv2rb, sv2cb = \
                [], [], [], [], [], [], []
            for p in range(2):
                hqb.append(esB2.enter_context(
                    nc.sbuf_tensor("hq%d_s" % p, [128, 32, 128], BF16)))
                xprb.append(esB2.enter_context(
                    nc.sbuf_tensor("xpr%d_s" % p, [128, HS], F32)))
                x2tb.append(esB2.enter_context(
                    nc.sbuf_tensor("x2t%d_s" % p, [128, HS], F32)))
                ln2b.append(esB2.enter_context(
                    nc.sbuf_tensor("ln2%d_s" % p, [128, HS], F32)))
                outtb.append(esB2.enter_context(
                    nc.sbuf_tensor("outt%d_s" % p, [128, HS], F32)))
                sv2rb.append(esB2.enter_context(
                    nc.sbuf_tensor("sv2r%d_s" % p, [1, 128], BF16)))
                sv2cb.append(esB2.enter_context(
                    nc.sbuf_tensor("sv2c%d_s" % p, [128, 1], BF16)))

            t_w2l = []

            def _(g):
                w8(g, *w_done[1])
                w8(g, *tok_B1)
                t_w2l.append(fin(g.dma_start(
                    bass.AP(w2q, 0, [[32 * HS, 128], [HS, 32], [1, HS]]),
                    bass.AP(d_wq2.tensor, 0,
                            [[HS, 128], [128 * HS, 32], [1, HS]])), "gp"))
            GPS(_)

            t_hql = [None] * NT
            t_xpl2 = [None] * NT
            t_out = []
            prev_tr2 = []
            pv_sv2_pe = [None, None]
            pv_x2_free = [[], []]
            pv_out_dve = [None, None]
            pv_w2_tail = []
            pv_w2_pe = [None, None]
            pv_out_store = [None, None]

            def _(g):
                w8(g, *tok_B1)
                for c2 in range(2):
                    t0, cnt0 = BTILES[c2]
                    t_hql[c2] = fin(g.dma_start(
                        hqb[c2][:].rearrange("p a b -> p (a b)"),
                        bass.AP(d_hq.tensor, c2 * 128 * FF,
                                [[FF, 128], [1, FF]])), "gp")
                    t_xpl2[c2] = fin(g.dma_start(
                        xprb[c2][0:cnt0, :],
                        bass.AP(d_xp.tensor, t0 * HS, [[HS, cnt0], [1, HS]])),
                        "gp")
            GPS(_)

            S2_tok = [None] * NT

            def emit_P2(k):
                nonlocal prev_tr2
                p = k % 2
                co = 32 * p
                t0, cnt = BTILES[k]
                t_s2 = []

                def _(v, cnt=cnt, k=k, p=p, co=co):
                    w8(v, *tok_B1)
                    w8(v, pv_sv2_pe[p], pv_out_dve[p])
                    v.reciprocal(st[0:cnt, co + 21:co + 22],
                                 absh[0:cnt, k:k + 1])
                    v.tensor_mul(st[0:cnt, co + 22:co + 23],
                                 absh[0:cnt, k:k + 1],
                                 st[0:cnt, co + 21:co + 22])
                    v.tensor_scalar(st[0:cnt, co + 22:co + 23],
                                    st[0:cnt, co + 22:co + 23], -1.0, 2.0,
                                    ALU.mult, ALU.add)
                    v.tensor_mul(st[0:cnt, co + 21:co + 22],
                                 st[0:cnt, co + 21:co + 22],
                                 st[0:cnt, co + 22:co + 23])
                    v.reciprocal(st[0:cnt, co + 24:co + 25], mu2b[0:cnt, 0:1])
                    v.tensor_mul(st[0:cnt, co + 25:co + 26],
                                 st[0:cnt, co + 21:co + 22],
                                 st[0:cnt, co + 24:co + 25])
                    v.tensor_scalar_mul(st[0:cnt, co + 25:co + 26],
                                        st[0:cnt, co + 25:co + 26], 254.0)
                    v.tensor_copy(sv2cb[p][0:cnt, 0:1],
                                  st[0:cnt, co + 25:co + 26])
                    v.tensor_mul(st[0:cnt, co + 26:co + 27],
                                 absh[0:cnt, k:k + 1], mu2b[0:cnt, 0:1])
                    t_s2.append(fin(v.tensor_scalar_mul(
                        st[0:cnt, co + 26:co + 27],
                        st[0:cnt, co + 26:co + 27], 1.0 / 254.0), "dve"))
                VEC(_)

                def _(t, cnt=cnt, p=p):
                    w8(t, t_s2[0], *prev_tr2)
                    _m = t.transpose(PT1[0:1, 0:cnt], sv2cb[p][0:cnt, 0:1],
                                     idb[0:cnt, 0:cnt])
                    t_s2.append(fin(_m, "pe"))
                blk.tensor(_)
                pv_sv2_pe[p] = t_s2[1]

                def _(v, cnt=cnt, p=p):
                    w8(v, t_s2[1], pv_w2_pe[p])
                    t_s2.append(fin(v.tensor_copy(sv2rb[p][0:1, 0:cnt],
                                                  PT1[0:1, 0:cnt]), "dve"))
                VEC(_)
                prev_tr2 = [t_s2[2]]
                S2_tok[k] = t_s2

            T_x2 = [None] * NT

            def emit_M2(i):
                p = i % 2
                co = 32 * p
                t0, cnt = BTILES[i]

                def _pe_w2(t, j, ps, cnt=cnt, i=i, p=p):
                    if j == 0:
                        w8(t, t_hql[i], t_w2l[0], S2_tok[i][2],
                           tok_const["b2"])
                    for fc in range(32):
                        t.matmul(ps[0:cnt, 0:512], hqb[p][:, fc, 0:cnt],
                                 w2q[:, fc, j * 512:(j + 1) * 512],
                                 start=(fc == 0), stop=False)
                    return t.matmul(ps[0:cnt, 0:512], sv2rb[p][0:1, 0:cnt],
                                    b2r[0:1, j * 512:(j + 1) * 512],
                                    start=False, stop=True)

                def _ev_w2(sc, j, ps, cnt=cnt, p=p, co=co):
                    return sc.activation(
                        x2tb[p][0:cnt, j * 512:(j + 1) * 512],
                        ps[0:cnt, 0:512], AF.Identity,
                        scale=st[0:cnt, co + 26:co + 27])

                t_x2 = pingpong(2, [PSA, PSB], _pe_w2, _ev_w2, "act",
                                [S2_tok[i][0]] + list(pv_x2_free[p]),
                                list(pv_w2_tail))
                pv_w2_tail[:] = [t_x2[0][1], t_x2[1][1]]
                pv_w2_pe[p] = t_x2[1][0]
                T_x2[i] = t_x2

            def emit_Q2(i):
                p = i % 2
                co = 32 * p
                t0, cnt = BTILES[i]
                t_x2 = T_x2[i]
                sq_tok, a_tok, d_tok = emit_ln_stats(
                    x2tb[p][0:cnt, :], cnt, ln2b[p],
                    [t_x2[0][1], t_x2[1][1], pv_out_dve[p]], co=co,
                    apply_out=ln2b[p][0:cnt, :])
                t_fin = [a_tok]

                def _(v, cnt=cnt, i=i, p=p):
                    w8(v, t_fin[0], t_xpl2[i], pv_out_store[p])
                    t_fin.append(fin(v.tensor_add(
                        outtb[p][0:cnt, :], ln2b[p][0:cnt, :],
                        xprb[p][0:cnt, :]), "dve"))
                VEC(_)
                pv_x2_free[p] = [sq_tok, d_tok, t_fin[0]]
                pv_out_dve[p] = t_fin[1]

                def _(g, t0=t0, cnt=cnt, p=p):
                    w8(g, t_fin[1])
                    t_out.append(fin(g.dma_start(
                        d_out[t0:t0 + cnt, :], outtb[p][0:cnt, :]), "gp"))
                    pv_out_store[p] = t_out[-1]
                GPS(_)

                if i + 2 < NT:
                    def _(g, i=i, sq_tok=sq_tok, t_fin=t_fin, t_x2=t_x2, p=p):
                        w8(g, t_x2[1][1], sq_tok, t_fin[1])
                        t2, cnt2 = BTILES[i + 2]
                        t_hql[i + 2] = fin(g.dma_start(
                            hqb[p][:].rearrange("p a b -> p (a b)"),
                            bass.AP(d_hq.tensor, (i + 2) * 128 * FF,
                                    [[FF, 128], [1, FF]])), "gp")
                        t_xpl2[i + 2] = fin(g.dma_start(
                            xprb[p][0:cnt2, :],
                            bass.AP(d_xp.tensor, t2 * HS,
                                    [[HS, cnt2], [1, HS]])), "gp")
                    GPS(_)

            emit_P2(0)
            for i in range(NT):
                emit_M2(i)
                if i + 1 < NT:
                    emit_P2(i + 1)
                emit_Q2(i)

            def _(g):
                for _q in ("gp", "hw"):
                    for _i in range(4):
                        _k = _q + str(_i)
                        g.wait_ge(SEMS[_k], C[_k])
            GPS(_)

    return nc


def _host_inputs(L0_attn_x, L0_attn_pos_emb, L0_attn_Wq, L0_attn_Wk, L0_attn_Wv,
                 L0_attn_Wpost, L0_attn_Wrel, L0_attn_per_dim_scale,
                 L0_ffn_w1, L0_ffn_b1, L0_ffn_w2, L0_ffn_b2):
    f = lambda a: np.ascontiguousarray(np.asarray(a), dtype=np.float32)
    x = f(L0_attn_x)
    B, S, _ = x.shape
    shards = []
    for b in range(B):
        for half in range(2):
            s = half * TQ
            sh = np.zeros((SHARD, HS), np.float32)
            lo = max(0, s - HALO)
            hi = min(S, s + SHARD - HALO)
            sh[lo - (s - HALO):hi - (s - HALO)] = x[b, lo:hi]
            shards.append(sh)

    mask = np.zeros((KTW, H, SUBT), np.float32)
    for j in range(SUBB):
        for qq in range(CHUNK):
            mask[12 * j:12 * j + CTX, :, 12 * j + qq] = 1.0
    mask8 = np.ascontiguousarray(mask.reshape(KTW, H * SUBT)).astype(bf)

    cstv = np.zeros((128, 8), np.float32)
    cstv[:, 0] = 0.5
    cstv[:, 1] = -0.5
    cstv[:, 2] = 1.0
    cstv[:, 3] = EPS

    common = dict(
        wq=f(L0_attn_Wq), wk=f(L0_attn_Wk), wv=f(L0_attn_Wv),
        wpost=f(L0_attn_Wpost), wrel=f(L0_attn_Wrel),
        posT=f(np.asarray(L0_attn_pos_emb).T),
        pds=f(L0_attn_per_dim_scale),
        w1T=f(np.asarray(L0_ffn_w1).T), w2T=f(np.asarray(L0_ffn_w2).T),
        b1=f(L0_ffn_b1), b2=f(L0_ffn_b2),
        cst=cstv,
        identb=np.eye(128, dtype=np.float32).astype(bf),
        onescol_b=np.ones((1, 128), np.float32).astype(bf),
        onescol_f=np.ones((1, 128), np.float32),
        ones128_f=np.ones((128, 1), np.float32),
        ones120_b=np.ones((128, 1), np.float32).astype(bf),
        mask8=mask8,
        bdz=np.zeros(2 * SUB_B, np.float32),
    )
    return [dict(common, xsh=shards[c]) for c in range(8)]


def kernel(**inputs):
    in_maps = _host_inputs(**inputs)
    nc = build_nc()
    res = run_bass_kernel_spmd(nc, in_maps, core_ids=list(range(8)))
    x = np.asarray(inputs["L0_attn_x"])
    B, S, _ = x.shape
    out = np.empty((B, S, HS), np.float32)
    for b in range(B):
        for half in range(2):
            s = half * TQ
            own = min(TQ, S - s)
            out[b, s:s + own] = np.asarray(
                res.results[2 * b + half]["out"])[0:own]
    return out


# version 15
# speedup vs baseline: 1.0693x; 1.0632x over previous
import contextlib
import math

import numpy as np
import ml_dtypes

import concourse.bass as bass
import concourse.mybir as mybir
from concourse.bass_utils import run_bass_kernel_spmd

F32 = mybir.dt.float32
BF16 = mybir.dt.bfloat16
I32 = mybir.dt.int32
AF = mybir.ActivationFunctionType
ALU = mybir.AluOpType
AX = mybir.AxisListType.X
bf = ml_dtypes.bfloat16

HS = 1024
H, D = 8, 128
CHUNK, CTX = 12, 24
PP = CTX + 1
SOFTCAP = 50.0
Q_SCALE = D ** (-0.5) / math.log(2)
K_SCALE = math.log(1.0 + math.e) / math.log(2)
QK = Q_SCALE * K_SCALE
EPS = 1e-5
FF = 4096

NBLK = 171
TQ = NBLK * CHUNK
HALO = 12
SHARD = 2176
NCHUNK = 17

SUBB = 9
SUBT = SUBB * CHUNK
NSUB = NBLK // SUBB
KTW = SUBT + 12
SUPER = [(0, 4), (4, 4), (8, 4), (12, 4), (16, 3)]

CSTRIDE = SUBT * PP
BSTRIDE = SUBB * (CHUNK * KTW + CHUNK)
SUB_C = H * CSTRIDE
SUB_B = H * BSTRIDE

BTILES = [(k * 128, min(128, TQ - k * 128)) for k in range((TQ + 127) // 128)]
NT = len(BTILES)
DEBUG_TAPS = False


def build_nc():
    nc = bass.Bass("TRN2", target_bir_lowering=False, debug=False)

    d_x = nc.dram_tensor("xsh", [SHARD, HS], F32, kind="ExternalInput").ap()
    d_wq = nc.dram_tensor("wq", [HS, HS], F32, kind="ExternalInput").ap()
    d_wk = nc.dram_tensor("wk", [HS, HS], F32, kind="ExternalInput").ap()
    d_wv = nc.dram_tensor("wv", [HS, HS], F32, kind="ExternalInput").ap()
    d_wp = nc.dram_tensor("wpost", [HS, HS], F32, kind="ExternalInput").ap()
    d_wr = nc.dram_tensor("wrel", [HS, HS], F32, kind="ExternalInput").ap()
    d_pt = nc.dram_tensor("posT", [HS, PP], F32, kind="ExternalInput").ap()
    d_pds = nc.dram_tensor("pds", [D], F32, kind="ExternalInput").ap()
    d_w1t = nc.dram_tensor("w1T", [HS, FF], F32, kind="ExternalInput").ap()
    d_w2t = nc.dram_tensor("w2T", [FF, HS], F32, kind="ExternalInput").ap()
    d_b1 = nc.dram_tensor("b1", [FF], F32, kind="ExternalInput").ap()
    d_b2 = nc.dram_tensor("b2", [HS], F32, kind="ExternalInput").ap()
    d_cst = nc.dram_tensor("cst", [128, 8], F32, kind="ExternalInput").ap()
    d_idb = nc.dram_tensor("identb", [128, 128], BF16, kind="ExternalInput").ap()
    d_oc_b = nc.dram_tensor("onescol_b", [1, 128], BF16, kind="ExternalInput").ap()
    d_oc_f = nc.dram_tensor("onescol_f", [1, 128], F32, kind="ExternalInput").ap()
    d_o128f = nc.dram_tensor("ones128_f", [128, 1], F32, kind="ExternalInput").ap()
    d_o120b = nc.dram_tensor("ones120_b", [128, 1], BF16, kind="ExternalInput").ap()
    d_mask = nc.dram_tensor("mask8", [KTW, H * SUBT], BF16, kind="ExternalInput").ap()
    d_out = nc.dram_tensor("out", [TQ, HS], F32, kind="ExternalOutput").ap()

    d_xp = nc.dram_tensor("xp_scr", [TQ * HS], BF16, kind="Internal").ap()
    d_bdc = nc.dram_tensor("bdc_scr", [2 * SUB_C], F32, kind="Internal").ap()
    d_bdb = nc.dram_tensor("bdz", [2 * SUB_B], F32, kind="ExternalInput").ap()
    d_wq1 = nc.dram_tensor("wq1_scr", [HS * FF], BF16, kind="Internal").ap()
    d_wq2 = nc.dram_tensor("wq2_scr", [FF * HS], BF16, kind="Internal").ap()
    d_hq = nc.dram_tensor("hq_scr", [NT * 128 * FF], BF16, kind="Internal").ap()
    d_wscr = nc.dram_tensor("wscr_scr", [512], F32, kind="Internal").ap()
    d_txq = (nc.dram_tensor("tapxq_scr", [NT * 128 * HS], BF16,
                            kind="Internal").ap() if DEBUG_TAPS else None)
    d_tyh = (nc.dram_tensor("tapyh_scr", [NT * 128 * FF], F32,
                            kind="Internal").ap() if DEBUG_TAPS else None)

    es = contextlib.ExitStack()
    with es:
        es.enter_context(nc.allow_low_precision(
            reason="bf16 matmul operands by design; fp32 accumulation kept"))
        SEMS = {}
        for _k in ["pe", "dve", "act"]:
            SEMS[_k] = es.enter_context(nc.semaphore("s_" + _k))
        for _q in ["gp", "hw", "ad"]:
            for _i in range(4):
                SEMS[_q + str(_i)] = es.enter_context(
                    nc.semaphore("s_%s%d" % (_q, _i)))
        SEMS["pl"] = es.enter_context(nc.semaphore("s_pl"))
        C = {k: 0 for k in SEMS}
        DMAQ = {"gp": {"idx": 0, "last": None}, "hw": {"idx": 0, "last": None},
                "ad": {"idx": 0, "last": None}}

        class _DmaQ:
            def __init__(self, eng, q):
                self._e = eng
                self._q = q

            def dma_start(self, *a, **kw):
                q = DMAQ[self._q]
                k = self._q + str(q["idx"] % 4)
                q["idx"] += 1
                self._e.wait_ge(SEMS[k], C[k])
                ins = self._e.dma_start(*a, **kw)
                ins.then_inc(SEMS[k], 16)
                C[k] += 16
                q["last"] = (k, C[k])
                return ins

            def __getattr__(self, name):
                return getattr(self._e, name)

        def GPS(f):
            blk.gpsimd(lambda g: f(_DmaQ(g, "gp")))

        def SYN(f):
            blk.sync(lambda y: f(_DmaQ(y, "hw")))

        def SCD(f):
            blk.scalar(lambda s2: f(_DmaQ(s2, "ad")))

        def fin(ins, k):
            if k in ("dve", "act", "pl"):
                return (k, C[k])
            if k in ("gp", "hw", "ad"):
                return DMAQ[k]["last"]
            ins.then_inc(SEMS[k], 1)
            C[k] += 1
            return (k, C[k])

        def w8(eng, *toks):
            best = {}
            for t in toks:
                if t is None:
                    continue
                k, v = t
                if v > 0 and v > best.get(k, 0):
                    best[k] = v
            for k, v in best.items():
                eng.wait_ge(SEMS[k], v)

        def now():
            return [(k, C[k]) for k in C]

        _CHAIN_OPS = {
            "activation", "tensor_copy", "tensor_tensor", "tensor_add",
            "tensor_sub", "tensor_mul", "tensor_max", "tensor_scalar",
            "tensor_single_scalar", "tensor_scalar_add", "tensor_scalar_sub",
            "tensor_scalar_mul", "tensor_scalar_max", "tensor_scalar_min",
            "tensor_relu", "tensor_reduce", "reduce_max", "reduce_sum",
            "scalar_tensor_tensor", "reciprocal", "memset", "memzero",
            "select", "copy_predicated", "sign", "square", "sqrt", "mul",
            "add", "copy",
        }

        class _Chain:
            def __init__(self, eng, key):
                self._e = eng
                self._k = key

            def __getattr__(self, name):
                f = getattr(self._e, name)
                if name in _CHAIN_OPS:
                    k = self._k

                    def g(*a, **kw):
                        self._e.wait_ge(SEMS[k], C[k])
                        ins = f(*a, **kw)
                        ins.then_inc(SEMS[k], 1)
                        C[k] += 1
                        return ins
                    return g
                return f

        def VEC(f):
            blk.vector(lambda v: f(_Chain(v, "dve")))

        def SCL(f):
            blk.scalar(lambda s: f(_Chain(s, "act")))

        def POOL(f):
            blk.gpsimd(lambda g: f(_Chain(_DmaQ(g, "gp"), "pl")))

        cst = es.enter_context(nc.sbuf_tensor("cst_s", [128, 8], F32))
        idb = es.enter_context(nc.sbuf_tensor("idb_s", [128, 128], BF16))
        oc_b = es.enter_context(nc.sbuf_tensor("ocb_s", [1, 128], BF16))
        oc_f = es.enter_context(nc.sbuf_tensor("ocf_s", [1, 128], F32))
        o128f = es.enter_context(nc.sbuf_tensor("o128f_s", [128, 1], F32))
        o120b = es.enter_context(nc.sbuf_tensor("o120b_s", [128, 1], BF16))
        maskb = es.enter_context(nc.sbuf_tensor("mask_s", [KTW, H * SUBT], BF16))
        b1r = es.enter_context(nc.sbuf_tensor("b1r_s", [1, FF], BF16))
        b2r = es.enter_context(nc.sbuf_tensor("b2r_s", [1, HS], BF16))
        qsc = es.enter_context(nc.sbuf_tensor("qsc_s", [128, 1], F32))
        st = es.enter_context(nc.sbuf_tensor("st_s", [128, 64], F32))
        absx = es.enter_context(nc.sbuf_tensor("absx_s", [128, NT], F32))
        absh = es.enter_context(nc.sbuf_tensor("absh_s", [128, NT], F32))
        mu1b = es.enter_context(nc.sbuf_tensor("mu1b_s", [128, 1], F32))
        mu2b = es.enter_context(nc.sbuf_tensor("mu2b_s", [128, 1], F32))

        PSA = es.enter_context(nc.psum_tensor("psA", [128, 512], F32))
        PSB = es.enter_context(nc.psum_tensor("psB", [128, 512], F32))
        PSC = es.enter_context(nc.psum_tensor("psC", [128, 512], F32))
        PSD = es.enter_context(nc.psum_tensor("psD", [128, 512], F32))
        PT1 = es.enter_context(nc.psum_tensor("psT1", [128, 1024], BF16))
        PT2 = es.enter_context(nc.psum_tensor("psT2", [128, 1024], BF16))
        PT3 = es.enter_context(nc.psum_tensor("psT3", [128, 1024], BF16))
        PT4 = es.enter_context(nc.psum_tensor("psT4", [128, 1024], BF16))
        PTS = [PT1, PT2, PT3, PT4]

        blk = es.enter_context(nc.Block())

        AP_HALF = cst[:, 0:1]
        AP_NHALF = cst[:, 1:2]
        AP_ONE = cst[:, 2:3]
        AP_EPS = cst[:, 3:4]

        def pingpong(n, slots, pe_emit, ev_emit, ev_eng, pre_deps, prev_users):
            toks = []
            ns = len(slots)
            for w0 in range(0, n, ns):
                wN = min(n, w0 + ns)

                def _t(t, w0=w0, wN=wN):
                    for i in range(w0, wN):
                        if i < ns:
                            w8(t, *pre_deps)
                            w8(t, *prev_users)
                        else:
                            w8(t, toks[i - ns][1])
                        last = pe_emit(t, i, slots[i % ns])
                        toks.append([fin(last, "pe"), None])
                blk.tensor(_t)

                def _e(e, w0=w0, wN=wN):
                    for i in range(w0, wN):
                        w8(e, toks[i][0])
                        toks[i][1] = fin(ev_emit(e, i, slots[i % ns]), ev_eng)
                (SCL if ev_eng == "act" else VEC)(_e)
            return toks

        tok_const = {}

        def _(g):
            tok_const["cst"] = fin(g.dma_start(cst[:], d_cst), "gp")
            tok_const["idb"] = fin(g.dma_start(idb[:], d_idb), "gp")
            tok_const["oc_b"] = fin(g.dma_start(oc_b[:], d_oc_b), "gp")
            tok_const["oc_f"] = fin(g.dma_start(oc_f[:], d_oc_f), "gp")
            tok_const["o128f"] = fin(g.dma_start(o128f[:], d_o128f), "gp")
            tok_const["o120b"] = fin(g.dma_start(o120b[:], d_o120b), "gp")
            tok_const["mask"] = fin(g.dma_start(maskb[:], d_mask), "gp")
            tok_const["pds"] = fin(g.dma_start(
                qsc[:], bass.AP(d_pds.tensor, 0, [[1, 128], [1, 1]])), "gp")
            tok_const["b1"] = fin(g.dma_start(b1r[:], d_b1.unsqueeze(0)), "gp")
            tok_const["b2"] = fin(g.dma_start(b2r[:], d_b2.unsqueeze(0)), "gp")
        GPS(_)

        def emit_ln_sq(X, cnt, scratch, deps, co=0, c_extra=False,
                       minmax=False):
            W = X.shape[-1]
            ta, td = [], []

            def sl(a, b=None):
                return st[0:cnt, co + a:co + (b if b is not None else a + 1)]

            def _(sc):
                w8(sc, *deps)
                ta.append(fin(sc.activation(scratch[0:cnt, 0:W], X, AF.Square,
                                            accum_out=sl(1)), "act"))
            SCL(_)

            def _(v):
                w8(v, *deps)
                if minmax:
                    v.tensor_reduce(sl(17), X, AX, ALU.max)
                    v.tensor_reduce(sl(18), X, AX, ALU.min)
                v.reduce_sum(sl(0), X, axis=AX)
                v.tensor_scalar_mul(sl(2), sl(0), 1.0 / W)
                v.tensor_mul(sl(3), sl(2), sl(2))
                w8(v, ta[0])
                v.tensor_scalar(sl(4), sl(1), 1.0 / W, None, ALU.mult)
                last = v.tensor_sub(sl(4), sl(4), sl(3))
                if c_extra:
                    v.tensor_scalar_add(sl(9), sl(4), EPS)
                    v.reciprocal(sl(10), sl(9))
                    last = v.tensor_mul(sl(11), sl(4), sl(10))
                td.append(fin(last, "dve"))
            VEC(_)
            return ta[0], td[0]

        def emit_ln_fin(X, cnt, d_tok, co=0, c_extra=False, apply_out=None,
                        deps=()):
            ta = []

            def sl(a, b=None):
                return st[0:cnt, co + a:co + (b if b is not None else a + 1)]

            def _(sc):
                w8(sc, d_tok, *deps)
                sc.activation(sl(5), sl(4), AF.Ln, bias=cst[0:cnt, 3:4])
                sc.activation(sl(6), sl(5), AF.Exp, scale=-0.5)
                sc.activation(sl(7), sl(6), AF.Identity, scale=-1.0)
                last = sc.activation(sl(8), sl(2), AF.Identity, scale=sl(7))
                if c_extra:
                    sc.activation(sl(12), sl(11), AF.Ln, bias=cst[0:cnt, 3:4])
                    sc.activation(sl(13), sl(12), AF.Exp, scale=-0.5)
                    sc.activation(sl(15), sl(6), AF.Identity, scale=sl(13))
                    sc.activation(sl(16), sl(15), AF.Identity, scale=-1.0)
                    last = sc.activation(sl(14), sl(2), AF.Identity,
                                         scale=sl(16))
                if apply_out is not None:
                    if c_extra:
                        last = sc.activation(apply_out, X, AF.Identity,
                                             scale=sl(15), bias=sl(14))
                    else:
                        last = sc.activation(apply_out, X, AF.Identity,
                                             scale=sl(6), bias=sl(8))
                ta.append(fin(last, "act"))
            SCL(_)
            return ta[0]

        def emit_ln_stats(X, cnt, scratch, deps, c_extra=False, co=0,
                          apply_out=None, minmax=False):
            sq_tok, d_tok = emit_ln_sq(X, cnt, scratch, deps, co=co,
                                       c_extra=c_extra, minmax=minmax)
            a_tok = emit_ln_fin(X, cnt, d_tok, co=co, c_extra=c_extra,
                                apply_out=apply_out)
            return sq_tok, a_tok, d_tok

        def emit_round(Y, Q0I, OUT, deps):
            tk = {}

            def _(v):
                w8(v, *deps)
                v.tensor_copy(Q0I, Y)
                tk["q"] = fin(v.tensor_copy(OUT, Q0I), "dve")
            VEC(_)
            return tk["q"]

        with contextlib.ExitStack() as esA:
            wq_s = esA.enter_context(nc.sbuf_tensor("wq_s", [128, 8, HS], BF16))
            wk_s = esA.enter_context(nc.sbuf_tensor("wk_s", [128, 8, HS], BF16))
            wv_s = esA.enter_context(nc.sbuf_tensor("wv_s", [128, 8, HS], BF16))
            wp_s = esA.enter_context(nc.sbuf_tensor("wp_s", [128, 8, HS], BF16))
            rel_s = esA.enter_context(nc.sbuf_tensor("rel_s", [128, 8, PP], BF16))
            tok_bdb_zero = None

            def _(g):
                for nm, dst, srct in [("wq", wq_s, d_wq), ("wk", wk_s, d_wk),
                                      ("wv", wv_s, d_wv), ("wp", wp_s, d_wp)]:
                    tok_const[nm] = fin(g.dma_start(
                        bass.AP(dst, 0, [[8 * HS, 128], [HS, 8], [1, HS]]),
                        bass.AP(srct.tensor, 0,
                                [[HS, 128], [128 * HS, 8], [1, HS]])), "gp")
            GPS(_)

            t_qs = []

            def _(sc):
                w8(sc, tok_const["pds"], tok_const["cst"])
                sc.activation(st[:, 16:17], qsc[:], AF.Abs)
                sc.activation(st[:, 17:18], st[:, 16:17], AF.Exp, scale=-1.0)
                t_qs.append(fin(sc.activation(st[:, 18:19], st[:, 17:18], AF.Ln,
                                              bias=AP_ONE), "act"))
            SCL(_)

            def _(v):
                w8(v, t_qs[0])
                v.tensor_scalar_max(st[:, 19:20], qsc[:], 0.0)
                v.tensor_add(st[:, 20:21], st[:, 19:20], st[:, 18:19])
                t_qs.append(fin(v.tensor_scalar_mul(qsc[:], st[:, 20:21], QK), "dve"))
            VEC(_)


            with contextlib.ExitStack() as esR:
                wr_s = esR.enter_context(
                    nc.sbuf_tensor("wr_s", [128, 8, HS], BF16))
                pt_s = esR.enter_context(
                    nc.sbuf_tensor("pt_s", [128, 8, PP], BF16))

                def _(g):
                    tok_const["wr"] = fin(g.dma_start(
                        bass.AP(wr_s, 0, [[8 * HS, 128], [HS, 8], [1, HS]]),
                        bass.AP(d_wr.tensor, 0,
                                [[HS, 128], [128 * HS, 8], [1, HS]])), "gp")
                    tok_const["pt"] = fin(g.dma_start(
                        bass.AP(pt_s, 0, [[8 * PP, 128], [PP, 8], [1, PP]]),
                        bass.AP(d_pt.tensor, 0,
                                [[PP, 128], [128 * PP, 8], [1, PP]])), "gp")
                GPS(_)

                def _pe_rel(t, i, ps):
                    for dc in range(8):
                        last = t.matmul(ps[0:128, 0:PP],
                                        wr_s[:, dc, i * 128:(i + 1) * 128],
                                        pt_s[:, dc, :], start=(dc == 0),
                                        stop=(dc == 7))
                    return last

                def _ev_rel(sc, i, ps):
                    return sc.activation(rel_s[:, i, :], ps[0:128, 0:PP],
                                         AF.Identity, scale=1.0 / K_SCALE)

                t_rel = pingpong(H, [PSA, PSB, PSC, PSD], _pe_rel, _ev_rel,
                                 "act", [tok_const["wr"], tok_const["pt"]], [])
                tok_rel = t_rel[H - 1][1]

            wcr = [esA.enter_context(
                nc.sbuf_tensor("wcr%d_s" % i, [128, 512], F32))
                for i in range(4)]
            sg1w = [esA.enter_context(
                nc.sbuf_tensor("sg1%d_s" % i, [128, 512], BF16))
                for i in range(2)]
            sg2w = [esA.enter_context(
                nc.sbuf_tensor("sg2%d_s" % i, [128, 512], BF16))
                for i in range(2)]
            wqow = [esA.enter_context(
                nc.sbuf_tensor("wqo%d_s" % i, [128, 512], BF16))
                for i in range(2)]
            asum = esA.enter_context(nc.sbuf_tensor("asum_s", [128, 66], F32))
            thbw = esA.enter_context(nc.sbuf_tensor("thb_s", [128, 8], F32))
            thrw = esA.enter_context(nc.sbuf_tensor("thr_s", [1, 8], F32))
            prow = esA.enter_context(nc.sbuf_tensor("prow_s", [1, 256], F32))

            def _w_src(wi, u):
                if wi == 0:
                    c2, q2 = divmod(u, 8)
                    return bass.AP(d_w1t.tensor, c2 * 128 * FF + q2 * 512,
                                   [[FF, 128], [1, 512]])
                c2, q2 = divmod(u, 2)
                return bass.AP(d_w2t.tensor, c2 * 128 * HS + q2 * 512,
                               [[HS, 128], [1, 512]])

            def _w_dst(wi, u):
                if wi == 0:
                    c2, q2 = divmod(u, 8)
                    return bass.AP(d_wq1.tensor, c2 * 128 * FF + q2 * 512,
                                   [[FF, 128], [1, 512]])
                c2, q2 = divmod(u, 2)
                return bass.AP(d_wq2.tensor, c2 * 128 * HS + q2 * 512,
                               [[HS, 128], [1, 512]])

            WUNITS = ([("scan", 0, u) for u in range(64)] + [("fin", 0, 0)] +
                      [("sign", 0, u) for u in range(64)] +
                      [("scan", 1, u) for u in range(64)] + [("fin", 1, 0)] +
                      [("sign", 1, u) for u in range(64)])
            WLOADS = [un for un in WUNITS if un[0] != "fin"]
            WLIDX = {un: j for j, un in enumerate(WLOADS)}
            w_load_tok, w_comp_tok, w_fin_tok = {}, {}, {}
            w_asum_rd, w_prow_rd = {}, {}
            w_done = {0: [], 1: []}
            w_sg_pool = [None, None]
            w_wqo_st = [None, None]
            w_li = [0]
            w_ci = [0]

            def _w_emit_loads(upto):
                lo, hi = w_li[0], min(upto, len(WLOADS))
                if lo >= hi:
                    return

                def _(sd):
                    for j in range(lo, hi):
                        un = WLOADS[j]
                        if j >= 4:
                            w8(sd, w_comp_tok.get(WLOADS[j - 4]))
                        else:
                            w8(sd, tok_rel)
                        w_load_tok[un] = fin(sd.dma_start(
                            wcr[j % 4][:], _w_src(un[1], un[2])), "ad")
                SCD(_)
                w_li[0] = hi

            def _w_emit_fin(wi):
                base = 4 * wi
                po = 128 * wi
                so = 256 * wi
                toks = []

                def _(v):
                    w8(v, *[w_comp_tok[("scan", wi, u)] for u in range(64)])
                    w8(v, w_asum_rd.get(1 - wi))
                    toks.append(fin(v.tensor_reduce(
                        asum[:, 64:65], asum[:, 0:64], AX, ALU.add), "dve"))
                VEC(_)

                def _(g):
                    w8(g, toks[0])
                    toks.append(fin(g.dma_start(
                        bass.AP(d_wscr.tensor, so, [[1, 128]]),
                        asum[:, 64:65]), "gp"))
                GPS(_)
                w_asum_rd[wi] = toks[1]

                def _(g):
                    w8(g, toks[1], w_prow_rd.get(1 - wi))
                    toks.append(fin(g.dma_start(
                        prow[0:1, po:po + 128],
                        bass.AP(d_wscr.tensor, so, [[1, 128]])), "gp"))
                GPS(_)

                def _(v):
                    w8(v, toks[2])
                    v.tensor_reduce(thrw[0:1, base:base + 1],
                                    prow[0:1, po:po + 128], AX, ALU.add)
                    v.tensor_scalar_mul(thrw[0:1, base:base + 1],
                                        thrw[0:1, base:base + 1],
                                        1.0 / (HS * FF))
                    v.tensor_scalar_max(thrw[0:1, base:base + 1],
                                        thrw[0:1, base:base + 1], 1e-5)
                    v.tensor_scalar_mul(thrw[0:1, base + 1:base + 2],
                                        thrw[0:1, base:base + 1], 0.5)
                    toks.append(fin(v.tensor_scalar_mul(
                        thrw[0:1, base + 2:base + 3],
                        thrw[0:1, base + 1:base + 2], -1.0), "dve"))
                VEC(_)
                w_prow_rd[wi] = toks[3]

                def _(g):
                    w8(g, toks[3])
                    toks.append(fin(g.dma_start(
                        bass.AP(d_wscr.tensor, so + 128, [[1, 3]]),
                        thrw[0:1, base:base + 3]), "gp"))
                GPS(_)

                def _(g):
                    w8(g, toks[4])
                    t_thb = fin(g.dma_start(
                        thbw[:, 3 * wi:3 * wi + 3],
                        bass.AP(d_wscr.tensor, so + 128,
                                [[0, 128], [1, 3]])), "gp")
                    mub = mu1b if wi == 0 else mu2b
                    t_mub = fin(g.dma_start(
                        mub[:, 0:1],
                        bass.AP(d_wscr.tensor, so + 128,
                                [[0, 128], [1, 1]])), "gp")
                    toks.append((t_thb, t_mub))
                GPS(_)
                w_fin_tok[wi] = toks[5]

            def wpump(k):
                if w_ci[0] >= len(WUNITS):
                    return
                batch = []
                while len(batch) < k and w_ci[0] < len(WUNITS):
                    un = WUNITS[w_ci[0]]
                    if un[0] != "fin" and WLIDX[un] >= w_li[0]:
                        break
                    batch.append(un)
                    w_ci[0] += 1
                    if un[0] == "fin":
                        break
                scans = [u2 for u2 in batch if u2[0] == "scan"]
                signs = [u2 for u2 in batch if u2[0] == "sign"]
                fins = [u2 for u2 in batch if u2[0] == "fin"]
                if scans:
                    def _(v, scans=scans):
                        for un in scans:
                            j = WLIDX[un]
                            w8(v, w_load_tok[un])
                            w_comp_tok[un] = fin(v.tensor_reduce(
                                asum[:, un[2]:un[2] + 1], wcr[j % 4][:],
                                AX, ALU.add, apply_absolute_value=True),
                                "dve")
                    VEC(_)
                for pi in range(0, len(signs), 2):
                    pair = signs[pi:pi + 2]

                    def _(sc, pair=pair):
                        for pj, un in enumerate(pair):
                            wi, u = un[1], un[2]
                            j = WLIDX[un]
                            w8(sc, w_load_tok[un], *w_fin_tok[wi])
                            w8(sc, w_sg_pool[pj])
                            sc.activation(sg1w[pj][:], wcr[j % 4][:], AF.Sign,
                                          bias=thbw[:, 3 * wi + 2:3 * wi + 3])
                            w_comp_tok[un] = fin(sc.activation(
                                sg2w[pj][:], wcr[j % 4][:], AF.Sign,
                                bias=thbw[:, 3 * wi + 1:3 * wi + 2]), "act")
                    SCL(_)

                    def _(g, pair=pair):
                        for pj, un in enumerate(pair):
                            wi, u = un[1], un[2]
                            w8(g, w_comp_tok[un], w_wqo_st[pj])
                            t_add = fin(g.tensor_add(
                                wqow[pj][:], sg1w[pj][:], sg2w[pj][:]), "pl")
                            w_sg_pool[pj] = t_add
                            w8(g, t_add)
                            tst = fin(g.dma_start(_w_dst(wi, u),
                                                  wqow[pj][:]), "gp")
                            w_wqo_st[pj] = tst
                            w_done[wi].append(tst)
                    POOL(_)
                for un in fins:
                    _w_emit_fin(un[1])
                ncl = sum(1 for x2 in WUNITS[:w_ci[0]] if x2[0] != "fin")
                _w_emit_loads(ncl + 4)

            _w_emit_loads(4)


            xn = esA.enter_context(nc.sbuf_tensor("xn_s", [128, 8, SHARD], BF16))
            xt0 = esA.enter_context(nc.sbuf_tensor("xt0_s", [128, HS], F32))
            xt1 = esA.enter_context(nc.sbuf_tensor("xt1_s", [128, HS], F32))
            xt2 = esA.enter_context(nc.sbuf_tensor("xt2_s", [128, HS], F32))
            xnt = esA.enter_context(nc.sbuf_tensor("xnt_s", [128, HS], BF16))
            q_st = esA.enter_context(nc.sbuf_tensor("q_st", [128, 8, 432], BF16))
            k_st = esA.enter_context(nc.sbuf_tensor("k_st", [128, 8, 444], BF16))
            a_st = esA.enter_context(nc.sbuf_tensor("a_st", [128, 8, 432], BF16))
            v_s0 = esA.enter_context(nc.sbuf_tensor("v_s0", [128, HS], BF16))
            v_s1 = esA.enter_context(nc.sbuf_tensor("v_s1", [128, HS], BF16))
            bdst2 = [esA.enter_context(
                nc.sbuf_tensor("bdst%d_s" % _i, [128, H, PP], F32))
                for _i in range(2)]
            scst = esA.enter_context(nc.sbuf_tensor("scst_s", [128, H, KTW], F32))
            bdbs2 = [esA.enter_context(
                nc.sbuf_tensor("bdbs%d_s" % _i, [128, H, KTW], F32))
                for _i in range(2)]
            s1st = esA.enter_context(nc.sbuf_tensor("s1st_s", [128, H, KTW], F32))
            ths = esA.enter_context(nc.sbuf_tensor("ths_s", [128, H, KTW], BF16))
            emst = esA.enter_context(nc.sbuf_tensor("emst_s", [KTW, H, SUBT], BF16))
            rcr = esA.enter_context(nc.sbuf_tensor("rcr_s", [1, H * SUBT], BF16))
            rbst = esA.enter_context(nc.sbuf_tensor("rbst_s", [128, H * SUBT], F32))
            y_fm = esA.enter_context(nc.sbuf_tensor("yfm_s", [128, 8, 432], BF16))
            y_tm = esA.enter_context(nc.sbuf_tensor("ytm_s", [128, HS], BF16))
            lny = esA.enter_context(nc.sbuf_tensor("lny_s", [128, HS], F32))
            xpt = esA.enter_context(nc.sbuf_tensor("xpt_s", [128, HS], BF16))

            xtiles = [xt0, xt1, xt2]
            t_xld = [None] * NCHUNK
            prev_tr = []
            t_a1_last = [None] * NCHUNK

            def _(g):
                for c2 in range(3):
                    t_xld[c2] = fin(g.dma_start(
                        xtiles[c2][:], d_x[c2 * 128:(c2 + 1) * 128, :]), "gp")
            GPS(_)

            prev_xnt_pe = None
            A1sq = [None] * NCHUNK
            A1sq[0] = emit_ln_sq(xtiles[0][:], 128, xnt,
                                 [t_xld[0], tok_const["cst"]], co=0)
            for c in range(NCHUNK):
                wpump(4)
                X = xtiles[c % 3]
                co = 32 * (c % 2)
                if c + 1 < NCHUNK:
                    A1sq[c + 1] = emit_ln_sq(
                        xtiles[(c + 1) % 3][:], 128, xnt,
                        [t_xld[c + 1], prev_xnt_pe], co=32 * ((c + 1) % 2))
                sq_tok, d_tok = A1sq[c]
                a_tok = emit_ln_fin(X[:], 128, d_tok, co=co,
                                    apply_out=xnt[:], deps=(prev_xnt_pe,))
                t_ap = [a_tok]

                def _pe_tr(t, i, pt, c=c, t_ap=t_ap):
                    if i == 0:
                        w8(t, t_ap[0], tok_const["idb"])
                    return t.transpose(pt[0:128, 0:128],
                                       xnt[:, i * 128:(i + 1) * 128], idb[:])

                def _ev_tr(v, i, pt, c=c):
                    return v.tensor_copy(xn[:, i, c * 128:(c + 1) * 128],
                                         pt[0:128, 0:128])

                t_tr = pingpong(8, PTS, _pe_tr, _ev_tr, "dve",
                                [], list(prev_tr))
                prev_tr = [t_tr[4][1], t_tr[5][1], t_tr[6][1], t_tr[7][1]]
                t_a1_last[c] = t_tr[7][1]
                prev_xnt_pe = t_tr[7][0]

                if c + 3 < NCHUNK:
                    def _(g, c=c, sq_tok=sq_tok, d_tok=d_tok, t_ap=t_ap):
                        w8(g, t_ap[0], sq_tok, d_tok)
                        t_xld[c + 3] = fin(g.dma_start(
                            xtiles[c % 3][:],
                            d_x[(c + 3) * 128:(c + 4) * 128, :]), "gp")
                    GPS(_)

            tok_xn = now()

            last_super = now()
            prev_ep = list(tok_xn)
            prev_pv_evac = []
            prev_rb_acts = []
            prev_v_evac = []
            t_pv = []
            cd_users = []
            t_bdw2 = [None, None]
            t_shr = [None, None]
            t_rdb2 = [None, None]
            pv_s1 = [None, None]
            pv_tanh = [None]
            t_bd_h = [None, None]
            pv_qk_pe = [None]
            pv_wp_ev = []
            for si, (sub0, nsub) in enumerate(SUPER):
                S = sub0 * SUBT
                T = nsub * SUBT
                sh = S + HALO

                def _pe_q(t, i, ps, S=S, T=T, sh=sh):
                    if i == 0:
                        w8(t, *tok_xn)
                        w8(t, tok_const["wq"], tok_const["wk"], t_qs[-1],
                           pv_qk_pe[0])
                    wgt, width, off = (wq_s, T, sh) if i < 8 else \
                        (wk_s, T + 12, sh - 12)
                    hc = i % 8
                    for dc in range(8):
                        last = t.matmul(ps[0:128, 0:width],
                                        wgt[:, dc, hc * 128:(hc + 1) * 128],
                                        xn[:, dc, off:off + width],
                                        start=(dc == 0), stop=(dc == 7))
                    return last

                def _ev_q(sc, i, ps, T=T):
                    hc = i % 8
                    if i < 8:
                        return sc.activation(q_st[:, hc, 0:T], ps[0:128, 0:T],
                                             AF.Identity, scale=qsc[:, 0:1])
                    return sc.activation(k_st[:, hc, 0:T + 12],
                                         ps[0:128, 0:T + 12], AF.Identity)

                t_proj = pingpong(16, [PSA, PSB, PSC, PSD], _pe_q, _ev_q, "act",
                                  [], list(prev_rb_acts) + list(prev_pv_evac)
                                  + list(pv_wp_ev))

                cd_users[:] = [t_proj[14][1], t_proj[15][1]]
                prev_pv_evac = []

                def emit_bd(su):
                    p = su % 2
                    s0 = su * SUBT

                    def _pe_bd(t, i, ps, s0=s0, p=p):
                        if i == 0:
                            w8(t, t_proj[7][1], tok_rel, t_bdw2[p])
                        return t.matmul(ps[0:SUBT, 0:PP],
                                        q_st[:, i, s0:s0 + SUBT],
                                        rel_s[:, i, :], start=True, stop=True)

                    def _ev_bd(v, i, ps, p=p):
                        return v.tensor_copy(bdst2[p][0:SUBT, i, :],
                                             ps[0:SUBT, 0:PP])

                    t_bd = pingpong(H, [PSC, PSD], _pe_bd, _ev_bd, "dve",
                                    [], list(cd_users))
                    cd_users[:] = [t_bd[H - 2][1], t_bd[H - 1][1]]
                    t_bd_h[p] = t_bd

                    def _(y, p=p):
                        w8(y, t_bd[H - 1][1], t_shr[p])
                        t_bdw2[p] = fin(y.dma_start(
                            bass.AP(d_bdc.tensor, p * SUB_C,
                                    [[PP, SUBT], [CSTRIDE, H], [1, PP]]),
                            bass.AP(bdst2[p], 0,
                                    [[H * PP, SUBT], [PP, H], [1, PP]])), "hw")
                        w8(y, t_bdw2[p], t_rdb2[p])
                        t_shr[p] = fin(y.dma_start(
                            bass.AP(d_bdb.tensor, p * SUB_B,
                                    [[CHUNK * KTW + CHUNK, H * SUBB],
                                     [KTW, CHUNK], [1, CTX]]),
                            bass.AP(d_bdc.tensor, p * SUB_C,
                                    [[CHUNK * PP, H * SUBB],
                                     [CTX, CHUNK], [1, CTX]])), "hw")
                        w8(y, t_shr[p], pv_s1[p])
                        t_rdb2[p] = fin(y.dma_start(
                            bass.AP(bdbs2[p], 0,
                                    [[H * KTW, SUBT], [KTW, H], [1, KTW]]),
                            bass.AP(d_bdb.tensor, p * SUB_B,
                                    [[KTW, SUBT], [BSTRIDE, H], [1, KTW]])),
                            "hw")
                    SYN(_)

                emit_bd(0)
                for su in range(nsub):
                    wpump(4)
                    p_su = su % 2
                    s0 = su * SUBT
                    w0 = sh + s0 - 12
                    if su + 1 < nsub:
                        emit_bd(su + 1)
                    t_bd = t_bd_h[p_su]

                    vbuf = v_s0 if su % 2 == 0 else v_s1

                    def _pe_v(t, i, ps, w0=w0):
                        if i == 0:
                            w8(t, tok_const["wv"], t_proj[12][1], t_proj[13][1])
                            w8(t, *prev_v_evac)
                        for dc in range(8):
                            last = t.matmul(ps[0:KTW, 0:512],
                                            xn[:, dc, w0:w0 + KTW],
                                            wv_s[:, dc, i * 512:(i + 1) * 512],
                                            start=(dc == 0), stop=(dc == 7))
                        return last

                    def _ev_v(v, i, ps, vbuf=vbuf):
                        return v.tensor_copy(
                            vbuf[0:KTW, i * 512:(i + 1) * 512], ps[0:KTW, 0:512])

                    t_vp = pingpong(2, [PSA, PSB], _pe_v, _ev_v, "dve",
                                    [], [t_proj[12][1], t_proj[13][1]]
                                    + list(prev_rb_acts))
                    prev_v_evac = [t_vp[0][1], t_vp[1][1]]

                    def _pe_s(t, i, ps, s0=s0, t_bd=t_bd):
                        if i < 2:
                            w8(t, t_bd[H - 2 + i][1])
                        return t.matmul(ps[0:SUBT, 0:KTW],
                                        q_st[:, i, s0:s0 + SUBT],
                                        k_st[:, i, s0:s0 + KTW],
                                        start=True, stop=True)

                    def _ev_s(v, i, ps):
                        return v.tensor_copy(scst[0:SUBT, i, :],
                                             ps[0:SUBT, 0:KTW])

                    t_sc = pingpong(H, [PSC, PSD], _pe_s, _ev_s, "dve",
                                    [], list(cd_users))
                    cd_users[:] = [t_sc[H - 2][1], t_sc[H - 1][1]]
                    pv_qk_pe[0] = t_sc[H - 1][0]

                    t_s1 = []

                    def _(v, p_su=p_su):
                        w8(v, t_sc[H - 1][1], t_rdb2[p_su], pv_tanh[0])
                        t_s1.append(fin(v.tensor_add(
                            s1st[0:SUBT, :, :].rearrange("p a b -> p (a b)"),
                            scst[0:SUBT, :, :].rearrange("p a b -> p (a b)"),
                            bdbs2[p_su][0:SUBT, :, :].rearrange(
                                "p a b -> p (a b)")), "dve"))
                    VEC(_)
                    pv_s1[p_su] = t_s1[0]

                    t_tanh = []

                    def _(sc):
                        w8(sc, t_s1[0])
                        t_tanh.append(fin(sc.activation(
                            ths[0:SUBT, :, :].rearrange("p a b -> p (a b)"),
                            s1st[0:SUBT, :, :].rearrange("p a b -> p (a b)"),
                            AF.Tanh, scale=1.0 / SOFTCAP), "act"))
                    SCL(_)
                    pv_tanh[0] = t_tanh[0]

                    def _pe_te(t, i, pt):
                        if i == 0:
                            w8(t, t_tanh[0])
                        return t.transpose(pt[0:KTW, 0:SUBT],
                                           ths[0:SUBT, i, :],
                                           idb[0:SUBT, 0:SUBT])

                    def _ev_te(sc, i, pt):
                        return sc.activation(emst[:, i, :], pt[0:KTW, 0:SUBT],
                                             AF.Exp, scale=SOFTCAP)

                    t_te = pingpong(H, PTS, _pe_te, _ev_te, "act",
                                    [], [t_tanh[0]])

                    t_mask = []

                    def _(v):
                        w8(v, t_te[H - 1][1], tok_const["mask"])
                        t_mask.append(fin(v.tensor_mul(
                            emst[:].rearrange("p a b -> p (a b)"),
                            emst[:].rearrange("p a b -> p (a b)"),
                            maskb[:]), "dve"))
                    VEC(_)

                    t_den = []

                    def _(t):
                        w8(t, t_mask[0], tok_const["o120b"])
                        w8(t, *cd_users)
                        t.matmul(PSC[0:1, 0:432], o120b[0:KTW, 0:1],
                                 emst[:].rearrange("p a b -> p (a b)")[:, 0:432],
                                 start=True, stop=True)
                        _m2 = t.matmul(
                            PSD[0:1, 0:432], o120b[0:KTW, 0:1],
                            emst[:].rearrange("p a b -> p (a b)")[:, 432:864],
                            start=True, stop=True)
                        t_den.append(fin(_m2, "pe"))
                    blk.tensor(_)

                    def _(v):
                        w8(v, t_den[0])
                        v.reciprocal(rcr[0:1, 0:432], PSC[0:1, 0:432])
                        t_den.append(fin(v.reciprocal(rcr[0:1, 432:864],
                                                      PSD[0:1, 0:432]), "dve"))
                    VEC(_)
                    cd_users[:] = [t_den[1]]

                    def _pe_rb(t, i, ps):
                        if i == 0:
                            w8(t, t_den[1], tok_const["oc_b"])
                        return t.matmul(ps[0:128, 0:432], oc_b[:],
                                        rcr[0:1, i * 432:(i + 1) * 432],
                                        start=True, stop=True)

                    def _ev_rb(v, i, ps):
                        return v.tensor_copy(rbst[:, i * 432:(i + 1) * 432],
                                             ps[0:128, 0:432])

                    wpump(4)
                    t_rb = pingpong(2, [PSA, PSB], _pe_rb, _ev_rb, "dve",
                                    [], [t_vp[0][1], t_vp[1][1]])
                    prev_rb_acts = [t_rb[0][1], t_rb[1][1]]

                    def _pe_pv(t, i, ps, vbuf=vbuf):
                        if i == 0:
                            w8(t, t_rb[1][1])
                        return t.matmul(ps[0:128, 0:SUBT],
                                        vbuf[0:KTW, i * 128:(i + 1) * 128],
                                        emst[:, i, :], start=True, stop=True)

                    def _ev_pv(v, i, ps, s0=s0):
                        return v.tensor_mul(
                            a_st[:, i, s0:s0 + SUBT], ps[0:128, 0:SUBT],
                            rbst[:, i * SUBT:(i + 1) * SUBT])

                    t_pv = pingpong(H, [PSC, PSD], _pe_pv, _ev_pv, "dve",
                                    [], list(cd_users))
                    cd_users[:] = [t_pv[H - 2][1], t_pv[H - 1][1]]
                    prev_pv_evac = [t_pv[H - 2][1], t_pv[H - 1][1]]

                def _pe_wp(t, i, ps, T=T):
                    if i == 0:
                        w8(t, t_pv[H - 1][1], tok_const["wp"])
                        w8(t, *prev_rb_acts)
                    for dci in range(8):
                        last = t.matmul(ps[0:128, 0:T],
                                        wp_s[:, dci, i * 128:(i + 1) * 128],
                                        a_st[:, dci, 0:T], start=(dci == 0),
                                        stop=(dci == 7))
                    return last

                def _ev_wp(v, i, ps, T=T):
                    return v.tensor_copy(y_fm[:, i, 0:T], ps[0:128, 0:T])

                t_wp = pingpong(8, [PSA, PSB], _pe_wp, _ev_wp, "dve",
                                [], list(prev_rb_acts))
                prev_rb_acts = []
                pv_wp_ev[:] = [t_wp[6][1], t_wp[7][1]]

                a = 0
                while a < T:
                    wpump(4)
                    cnt = min(128, T - a)

                    def _pe_yt(t, i, pt, a=a, cnt=cnt, prev_ep=list(prev_ep)):
                        if i == 0:
                            w8(t, t_wp[7][1], *prev_ep)
                        return t.transpose(pt[0:cnt, 0:128],
                                           y_fm[:, i, a:a + cnt], idb[:])

                    def _ev_yt(v, i, pt, cnt=cnt):
                        return v.tensor_copy(
                            y_tm[0:cnt, i * 128:(i + 1) * 128], pt[0:cnt, 0:128])

                    t_yt = pingpong(8, PTS, _pe_yt, _ev_yt, "dve",
                                    [], list(prev_tr))
                    prev_tr = [t_yt[4][1], t_yt[5][1], t_yt[6][1], t_yt[7][1]]

                    t_xr = []

                    def _(g, S=S, a=a, cnt=cnt, prev_ep=list(prev_ep)):
                        w8(g, *prev_ep)
                        t_xr.append(fin(g.dma_start(
                            xt0[0:cnt, :],
                            d_x[HALO + S + a:HALO + S + a + cnt, :]), "gp"))
                    GPS(_)

                    sq_tok, a_tok, d_tok = emit_ln_stats(
                        y_tm[0:cnt, :], cnt, xnt, [t_yt[7][1]],
                        apply_out=lny[0:cnt, :])
                    t_ep = [a_tok]

                    def _(v, cnt=cnt):
                        w8(v, t_ep[0], t_xr[0])
                        t_ep.append(fin(v.tensor_add(
                            xpt[0:cnt, :], lny[0:cnt, :], xt0[0:cnt, :]), "dve"))
                    VEC(_)

                    def _(g, S=S, a=a, cnt=cnt):
                        w8(g, t_ep[1])
                        t_ep.append(fin(g.dma_start(
                            bass.AP(d_xp.tensor, (S + a) * HS,
                                    [[HS, cnt], [1, HS]]), xpt[0:cnt, :]), "gp"))
                    GPS(_)
                    prev_ep = [t_ep[1], t_ep[2], sq_tok]
                    a += cnt

                last_super = now()

            while w_ci[0] < len(WUNITS):
                wpump(6)
            tok_A = now()

        tok_W = now()

        with contextlib.ExitStack() as esB:
            w1q = esB.enter_context(nc.sbuf_tensor("w1q_s", [128, 8, FF], BF16))
            xpb, xnrb, hscb, xqtmb, xqfmb, svrb, svcb = \
                [], [], [], [], [], [], []
            htmb, hqtmb, hqfmb = [], [], []
            for p in range(2):
                xpb.append(esB.enter_context(
                    nc.sbuf_tensor("xp%d_s" % p, [128, HS], F32)))
                xnrb.append(esB.enter_context(
                    nc.sbuf_tensor("xnr%d_s" % p, [128, HS], F32)))
                hscb.append(esB.enter_context(
                    nc.sbuf_tensor("hsc%d_s" % p, [128, 16], F32)))
                xqtmb.append(esB.enter_context(
                    nc.sbuf_tensor("xqtm%d_s" % p, [128, HS], BF16)))
                xqfmb.append(esB.enter_context(
                    nc.sbuf_tensor("xqfm%d_s" % p, [128, 8, 128], BF16)))
                svrb.append(esB.enter_context(
                    nc.sbuf_tensor("svr%d_s" % p, [1, 128], BF16)))
                svcb.append(esB.enter_context(
                    nc.sbuf_tensor("svc%d_s" % p, [128, 1], BF16)))
                htmb.append(esB.enter_context(
                    nc.sbuf_tensor("htm%d_s" % p, [128, FF], F32)))
                hqtmb.append(esB.enter_context(
                    nc.sbuf_tensor("hqtm%d_s" % p, [128, FF], BF16)))
                hqfmb.append(esB.enter_context(
                    nc.sbuf_tensor("hqfm%d_s" % p, [128, 32, 128], BF16)))
            yxs = esB.enter_context(nc.sbuf_tensor("yxs_s", [128, HS], F32))
            q0xs = esB.enter_context(nc.sbuf_tensor("q0xs_s", [128, HS], I32))
            yhfs = esB.enter_context(nc.sbuf_tensor("yhfs_s", [128, FF], F32))
            q0hs = esB.enter_context(nc.sbuf_tensor("q0hs_s", [128, FF], I32))

            t_w1l = []

            def _(g):
                w8(g, *w_done[0])
                t_w1l.append(fin(g.dma_start(
                    bass.AP(w1q, 0, [[8 * FF, 128], [FF, 8], [1, FF]]),
                    bass.AP(d_wq1.tensor, 0, [[FF, 128], [128 * FF, 8], [1, FF]])),
                    "gp"))
            GPS(_)

            t_xpl = [None] * NT
            t_store = []

            def _(g):
                w8(g, *tok_W)
                for c2 in range(2):
                    t0, cnt0 = BTILES[c2]
                    t_xpl[c2] = fin(g.dma_start(
                        xpb[c2][0:cnt0, :],
                        bass.AP(d_xp.tensor, t0 * HS, [[HS, cnt0], [1, HS]])),
                        "gp")
            GPS(_)

            prev_tr = []
            t_tapyh = [None]
            pv_stats_free = [[], []]
            pv_hfree = [[], []]
            pv_xtr_pe = [None, None]
            pv_htr_pe = [None, None]
            pv_w1_tail = []
            pv_sv_pe = [None, None]
            pv_xnr_dve = [[], []]
            pv_rnd_x = [None]
            pv_rnd_h = [None]
            pv_w1_pe = [None, None]
            pv_hq_store = [None, None]
            S_tq = [None] * NT
            S_vec = [None] * NT
            X_sv = [None] * NT
            X_tr = [None] * NT
            W_ev = [None] * NT
            H_tq = [None] * NT

            S_dt = [None] * NT
            S_bv = [None] * NT

            def emit_S_a(k):
                p = k % 2
                co = 32 * p
                t0, cnt = BTILES[k]
                S_dt[k] = emit_ln_sq(xpb[p][0:cnt, :], cnt, xnrb[p],
                                     [t_xpl[k]], co=co, c_extra=True,
                                     minmax=True)

            def emit_S_b(k):
                p = k % 2
                co = 32 * p
                t0, cnt = BTILES[k]
                a_tok = emit_ln_fin(None, cnt, S_dt[k][1], co=co,
                                    c_extra=True)
                t_x1 = [a_tok]

                def _(v, cnt=cnt, k=k, p=p, co=co):
                    w8(v, t_x1[0], *tok_W)
                    w8(v, pv_sv_pe[p])
                    v.tensor_sub(st[0:cnt, co + 19:co + 20],
                                 st[0:cnt, co + 17:co + 18],
                                 st[0:cnt, co + 2:co + 3])
                    v.tensor_sub(st[0:cnt, co + 20:co + 21],
                                 st[0:cnt, co + 2:co + 3],
                                 st[0:cnt, co + 18:co + 19])
                    v.tensor_max(st[0:cnt, co + 20:co + 21],
                                 st[0:cnt, co + 20:co + 21],
                                 st[0:cnt, co + 19:co + 20])
                    v.tensor_mul(st[0:cnt, co + 20:co + 21],
                                 st[0:cnt, co + 20:co + 21],
                                 st[0:cnt, co + 15:co + 16])
                    v.tensor_scalar_max(absx[0:cnt, k:k + 1],
                                        st[0:cnt, co + 20:co + 21], 1e-5)
                    v.reciprocal(st[0:cnt, co + 21:co + 22],
                                 absx[0:cnt, k:k + 1])
                    v.tensor_mul(st[0:cnt, co + 22:co + 23],
                                 absx[0:cnt, k:k + 1],
                                 st[0:cnt, co + 21:co + 22])
                    v.tensor_scalar(st[0:cnt, co + 22:co + 23],
                                    st[0:cnt, co + 22:co + 23], -1.0, 2.0,
                                    ALU.mult, ALU.add)
                    v.tensor_mul(st[0:cnt, co + 21:co + 22],
                                 st[0:cnt, co + 21:co + 22],
                                 st[0:cnt, co + 22:co + 23])
                    v.tensor_scalar_mul(st[0:cnt, co + 23:co + 24],
                                        st[0:cnt, co + 21:co + 22], 127.0)
                    v.tensor_mul(st[0:cnt, co + 28:co + 29],
                                 st[0:cnt, co + 15:co + 16],
                                 st[0:cnt, co + 23:co + 24])
                    v.tensor_mul(st[0:cnt, co + 29:co + 30],
                                 st[0:cnt, co + 14:co + 15],
                                 st[0:cnt, co + 23:co + 24])
                    v.reciprocal(st[0:cnt, co + 24:co + 25], mu1b[0:cnt, 0:1])
                    v.tensor_mul(st[0:cnt, co + 25:co + 26],
                                 st[0:cnt, co + 21:co + 22],
                                 st[0:cnt, co + 24:co + 25])
                    v.tensor_scalar_mul(st[0:cnt, co + 25:co + 26],
                                        st[0:cnt, co + 25:co + 26], 254.0)
                    v.tensor_copy(svcb[p][0:cnt, 0:1],
                                  st[0:cnt, co + 25:co + 26])
                    v.tensor_mul(st[0:cnt, co + 26:co + 27],
                                 absx[0:cnt, k:k + 1], mu1b[0:cnt, 0:1])
                    t_x1.append(fin(v.tensor_scalar_mul(
                        st[0:cnt, co + 26:co + 27],
                        st[0:cnt, co + 26:co + 27], 1.0 / 254.0), "dve"))
                VEC(_)
                S_bv[k] = t_x1[1]

            def emit_S_c(k):
                p = k % 2
                co = 32 * p
                t0, cnt = BTILES[k]
                t_yx = []

                def _(sc, cnt=cnt, p=p, co=co):
                    w8(sc, S_bv[k], pv_rnd_x[0])
                    t_yx.append(fin(sc.activation(
                        yxs[0:cnt, :], xpb[p][0:cnt, :], AF.Identity,
                        scale=st[0:cnt, co + 28:co + 29],
                        bias=st[0:cnt, co + 29:co + 30]), "act"))
                SCL(_)
                pv_stats_free[p] = [S_dt[k][0], S_dt[k][1], S_bv[k], t_yx[0]]

                t_q = emit_round(yxs[0:cnt, :], q0xs[0:cnt, :],
                                 xqtmb[p][0:cnt, :], [t_yx[0], pv_xtr_pe[p]])
                pv_rnd_x[0] = t_q
                S_tq[k] = t_q
                S_vec[k] = S_bv[k]

                if k + 2 < NT:
                    def _(g, k=k, p=p):
                        w8(g, *pv_stats_free[p])
                        t2, cnt2 = BTILES[k + 2]
                        t_xpl[k + 2] = fin(g.dma_start(
                            xpb[p][0:cnt2, :],
                            bass.AP(d_xp.tensor, t2 * HS,
                                    [[HS, cnt2], [1, HS]])), "gp")
                    GPS(_)

            def emit_X(i):
                nonlocal prev_tr
                p = i % 2
                t0, cnt = BTILES[i]

                def _pe_xt(t, j, pt, cnt=cnt, i=i, p=p):
                    if j == 0:
                        w8(t, S_tq[i])
                    return t.transpose(pt[0:128, 0:cnt],
                                       xqtmb[p][0:cnt, j * 128:(j + 1) * 128],
                                       idb[0:cnt, 0:cnt])

                def _ev_xt(v, j, pt, cnt=cnt, p=p):
                    return v.tensor_copy(xqfmb[p][:, j, 0:cnt],
                                         pt[0:128, 0:cnt])

                t_xtr = pingpong(8, PTS, _pe_xt, _ev_xt, "dve",
                                 [pv_w1_pe[p]], list(prev_tr))
                prev_tr = [t_xtr[j][1] for j in range(4, 8)]
                pv_xtr_pe[p] = t_xtr[7][0]
                X_tr[i] = t_xtr

                t_sv = []

                def _(t, cnt=cnt, i=i, p=p):
                    w8(t, S_vec[i], t_xtr[6][1], t_xtr[7][1])
                    _m = t.transpose(PT1[0:1, 0:cnt], svcb[p][0:cnt, 0:1],
                                     idb[0:cnt, 0:cnt])
                    t_sv.append(fin(_m, "pe"))
                blk.tensor(_)
                pv_sv_pe[p] = t_sv[0]

                def _(v, cnt=cnt, p=p):
                    w8(v, t_sv[0], pv_w1_pe[p])
                    t_sv.append(fin(v.tensor_copy(svrb[p][0:1, 0:cnt],
                                                  PT1[0:1, 0:cnt]), "dve"))
                VEC(_)
                prev_tr = prev_tr[1:] + [t_sv[1]]
                X_sv[i] = t_sv[1]

            def _mk_w1(i, off):
                p = i % 2
                co = 32 * p
                t0, cnt = BTILES[i]

                def _pe(t, j, ps, cnt=cnt, i=i, p=p, off=off):
                    fs = off + j
                    if j == 0 and off == 0:
                        w8(t, X_tr[i][7][1], X_sv[i], t_w1l[0],
                           tok_const["b1"])
                    for dc in range(8):
                        t.matmul(ps[0:cnt, 0:512], xqfmb[p][:, dc, 0:cnt],
                                 w1q[:, dc, fs * 512:(fs + 1) * 512],
                                 start=(dc == 0), stop=False)
                    return t.matmul(ps[0:cnt, 0:512], svrb[p][0:1, 0:cnt],
                                    b1r[0:1, fs * 512:(fs + 1) * 512],
                                    start=False, stop=True)

                def _ev(sc, j, ps, cnt=cnt, p=p, co=co, off=off):
                    fs = off + j
                    return sc.activation(
                        htmb[p][0:cnt, fs * 512:(fs + 1) * 512],
                        ps[0:cnt, 0:512], AF.Gelu,
                        scale=st[0:cnt, co + 26:co + 27])
                return _pe, _ev

            def emit_W1(i):
                p = i % 2
                pe, ev = _mk_w1(i, 0)
                t_h = pingpong(4, [PSA, PSB, PSC, PSD], pe, ev, "act",
                               [S_vec[i]] + list(pv_hfree[p]),
                               list(pv_w1_tail))
                W_ev[i] = [t_h[j][1] for j in range(4)]

            def emit_W2(i):
                p = i % 2
                pe, ev = _mk_w1(i, 4)
                t_h = pingpong(4, [PSA, PSB, PSC, PSD], pe, ev, "act",
                               [], list(W_ev[i]))
                W_ev[i] = W_ev[i] + [t_h[j][1] for j in range(4)]
                pv_w1_tail[:] = [t_h[j][1] for j in range(4)]
                pv_w1_pe[p] = t_h[3][0]

            def emit_H(i):
                p = i % 2
                co = 32 * p
                t0, cnt = BTILES[i]
                t_hst = []

                def _(sc, cnt=cnt, i=i, p=p, co=co):
                    w8(sc, *W_ev[i])
                    w8(sc, t_tapyh[0])
                    for fs in range(8):
                        sc.activation(xnrb[p][0:cnt, 0:512],
                                      htmb[p][0:cnt, fs * 512:(fs + 1) * 512],
                                      AF.Square,
                                      accum_out=hscb[p][0:cnt, fs:fs + 1])
                    t_hst.append(fin(sc.activation(
                        xnrb[p][0:cnt, 0:1], hscb[p][0:cnt, 0:1],
                        AF.Identity), "act"))
                SCL(_)

                def _(v, cnt=cnt, p=p, co=co):
                    w8(v, t_hst[0])
                    v.reduce_sum(st[0:cnt, co + 1:co + 2],
                                 hscb[p][0:cnt, 0:8], axis=AX)
                    v.reduce_max(st[0:cnt, co + 20:co + 21],
                                 htmb[p][0:cnt, :], axis=AX,
                                 apply_absolute_value=True)
                    t_hst.append(fin(v.tensor_scalar_mul(
                        st[0:cnt, co + 4:co + 5], st[0:cnt, co + 1:co + 2],
                        1.0 / FF), "dve"))
                VEC(_)

                def _(sc, cnt=cnt, co=co):
                    w8(sc, t_hst[1])
                    sc.activation(st[0:cnt, co + 5:co + 6],
                                  st[0:cnt, co + 4:co + 5], AF.Ln,
                                  bias=cst[0:cnt, 3:4])
                    t_hst.append(fin(sc.activation(
                        st[0:cnt, co + 6:co + 7], st[0:cnt, co + 5:co + 6],
                        AF.Exp, scale=-0.5), "act"))
                SCL(_)

                def _(v, cnt=cnt, i=i, p=p, co=co):
                    w8(v, t_hst[2])
                    v.tensor_mul(st[0:cnt, co + 9:co + 10],
                                 st[0:cnt, co + 6:co + 7],
                                 st[0:cnt, co + 20:co + 21])
                    v.tensor_scalar_max(absh[0:cnt, i:i + 1],
                                        st[0:cnt, co + 9:co + 10], 1e-5)
                    v.reciprocal(st[0:cnt, co + 21:co + 22],
                                 absh[0:cnt, i:i + 1])
                    v.tensor_mul(st[0:cnt, co + 22:co + 23],
                                 absh[0:cnt, i:i + 1],
                                 st[0:cnt, co + 21:co + 22])
                    v.tensor_scalar(st[0:cnt, co + 22:co + 23],
                                    st[0:cnt, co + 22:co + 23], -1.0, 2.0,
                                    ALU.mult, ALU.add)
                    v.tensor_mul(st[0:cnt, co + 21:co + 22],
                                 st[0:cnt, co + 21:co + 22],
                                 st[0:cnt, co + 22:co + 23])
                    v.tensor_scalar_mul(st[0:cnt, co + 23:co + 24],
                                        st[0:cnt, co + 21:co + 22], 127.0)
                    t_hst.append(fin(v.tensor_mul(
                        st[0:cnt, co + 27:co + 28], st[0:cnt, co + 6:co + 7],
                        st[0:cnt, co + 23:co + 24]), "dve"))
                VEC(_)

                t_yh = []

                def _(sc, cnt=cnt, p=p, co=co):
                    w8(sc, t_hst[3], pv_htr_pe[p], t_tapyh[0], pv_rnd_h[0])
                    t_yh.append(fin(sc.activation(
                        yhfs[0:cnt, :], htmb[p][0:cnt, :], AF.Identity,
                        scale=st[0:cnt, co + 27:co + 28]), "act"))
                SCL(_)
                if DEBUG_TAPS:
                    def _(g, i=i, cnt=cnt, t_yh=t_yh):
                        w8(g, t_yh[0])
                        t_tapyh[0] = fin(g.dma_start(
                            bass.AP(d_tyh.tensor, i * 128 * FF,
                                    [[FF, cnt], [1, FF]]),
                            yhfs[0:cnt, :]), "gp")
                    GPS(_)
                t_hq_last = emit_round(yhfs[0:cnt, :], q0hs[0:cnt, :],
                                       hqtmb[p][0:cnt, :], [t_yh[0]])
                pv_rnd_h[0] = t_hq_last
                pv_hfree[p] = [t_hst[1], t_hq_last]
                H_tq[i] = t_hq_last

            def emit_T(i):
                nonlocal prev_tr
                p = i % 2
                t0, cnt = BTILES[i]

                def _pe_ht(t, j, pt, cnt=cnt, i=i, p=p):
                    if j == 0:
                        w8(t, H_tq[i])
                    return t.transpose(pt[0:128, 0:cnt],
                                       hqtmb[p][0:cnt, j * 128:(j + 1) * 128],
                                       idb[0:cnt, 0:cnt])

                def _ev_ht(v, j, pt, cnt=cnt, p=p):
                    return v.tensor_copy(hqfmb[p][:, j, 0:cnt],
                                         pt[0:128, 0:cnt])

                t_htr = pingpong(32, PTS, _pe_ht, _ev_ht, "dve",
                                 [pv_hq_store[p]], list(prev_tr))
                prev_tr = [t_htr[j][1] for j in range(28, 32)]
                pv_htr_pe[p] = t_htr[31][0]

                def _(g, i=i, p=p):
                    w8(g, t_htr[31][1])
                    t_store.append(fin(g.dma_start(
                        bass.AP(d_hq.tensor, i * 128 * FF,
                                [[FF, 128], [1, FF]]),
                        hqfmb[p][:].rearrange("p a b -> p (a b)")), "gp"))
                    pv_hq_store[p] = t_store[-1]
                GPS(_)

            emit_S_a(0)
            emit_S_b(0)
            emit_S_c(0)
            for i in range(NT):
                if i + 1 < NT:
                    emit_S_a(i + 1)
                emit_X(i)
                emit_W1(i)
                if i + 1 < NT:
                    emit_S_b(i + 1)
                emit_W2(i)
                if i + 1 < NT:
                    emit_S_c(i + 1)
                if i >= 1:
                    emit_T(i - 1)
                emit_H(i)
            emit_T(NT - 1)

            tok_B1 = now()

        with contextlib.ExitStack() as esB2:
            w2q = esB2.enter_context(nc.sbuf_tensor("w2q_s", [128, 32, HS], BF16))
            hqb, xprb, x2tb, ln2b, outtb, sv2rb, sv2cb = \
                [], [], [], [], [], [], []
            for p in range(2):
                hqb.append(esB2.enter_context(
                    nc.sbuf_tensor("hq%d_s" % p, [128, 32, 128], BF16)))
                xprb.append(esB2.enter_context(
                    nc.sbuf_tensor("xpr%d_s" % p, [128, HS], F32)))
                x2tb.append(esB2.enter_context(
                    nc.sbuf_tensor("x2t%d_s" % p, [128, HS], F32)))
                ln2b.append(esB2.enter_context(
                    nc.sbuf_tensor("ln2%d_s" % p, [128, HS], F32)))
                outtb.append(esB2.enter_context(
                    nc.sbuf_tensor("outt%d_s" % p, [128, HS], F32)))
                sv2rb.append(esB2.enter_context(
                    nc.sbuf_tensor("sv2r%d_s" % p, [1, 128], BF16)))
                sv2cb.append(esB2.enter_context(
                    nc.sbuf_tensor("sv2c%d_s" % p, [128, 1], BF16)))

            t_w2l = []

            def _(g):
                w8(g, *w_done[1])
                w8(g, *tok_B1)
                t_w2l.append(fin(g.dma_start(
                    bass.AP(w2q, 0, [[32 * HS, 128], [HS, 32], [1, HS]]),
                    bass.AP(d_wq2.tensor, 0,
                            [[HS, 128], [128 * HS, 32], [1, HS]])), "gp"))
            GPS(_)

            t_hql = [None] * NT
            t_xpl2 = [None] * NT
            t_out = []
            prev_tr2 = []
            pv_sv2_pe = [None, None]
            pv_x2_free = [[], []]
            pv_out_dve = [None, None]
            pv_w2_tail = []
            pv_w2_pe = [None, None]
            pv_out_store = [None, None]

            def _(g):
                w8(g, *tok_B1)
                for c2 in range(2):
                    t0, cnt0 = BTILES[c2]
                    t_hql[c2] = fin(g.dma_start(
                        hqb[c2][:].rearrange("p a b -> p (a b)"),
                        bass.AP(d_hq.tensor, c2 * 128 * FF,
                                [[FF, 128], [1, FF]])), "gp")
                    t_xpl2[c2] = fin(g.dma_start(
                        xprb[c2][0:cnt0, :],
                        bass.AP(d_xp.tensor, t0 * HS, [[HS, cnt0], [1, HS]])),
                        "gp")
            GPS(_)

            S2_tok = [None] * NT

            def emit_P2(k):
                nonlocal prev_tr2
                p = k % 2
                co = 32 * p
                t0, cnt = BTILES[k]
                t_s2 = []

                def _(v, cnt=cnt, k=k, p=p, co=co):
                    w8(v, *tok_B1)
                    w8(v, pv_sv2_pe[p], pv_out_dve[p])
                    v.reciprocal(st[0:cnt, co + 21:co + 22],
                                 absh[0:cnt, k:k + 1])
                    v.tensor_mul(st[0:cnt, co + 22:co + 23],
                                 absh[0:cnt, k:k + 1],
                                 st[0:cnt, co + 21:co + 22])
                    v.tensor_scalar(st[0:cnt, co + 22:co + 23],
                                    st[0:cnt, co + 22:co + 23], -1.0, 2.0,
                                    ALU.mult, ALU.add)
                    v.tensor_mul(st[0:cnt, co + 21:co + 22],
                                 st[0:cnt, co + 21:co + 22],
                                 st[0:cnt, co + 22:co + 23])
                    v.reciprocal(st[0:cnt, co + 24:co + 25], mu2b[0:cnt, 0:1])
                    v.tensor_mul(st[0:cnt, co + 25:co + 26],
                                 st[0:cnt, co + 21:co + 22],
                                 st[0:cnt, co + 24:co + 25])
                    v.tensor_scalar_mul(st[0:cnt, co + 25:co + 26],
                                        st[0:cnt, co + 25:co + 26], 254.0)
                    v.tensor_copy(sv2cb[p][0:cnt, 0:1],
                                  st[0:cnt, co + 25:co + 26])
                    v.tensor_mul(st[0:cnt, co + 26:co + 27],
                                 absh[0:cnt, k:k + 1], mu2b[0:cnt, 0:1])
                    t_s2.append(fin(v.tensor_scalar_mul(
                        st[0:cnt, co + 26:co + 27],
                        st[0:cnt, co + 26:co + 27], 1.0 / 254.0), "dve"))
                VEC(_)

                def _(t, cnt=cnt, p=p):
                    w8(t, t_s2[0], *prev_tr2)
                    _m = t.transpose(PT1[0:1, 0:cnt], sv2cb[p][0:cnt, 0:1],
                                     idb[0:cnt, 0:cnt])
                    t_s2.append(fin(_m, "pe"))
                blk.tensor(_)
                pv_sv2_pe[p] = t_s2[1]

                def _(v, cnt=cnt, p=p):
                    w8(v, t_s2[1], pv_w2_pe[p])
                    t_s2.append(fin(v.tensor_copy(sv2rb[p][0:1, 0:cnt],
                                                  PT1[0:1, 0:cnt]), "dve"))
                VEC(_)
                prev_tr2 = [t_s2[2]]
                S2_tok[k] = t_s2

            T_x2 = [None] * NT

            def emit_M2(i):
                p = i % 2
                co = 32 * p
                t0, cnt = BTILES[i]

                def _pe_w2(t, j, ps, cnt=cnt, i=i, p=p):
                    if j == 0:
                        w8(t, t_hql[i], t_w2l[0], S2_tok[i][2],
                           tok_const["b2"])
                    for fc in range(32):
                        t.matmul(ps[0:cnt, 0:512], hqb[p][:, fc, 0:cnt],
                                 w2q[:, fc, j * 512:(j + 1) * 512],
                                 start=(fc == 0), stop=False)
                    return t.matmul(ps[0:cnt, 0:512], sv2rb[p][0:1, 0:cnt],
                                    b2r[0:1, j * 512:(j + 1) * 512],
                                    start=False, stop=True)

                def _ev_w2(sc, j, ps, cnt=cnt, p=p, co=co):
                    return sc.activation(
                        x2tb[p][0:cnt, j * 512:(j + 1) * 512],
                        ps[0:cnt, 0:512], AF.Identity,
                        scale=st[0:cnt, co + 26:co + 27])

                t_x2 = pingpong(2, [PSA, PSB], _pe_w2, _ev_w2, "act",
                                [S2_tok[i][0]] + list(pv_x2_free[p]),
                                list(pv_w2_tail))
                pv_w2_tail[:] = [t_x2[0][1], t_x2[1][1]]
                pv_w2_pe[p] = t_x2[1][0]
                T_x2[i] = t_x2

            def emit_Q2(i):
                p = i % 2
                co = 32 * p
                t0, cnt = BTILES[i]
                t_x2 = T_x2[i]
                sq_tok, a_tok, d_tok = emit_ln_stats(
                    x2tb[p][0:cnt, :], cnt, ln2b[p],
                    [t_x2[0][1], t_x2[1][1], pv_out_dve[p]], co=co,
                    apply_out=ln2b[p][0:cnt, :])
                t_fin = [a_tok]

                def _(v, cnt=cnt, i=i, p=p):
                    w8(v, t_fin[0], t_xpl2[i], pv_out_store[p])
                    t_fin.append(fin(v.tensor_add(
                        outtb[p][0:cnt, :], ln2b[p][0:cnt, :],
                        xprb[p][0:cnt, :]), "dve"))
                VEC(_)
                pv_x2_free[p] = [sq_tok, d_tok, t_fin[0]]
                pv_out_dve[p] = t_fin[1]

                def _(g, t0=t0, cnt=cnt, p=p):
                    w8(g, t_fin[1])
                    t_out.append(fin(g.dma_start(
                        d_out[t0:t0 + cnt, :], outtb[p][0:cnt, :]), "gp"))
                    pv_out_store[p] = t_out[-1]
                GPS(_)

                if i + 2 < NT:
                    def _(g, i=i, sq_tok=sq_tok, t_fin=t_fin, t_x2=t_x2, p=p):
                        w8(g, t_x2[1][1], sq_tok, t_fin[1])
                        t2, cnt2 = BTILES[i + 2]
                        t_hql[i + 2] = fin(g.dma_start(
                            hqb[p][:].rearrange("p a b -> p (a b)"),
                            bass.AP(d_hq.tensor, (i + 2) * 128 * FF,
                                    [[FF, 128], [1, FF]])), "gp")
                        t_xpl2[i + 2] = fin(g.dma_start(
                            xprb[p][0:cnt2, :],
                            bass.AP(d_xp.tensor, t2 * HS,
                                    [[HS, cnt2], [1, HS]])), "gp")
                    GPS(_)

            emit_P2(0)
            for i in range(NT):
                emit_M2(i)
                if i + 1 < NT:
                    emit_P2(i + 1)
                emit_Q2(i)

            def _(g):
                for _q in ("gp", "hw"):
                    for _i in range(4):
                        _k = _q + str(_i)
                        g.wait_ge(SEMS[_k], C[_k])
            GPS(_)

    return nc


def _host_inputs(L0_attn_x, L0_attn_pos_emb, L0_attn_Wq, L0_attn_Wk, L0_attn_Wv,
                 L0_attn_Wpost, L0_attn_Wrel, L0_attn_per_dim_scale,
                 L0_ffn_w1, L0_ffn_b1, L0_ffn_w2, L0_ffn_b2):
    f = lambda a: np.ascontiguousarray(np.asarray(a), dtype=np.float32)
    x = f(L0_attn_x)
    B, S, _ = x.shape
    shards = []
    for b in range(B):
        for half in range(2):
            s = half * TQ
            sh = np.zeros((SHARD, HS), np.float32)
            lo = max(0, s - HALO)
            hi = min(S, s + SHARD - HALO)
            sh[lo - (s - HALO):hi - (s - HALO)] = x[b, lo:hi]
            shards.append(sh)

    mask = np.zeros((KTW, H, SUBT), np.float32)
    for j in range(SUBB):
        for qq in range(CHUNK):
            mask[12 * j:12 * j + CTX, :, 12 * j + qq] = 1.0
    mask8 = np.ascontiguousarray(mask.reshape(KTW, H * SUBT)).astype(bf)

    cstv = np.zeros((128, 8), np.float32)
    cstv[:, 0] = 0.5
    cstv[:, 1] = -0.5
    cstv[:, 2] = 1.0
    cstv[:, 3] = EPS

    common = dict(
        wq=f(L0_attn_Wq), wk=f(L0_attn_Wk), wv=f(L0_attn_Wv),
        wpost=f(L0_attn_Wpost), wrel=f(L0_attn_Wrel),
        posT=f(np.asarray(L0_attn_pos_emb).T),
        pds=f(L0_attn_per_dim_scale),
        w1T=f(np.asarray(L0_ffn_w1).T), w2T=f(np.asarray(L0_ffn_w2).T),
        b1=f(L0_ffn_b1), b2=f(L0_ffn_b2),
        cst=cstv,
        identb=np.eye(128, dtype=np.float32).astype(bf),
        onescol_b=np.ones((1, 128), np.float32).astype(bf),
        onescol_f=np.ones((1, 128), np.float32),
        ones128_f=np.ones((128, 1), np.float32),
        ones120_b=np.ones((128, 1), np.float32).astype(bf),
        mask8=mask8,
        bdz=np.zeros(2 * SUB_B, np.float32),
    )
    return [dict(common, xsh=shards[c]) for c in range(8)]


def kernel(**inputs):
    in_maps = _host_inputs(**inputs)
    nc = build_nc()
    res = run_bass_kernel_spmd(nc, in_maps, core_ids=list(range(8)))
    x = np.asarray(inputs["L0_attn_x"])
    B, S, _ = x.shape
    out = np.empty((B, S, HS), np.float32)
    for b in range(B):
        for half in range(2):
            s = half * TQ
            own = min(TQ, S - s)
            out[b, s:s + own] = np.asarray(
                res.results[2 * b + half]["out"])[0:own]
    return out


# version 16
# speedup vs baseline: 1.0760x; 1.0063x over previous
import contextlib
import math

import numpy as np
import ml_dtypes

import concourse.bass as bass
import concourse.mybir as mybir
from concourse.bass_utils import run_bass_kernel_spmd

F32 = mybir.dt.float32
BF16 = mybir.dt.bfloat16
I32 = mybir.dt.int32
AF = mybir.ActivationFunctionType
ALU = mybir.AluOpType
AX = mybir.AxisListType.X
bf = ml_dtypes.bfloat16

HS = 1024
H, D = 8, 128
CHUNK, CTX = 12, 24
PP = CTX + 1
SOFTCAP = 50.0
Q_SCALE = D ** (-0.5) / math.log(2)
K_SCALE = math.log(1.0 + math.e) / math.log(2)
QK = Q_SCALE * K_SCALE
EPS = 1e-5
FF = 4096

NBLK = 171
TQ = NBLK * CHUNK
HALO = 12
SHARD = 2176
NCHUNK = 17

SUBB = 9
SUBT = SUBB * CHUNK
NSUB = NBLK // SUBB
KTW = SUBT + 12
SUPER = [(0, 4), (4, 4), (8, 4), (12, 4), (16, 3)]

CSTRIDE = SUBT * PP
BSTRIDE = SUBB * (CHUNK * KTW + CHUNK)
SUB_C = H * CSTRIDE
SUB_B = H * BSTRIDE

BTILES = [(k * 128, min(128, TQ - k * 128)) for k in range((TQ + 127) // 128)]
NT = len(BTILES)
DEBUG_TAPS = False


def build_nc():
    nc = bass.Bass("TRN2", target_bir_lowering=False, debug=False)

    d_x = nc.dram_tensor("xsh", [SHARD, HS], F32, kind="ExternalInput").ap()
    d_wq = nc.dram_tensor("wq", [HS, HS], F32, kind="ExternalInput").ap()
    d_wk = nc.dram_tensor("wk", [HS, HS], F32, kind="ExternalInput").ap()
    d_wv = nc.dram_tensor("wv", [HS, HS], F32, kind="ExternalInput").ap()
    d_wp = nc.dram_tensor("wpost", [HS, HS], F32, kind="ExternalInput").ap()
    d_wr = nc.dram_tensor("wrel", [HS, HS], F32, kind="ExternalInput").ap()
    d_pt = nc.dram_tensor("posT", [HS, PP], F32, kind="ExternalInput").ap()
    d_pds = nc.dram_tensor("pds", [D], F32, kind="ExternalInput").ap()
    d_w1t = nc.dram_tensor("w1T", [HS, FF], F32, kind="ExternalInput").ap()
    d_w2t = nc.dram_tensor("w2T", [FF, HS], F32, kind="ExternalInput").ap()
    d_b1 = nc.dram_tensor("b1", [FF], F32, kind="ExternalInput").ap()
    d_b2 = nc.dram_tensor("b2", [HS], F32, kind="ExternalInput").ap()
    d_cst = nc.dram_tensor("cst", [128, 8], F32, kind="ExternalInput").ap()
    d_idb = nc.dram_tensor("identb", [128, 128], BF16, kind="ExternalInput").ap()
    d_oc_b = nc.dram_tensor("onescol_b", [1, 128], BF16, kind="ExternalInput").ap()
    d_oc_f = nc.dram_tensor("onescol_f", [1, 128], F32, kind="ExternalInput").ap()
    d_o128f = nc.dram_tensor("ones128_f", [128, 1], F32, kind="ExternalInput").ap()
    d_o120b = nc.dram_tensor("ones120_b", [128, 1], BF16, kind="ExternalInput").ap()
    d_mask = nc.dram_tensor("mask8", [KTW, H * SUBT], BF16, kind="ExternalInput").ap()
    d_out = nc.dram_tensor("out", [TQ, HS], F32, kind="ExternalOutput").ap()

    d_xp = nc.dram_tensor("xp_scr", [TQ * HS], BF16, kind="Internal").ap()
    d_bdc = nc.dram_tensor("bdc_scr", [2 * SUB_C], F32, kind="Internal").ap()
    d_bdb = nc.dram_tensor("bdz", [2 * SUB_B], F32, kind="ExternalInput").ap()
    d_wq1 = nc.dram_tensor("wq1_scr", [HS * FF], BF16, kind="Internal").ap()
    d_wq2 = nc.dram_tensor("wq2_scr", [FF * HS], BF16, kind="Internal").ap()
    d_hq = nc.dram_tensor("hq_scr", [NT * 128 * FF], BF16, kind="Internal").ap()
    d_wscr = nc.dram_tensor("wscr_scr", [512], F32, kind="Internal").ap()
    d_txq = (nc.dram_tensor("tapxq_scr", [NT * 128 * HS], BF16,
                            kind="Internal").ap() if DEBUG_TAPS else None)
    d_tyh = (nc.dram_tensor("tapyh_scr", [NT * 128 * FF], F32,
                            kind="Internal").ap() if DEBUG_TAPS else None)

    es = contextlib.ExitStack()
    with es:
        es.enter_context(nc.allow_low_precision(
            reason="bf16 matmul operands by design; fp32 accumulation kept"))
        SEMS = {}
        for _k in ["pe", "dve", "act"]:
            SEMS[_k] = es.enter_context(nc.semaphore("s_" + _k))
        for _q in ["gp", "hw", "ad"]:
            for _i in range(4):
                SEMS[_q + str(_i)] = es.enter_context(
                    nc.semaphore("s_%s%d" % (_q, _i)))
        SEMS["pl"] = es.enter_context(nc.semaphore("s_pl"))
        C = {k: 0 for k in SEMS}
        DMAQ = {"gp": {"idx": 0, "last": None}, "hw": {"idx": 0, "last": None},
                "ad": {"idx": 0, "last": None}}

        class _DmaQ:
            def __init__(self, eng, q):
                self._e = eng
                self._q = q

            def dma_start(self, *a, **kw):
                q = DMAQ[self._q]
                k = self._q + str(q["idx"] % 4)
                q["idx"] += 1
                self._e.wait_ge(SEMS[k], C[k])
                ins = self._e.dma_start(*a, **kw)
                ins.then_inc(SEMS[k], 16)
                C[k] += 16
                q["last"] = (k, C[k])
                return ins

            def __getattr__(self, name):
                return getattr(self._e, name)

        def GPS(f):
            blk.gpsimd(lambda g: f(_DmaQ(g, "gp")))

        def SYN(f):
            blk.sync(lambda y: f(_DmaQ(y, "hw")))

        def SCD(f):
            blk.scalar(lambda s2: f(_DmaQ(s2, "ad")))

        def fin(ins, k):
            if k in ("dve", "act", "pl"):
                return (k, C[k])
            if k in ("gp", "hw", "ad"):
                return DMAQ[k]["last"]
            ins.then_inc(SEMS[k], 1)
            C[k] += 1
            return (k, C[k])

        def w8(eng, *toks):
            best = {}
            for t in toks:
                if t is None:
                    continue
                k, v = t
                if v > 0 and v > best.get(k, 0):
                    best[k] = v
            for k, v in best.items():
                eng.wait_ge(SEMS[k], v)

        def now():
            return [(k, C[k]) for k in C]

        _CHAIN_OPS = {
            "activation", "tensor_copy", "tensor_tensor", "tensor_add",
            "tensor_sub", "tensor_mul", "tensor_max", "tensor_scalar",
            "tensor_single_scalar", "tensor_scalar_add", "tensor_scalar_sub",
            "tensor_scalar_mul", "tensor_scalar_max", "tensor_scalar_min",
            "tensor_relu", "tensor_reduce", "reduce_max", "reduce_sum",
            "scalar_tensor_tensor", "reciprocal", "memset", "memzero",
            "select", "copy_predicated", "sign", "square", "sqrt", "mul",
            "add", "copy",
        }

        class _Chain:
            def __init__(self, eng, key):
                self._e = eng
                self._k = key

            def __getattr__(self, name):
                f = getattr(self._e, name)
                if name in _CHAIN_OPS:
                    k = self._k

                    def g(*a, **kw):
                        self._e.wait_ge(SEMS[k], C[k])
                        ins = f(*a, **kw)
                        ins.then_inc(SEMS[k], 1)
                        C[k] += 1
                        return ins
                    return g
                return f

        def VEC(f):
            blk.vector(lambda v: f(_Chain(v, "dve")))

        def SCL(f):
            blk.scalar(lambda s: f(_Chain(s, "act")))

        def POOL(f):
            blk.gpsimd(lambda g: f(_Chain(_DmaQ(g, "gp"), "pl")))

        cst = es.enter_context(nc.sbuf_tensor("cst_s", [128, 8], F32))
        idb = es.enter_context(nc.sbuf_tensor("idb_s", [128, 128], BF16))
        oc_b = es.enter_context(nc.sbuf_tensor("ocb_s", [1, 128], BF16))
        oc_f = es.enter_context(nc.sbuf_tensor("ocf_s", [1, 128], F32))
        o128f = es.enter_context(nc.sbuf_tensor("o128f_s", [128, 1], F32))
        o120b = es.enter_context(nc.sbuf_tensor("o120b_s", [128, 1], BF16))
        maskb = es.enter_context(nc.sbuf_tensor("mask_s", [KTW, H * SUBT], BF16))
        b1r = es.enter_context(nc.sbuf_tensor("b1r_s", [1, FF], BF16))
        b2r = es.enter_context(nc.sbuf_tensor("b2r_s", [1, HS], BF16))
        qsc = es.enter_context(nc.sbuf_tensor("qsc_s", [128, 1], F32))
        st = es.enter_context(nc.sbuf_tensor("st_s", [128, 64], F32))
        absx = es.enter_context(nc.sbuf_tensor("absx_s", [128, NT], F32))
        absh = es.enter_context(nc.sbuf_tensor("absh_s", [128, NT], F32))
        mu1b = es.enter_context(nc.sbuf_tensor("mu1b_s", [128, 1], F32))
        mu2b = es.enter_context(nc.sbuf_tensor("mu2b_s", [128, 1], F32))

        PSA = es.enter_context(nc.psum_tensor("psA", [128, 512], F32))
        PSB = es.enter_context(nc.psum_tensor("psB", [128, 512], F32))
        PSC = es.enter_context(nc.psum_tensor("psC", [128, 512], F32))
        PSD = es.enter_context(nc.psum_tensor("psD", [128, 512], F32))
        PT1 = es.enter_context(nc.psum_tensor("psT1", [128, 1024], BF16))
        PT2 = es.enter_context(nc.psum_tensor("psT2", [128, 1024], BF16))
        PT3 = es.enter_context(nc.psum_tensor("psT3", [128, 1024], BF16))
        PT4 = es.enter_context(nc.psum_tensor("psT4", [128, 1024], BF16))
        PTS = [PT1, PT2, PT3, PT4]

        blk = es.enter_context(nc.Block())

        AP_HALF = cst[:, 0:1]
        AP_NHALF = cst[:, 1:2]
        AP_ONE = cst[:, 2:3]
        AP_EPS = cst[:, 3:4]

        def pingpong(n, slots, pe_emit, ev_emit, ev_eng, pre_deps, prev_users):
            toks = []
            ns = len(slots)
            for w0 in range(0, n, ns):
                wN = min(n, w0 + ns)

                def _t(t, w0=w0, wN=wN):
                    for i in range(w0, wN):
                        if i < ns:
                            w8(t, *pre_deps)
                            w8(t, *prev_users)
                        else:
                            w8(t, toks[i - ns][1])
                        last = pe_emit(t, i, slots[i % ns])
                        toks.append([fin(last, "pe"), None])
                blk.tensor(_t)

                def _e(e, w0=w0, wN=wN):
                    for i in range(w0, wN):
                        w8(e, toks[i][0])
                        toks[i][1] = fin(ev_emit(e, i, slots[i % ns]), ev_eng)
                (SCL if ev_eng == "act" else VEC)(_e)
            return toks

        tok_const = {}

        def _(g):
            tok_const["cst"] = fin(g.dma_start(cst[:], d_cst), "gp")
            tok_const["idb"] = fin(g.dma_start(idb[:], d_idb), "gp")
            tok_const["oc_b"] = fin(g.dma_start(oc_b[:], d_oc_b), "gp")
            tok_const["oc_f"] = fin(g.dma_start(oc_f[:], d_oc_f), "gp")
            tok_const["o128f"] = fin(g.dma_start(o128f[:], d_o128f), "gp")
            tok_const["o120b"] = fin(g.dma_start(o120b[:], d_o120b), "gp")
            tok_const["mask"] = fin(g.dma_start(maskb[:], d_mask), "gp")
            tok_const["pds"] = fin(g.dma_start(
                qsc[:], bass.AP(d_pds.tensor, 0, [[1, 128], [1, 1]])), "gp")
            tok_const["b1"] = fin(g.dma_start(b1r[:], d_b1.unsqueeze(0)), "gp")
            tok_const["b2"] = fin(g.dma_start(b2r[:], d_b2.unsqueeze(0)), "gp")
        GPS(_)

        def emit_ln_sq(X, cnt, scratch, deps, co=0, c_extra=False,
                       minmax=False):
            W = X.shape[-1]
            ta, td = [], []

            def sl(a, b=None):
                return st[0:cnt, co + a:co + (b if b is not None else a + 1)]

            def _(sc):
                w8(sc, *deps)
                ta.append(fin(sc.activation(scratch[0:cnt, 0:W], X, AF.Square,
                                            accum_out=sl(1)), "act"))
            SCL(_)

            def _(v):
                w8(v, *deps)
                if minmax:
                    v.tensor_reduce(sl(17), X, AX, ALU.max)
                    v.tensor_reduce(sl(18), X, AX, ALU.min)
                v.reduce_sum(sl(0), X, axis=AX)
                v.tensor_scalar_mul(sl(2), sl(0), 1.0 / W)
                v.tensor_mul(sl(3), sl(2), sl(2))
                w8(v, ta[0])
                v.tensor_scalar(sl(4), sl(1), 1.0 / W, None, ALU.mult)
                last = v.tensor_sub(sl(4), sl(4), sl(3))
                if c_extra:
                    v.tensor_scalar_add(sl(9), sl(4), EPS)
                    v.reciprocal(sl(10), sl(9))
                    last = v.tensor_mul(sl(11), sl(4), sl(10))
                td.append(fin(last, "dve"))
            VEC(_)
            return ta[0], td[0]

        def emit_ln_fin(X, cnt, d_tok, co=0, c_extra=False, apply_out=None,
                        deps=()):
            ta = []

            def sl(a, b=None):
                return st[0:cnt, co + a:co + (b if b is not None else a + 1)]

            def _(sc):
                w8(sc, d_tok, *deps)
                sc.activation(sl(5), sl(4), AF.Ln, bias=cst[0:cnt, 3:4])
                sc.activation(sl(6), sl(5), AF.Exp, scale=-0.5)
                sc.activation(sl(7), sl(6), AF.Identity, scale=-1.0)
                last = sc.activation(sl(8), sl(2), AF.Identity, scale=sl(7))
                if c_extra:
                    sc.activation(sl(12), sl(11), AF.Ln, bias=cst[0:cnt, 3:4])
                    sc.activation(sl(13), sl(12), AF.Exp, scale=-0.5)
                    sc.activation(sl(15), sl(6), AF.Identity, scale=sl(13))
                    sc.activation(sl(16), sl(15), AF.Identity, scale=-1.0)
                    last = sc.activation(sl(14), sl(2), AF.Identity,
                                         scale=sl(16))
                if apply_out is not None:
                    if c_extra:
                        last = sc.activation(apply_out, X, AF.Identity,
                                             scale=sl(15), bias=sl(14))
                    else:
                        last = sc.activation(apply_out, X, AF.Identity,
                                             scale=sl(6), bias=sl(8))
                ta.append(fin(last, "act"))
            SCL(_)
            return ta[0]

        def emit_ln_stats(X, cnt, scratch, deps, c_extra=False, co=0,
                          apply_out=None, minmax=False):
            sq_tok, d_tok = emit_ln_sq(X, cnt, scratch, deps, co=co,
                                       c_extra=c_extra, minmax=minmax)
            a_tok = emit_ln_fin(X, cnt, d_tok, co=co, c_extra=c_extra,
                                apply_out=apply_out)
            return sq_tok, a_tok, d_tok

        def emit_round(Y, Q0I, OUT, deps):
            tk = {}

            def _(v):
                w8(v, *deps)
                v.tensor_copy(Q0I, Y)
                tk["q"] = fin(v.tensor_copy(OUT, Q0I), "dve")
            VEC(_)
            return tk["q"]

        with contextlib.ExitStack() as esA:
            wq_s = esA.enter_context(nc.sbuf_tensor("wq_s", [128, 8, HS], BF16))
            wk_s = esA.enter_context(nc.sbuf_tensor("wk_s", [128, 8, HS], BF16))
            wv_s = esA.enter_context(nc.sbuf_tensor("wv_s", [128, 8, HS], BF16))
            wp_s = esA.enter_context(nc.sbuf_tensor("wp_s", [128, 8, HS], BF16))
            rel_s = esA.enter_context(nc.sbuf_tensor("rel_s", [128, 8, PP], BF16))
            tok_bdb_zero = None

            def _(g):
                for nm, dst, srct in [("wq", wq_s, d_wq), ("wk", wk_s, d_wk),
                                      ("wv", wv_s, d_wv), ("wp", wp_s, d_wp)]:
                    tok_const[nm] = fin(g.dma_start(
                        bass.AP(dst, 0, [[8 * HS, 128], [HS, 8], [1, HS]]),
                        bass.AP(srct.tensor, 0,
                                [[HS, 128], [128 * HS, 8], [1, HS]])), "gp")
            GPS(_)

            t_qs = []

            def _(sc):
                w8(sc, tok_const["pds"], tok_const["cst"])
                sc.activation(st[:, 16:17], qsc[:], AF.Abs)
                sc.activation(st[:, 17:18], st[:, 16:17], AF.Exp, scale=-1.0)
                t_qs.append(fin(sc.activation(st[:, 18:19], st[:, 17:18], AF.Ln,
                                              bias=AP_ONE), "act"))
            SCL(_)

            def _(v):
                w8(v, t_qs[0])
                v.tensor_scalar_max(st[:, 19:20], qsc[:], 0.0)
                v.tensor_add(st[:, 20:21], st[:, 19:20], st[:, 18:19])
                t_qs.append(fin(v.tensor_scalar_mul(qsc[:], st[:, 20:21], QK), "dve"))
            VEC(_)


            with contextlib.ExitStack() as esR:
                wr_s = esR.enter_context(
                    nc.sbuf_tensor("wr_s", [128, 8, HS], BF16))
                pt_s = esR.enter_context(
                    nc.sbuf_tensor("pt_s", [128, 8, PP], BF16))

                def _(g):
                    tok_const["wr"] = fin(g.dma_start(
                        bass.AP(wr_s, 0, [[8 * HS, 128], [HS, 8], [1, HS]]),
                        bass.AP(d_wr.tensor, 0,
                                [[HS, 128], [128 * HS, 8], [1, HS]])), "gp")
                    tok_const["pt"] = fin(g.dma_start(
                        bass.AP(pt_s, 0, [[8 * PP, 128], [PP, 8], [1, PP]]),
                        bass.AP(d_pt.tensor, 0,
                                [[PP, 128], [128 * PP, 8], [1, PP]])), "gp")
                GPS(_)

                def _pe_rel(t, i, ps):
                    for dc in range(8):
                        last = t.matmul(ps[0:128, 0:PP],
                                        wr_s[:, dc, i * 128:(i + 1) * 128],
                                        pt_s[:, dc, :], start=(dc == 0),
                                        stop=(dc == 7))
                    return last

                def _ev_rel(sc, i, ps):
                    return sc.activation(rel_s[:, i, :], ps[0:128, 0:PP],
                                         AF.Identity, scale=1.0 / K_SCALE)

                t_rel = pingpong(H, [PSA, PSB, PSC, PSD], _pe_rel, _ev_rel,
                                 "act", [tok_const["wr"], tok_const["pt"]], [])
                tok_rel = t_rel[H - 1][1]

            wcr = [esA.enter_context(
                nc.sbuf_tensor("wcr%d_s" % i, [128, 512], F32))
                for i in range(4)]
            sg1w = [esA.enter_context(
                nc.sbuf_tensor("sg1%d_s" % i, [128, 512], BF16))
                for i in range(4)]
            sg2w = [esA.enter_context(
                nc.sbuf_tensor("sg2%d_s" % i, [128, 512], BF16))
                for i in range(4)]
            wqow = [esA.enter_context(
                nc.sbuf_tensor("wqo%d_s" % i, [128, 512], BF16))
                for i in range(2)]
            asum = esA.enter_context(nc.sbuf_tensor("asum_s", [128, 66], F32))
            thbw = esA.enter_context(nc.sbuf_tensor("thb_s", [128, 8], F32))
            thrw = esA.enter_context(nc.sbuf_tensor("thr_s", [1, 8], F32))
            prow = esA.enter_context(nc.sbuf_tensor("prow_s", [1, 256], F32))

            def _w_src(wi, u):
                if wi == 0:
                    c2, q2 = divmod(u, 8)
                    return bass.AP(d_w1t.tensor, c2 * 128 * FF + q2 * 512,
                                   [[FF, 128], [1, 512]])
                c2, q2 = divmod(u, 2)
                return bass.AP(d_w2t.tensor, c2 * 128 * HS + q2 * 512,
                               [[HS, 128], [1, 512]])

            def _w_dst(wi, u):
                if wi == 0:
                    c2, q2 = divmod(u, 8)
                    return bass.AP(d_wq1.tensor, c2 * 128 * FF + q2 * 512,
                                   [[FF, 128], [1, 512]])
                c2, q2 = divmod(u, 2)
                return bass.AP(d_wq2.tensor, c2 * 128 * HS + q2 * 512,
                               [[HS, 128], [1, 512]])

            WUNITS = ([("scan", 0, u) for u in range(64)] + [("fin", 0, 0)] +
                      [("sign", 0, u) for u in range(64)] +
                      [("scan", 1, u) for u in range(64)] + [("fin", 1, 0)] +
                      [("sign", 1, u) for u in range(64)])
            WLOADS = [un for un in WUNITS if un[0] != "fin"]
            WLIDX = {un: j for j, un in enumerate(WLOADS)}
            w_load_tok, w_comp_tok, w_fin_tok = {}, {}, {}
            w_asum_rd, w_prow_rd = {}, {}
            w_done = {0: [], 1: []}
            w_sg_pool = [None] * 4
            w_wqo_st = [None, None]
            w_li = [0]
            w_ci = [0]

            def _w_emit_loads(upto):
                lo, hi = w_li[0], min(upto, len(WLOADS))
                if lo >= hi:
                    return

                def _(sd):
                    for j in range(lo, hi):
                        un = WLOADS[j]
                        if j >= 4:
                            w8(sd, w_comp_tok.get(WLOADS[j - 4]))
                        else:
                            w8(sd, tok_rel)
                        w_load_tok[un] = fin(sd.dma_start(
                            wcr[j % 4][:], _w_src(un[1], un[2])), "ad")
                SCD(_)
                w_li[0] = hi

            def _w_emit_fin(wi):
                base = 4 * wi
                po = 128 * wi
                so = 256 * wi
                toks = []

                def _(v):
                    w8(v, *[w_comp_tok[("scan", wi, u)] for u in range(64)])
                    w8(v, w_asum_rd.get(1 - wi))
                    toks.append(fin(v.tensor_reduce(
                        asum[:, 64:65], asum[:, 0:64], AX, ALU.add), "dve"))
                VEC(_)

                def _(g):
                    w8(g, toks[0])
                    toks.append(fin(g.dma_start(
                        bass.AP(d_wscr.tensor, so, [[1, 128]]),
                        asum[:, 64:65]), "gp"))
                GPS(_)
                w_asum_rd[wi] = toks[1]

                def _(g):
                    w8(g, toks[1], w_prow_rd.get(1 - wi))
                    toks.append(fin(g.dma_start(
                        prow[0:1, po:po + 128],
                        bass.AP(d_wscr.tensor, so, [[1, 128]])), "gp"))
                GPS(_)

                def _(v):
                    w8(v, toks[2])
                    v.tensor_reduce(thrw[0:1, base:base + 1],
                                    prow[0:1, po:po + 128], AX, ALU.add)
                    v.tensor_scalar_mul(thrw[0:1, base:base + 1],
                                        thrw[0:1, base:base + 1],
                                        1.0 / (HS * FF))
                    v.tensor_scalar_max(thrw[0:1, base:base + 1],
                                        thrw[0:1, base:base + 1], 1e-5)
                    v.tensor_scalar_mul(thrw[0:1, base + 1:base + 2],
                                        thrw[0:1, base:base + 1], 0.5)
                    toks.append(fin(v.tensor_scalar_mul(
                        thrw[0:1, base + 2:base + 3],
                        thrw[0:1, base + 1:base + 2], -1.0), "dve"))
                VEC(_)
                w_prow_rd[wi] = toks[3]

                def _(g):
                    w8(g, toks[3])
                    toks.append(fin(g.dma_start(
                        bass.AP(d_wscr.tensor, so + 128, [[1, 3]]),
                        thrw[0:1, base:base + 3]), "gp"))
                GPS(_)

                def _(g):
                    w8(g, toks[4])
                    t_thb = fin(g.dma_start(
                        thbw[:, 3 * wi:3 * wi + 3],
                        bass.AP(d_wscr.tensor, so + 128,
                                [[0, 128], [1, 3]])), "gp")
                    mub = mu1b if wi == 0 else mu2b
                    t_mub = fin(g.dma_start(
                        mub[:, 0:1],
                        bass.AP(d_wscr.tensor, so + 128,
                                [[0, 128], [1, 1]])), "gp")
                    toks.append((t_thb, t_mub))
                GPS(_)
                w_fin_tok[wi] = toks[5]

            def wpump(k):
                if w_ci[0] >= len(WUNITS):
                    return
                batch = []
                while len(batch) < k and w_ci[0] < len(WUNITS):
                    un = WUNITS[w_ci[0]]
                    if un[0] != "fin" and WLIDX[un] >= w_li[0]:
                        break
                    batch.append(un)
                    w_ci[0] += 1
                    if un[0] == "fin":
                        break
                scans = [u2 for u2 in batch if u2[0] == "scan"]
                signs = [u2 for u2 in batch if u2[0] == "sign"]
                fins = [u2 for u2 in batch if u2[0] == "fin"]
                if scans:
                    def _(v, scans=scans):
                        for un in scans:
                            j = WLIDX[un]
                            w8(v, w_load_tok[un])
                            w_comp_tok[un] = fin(v.tensor_reduce(
                                asum[:, un[2]:un[2] + 1], wcr[j % 4][:],
                                AX, ALU.add, apply_absolute_value=True),
                                "dve")
                    VEC(_)
                for pi in range(0, len(signs), 4):
                    pair = signs[pi:pi + 4]

                    def _(sc, pair=pair):
                        for pj, un in enumerate(pair):
                            wi, u = un[1], un[2]
                            j = WLIDX[un]
                            w8(sc, w_load_tok[un], *w_fin_tok[wi])
                            w8(sc, w_sg_pool[pj])
                            sc.activation(sg1w[pj][:], wcr[j % 4][:], AF.Sign,
                                          bias=thbw[:, 3 * wi + 2:3 * wi + 3])
                            w_comp_tok[un] = fin(sc.activation(
                                sg2w[pj][:], wcr[j % 4][:], AF.Sign,
                                bias=thbw[:, 3 * wi + 1:3 * wi + 2]), "act")
                    SCL(_)

                    def _(g, pair=pair):
                        for pj, un in enumerate(pair):
                            wi, u = un[1], un[2]
                            w8(g, w_comp_tok[un], w_wqo_st[pj % 2])
                            t_add = fin(g.tensor_add(
                                wqow[pj % 2][:], sg1w[pj][:], sg2w[pj][:]),
                                "pl")
                            w_sg_pool[pj] = t_add
                            w8(g, t_add)
                            tst = fin(g.dma_start(_w_dst(wi, u),
                                                  wqow[pj % 2][:]), "gp")
                            w_wqo_st[pj % 2] = tst
                            w_done[wi].append(tst)
                    POOL(_)
                for un in fins:
                    _w_emit_fin(un[1])
                ncl = sum(1 for x2 in WUNITS[:w_ci[0]] if x2[0] != "fin")
                _w_emit_loads(ncl + 4)

            _w_emit_loads(4)


            xn = esA.enter_context(nc.sbuf_tensor("xn_s", [128, 8, SHARD], BF16))
            xt0 = esA.enter_context(nc.sbuf_tensor("xt0_s", [128, HS], F32))
            xt1 = esA.enter_context(nc.sbuf_tensor("xt1_s", [128, HS], F32))
            xnt = esA.enter_context(nc.sbuf_tensor("xnt_s", [128, HS], BF16))
            q_st = esA.enter_context(nc.sbuf_tensor("q_st", [128, 8, 432], BF16))
            k_st = esA.enter_context(nc.sbuf_tensor("k_st", [128, 8, 444], BF16))
            a_st = esA.enter_context(nc.sbuf_tensor("a_st", [128, 8, 432], BF16))
            v_s0 = esA.enter_context(nc.sbuf_tensor("v_s0", [128, HS], BF16))
            v_s1 = esA.enter_context(nc.sbuf_tensor("v_s1", [128, HS], BF16))
            bdst2 = [esA.enter_context(
                nc.sbuf_tensor("bdst%d_s" % _i, [128, H, PP], F32))
                for _i in range(2)]
            scst = esA.enter_context(nc.sbuf_tensor("scst_s", [128, H, KTW], F32))
            bdbs2 = [esA.enter_context(
                nc.sbuf_tensor("bdbs%d_s" % _i, [128, H, KTW], F32))
                for _i in range(2)]
            s1st = esA.enter_context(nc.sbuf_tensor("s1st_s", [128, H, KTW], F32))
            ths = esA.enter_context(nc.sbuf_tensor("ths_s", [128, H, KTW], BF16))
            emst = esA.enter_context(nc.sbuf_tensor("emst_s", [KTW, H, SUBT], BF16))
            rcr = esA.enter_context(nc.sbuf_tensor("rcr_s", [1, H * SUBT], BF16))
            rbst = esA.enter_context(nc.sbuf_tensor("rbst_s", [128, H * SUBT], F32))
            y_fm = esA.enter_context(nc.sbuf_tensor("yfm_s", [128, 8, 432], BF16))
            y_tm = esA.enter_context(nc.sbuf_tensor("ytm_s", [128, HS], BF16))
            lny = esA.enter_context(nc.sbuf_tensor("lny_s", [128, HS], F32))
            xpt = esA.enter_context(nc.sbuf_tensor("xpt_s", [128, HS], BF16))

            xtiles = [xt0, xt1]
            t_xld = [None] * NCHUNK
            prev_tr = []
            t_a1_last = [None] * NCHUNK

            def _(g):
                for c2 in range(2):
                    t_xld[c2] = fin(g.dma_start(
                        xtiles[c2][:], d_x[c2 * 128:(c2 + 1) * 128, :]), "gp")
            GPS(_)

            prev_xnt_pe = None
            A1sq = [None] * NCHUNK
            A1sq[0] = emit_ln_sq(xtiles[0][:], 128, xnt,
                                 [t_xld[0], tok_const["cst"]], co=0)
            for c in range(NCHUNK):
                wpump(4)
                X = xtiles[c % 2]
                co = 32 * (c % 2)
                if c + 1 < NCHUNK:
                    A1sq[c + 1] = emit_ln_sq(
                        xtiles[(c + 1) % 2][:], 128, xnt,
                        [t_xld[c + 1], prev_xnt_pe], co=32 * ((c + 1) % 2))
                sq_tok, d_tok = A1sq[c]
                a_tok = emit_ln_fin(X[:], 128, d_tok, co=co,
                                    apply_out=xnt[:], deps=(prev_xnt_pe,))
                t_ap = [a_tok]

                def _pe_tr(t, i, pt, c=c, t_ap=t_ap):
                    if i == 0:
                        w8(t, t_ap[0], tok_const["idb"])
                    return t.transpose(pt[0:128, 0:128],
                                       xnt[:, i * 128:(i + 1) * 128], idb[:])

                def _ev_tr(v, i, pt, c=c):
                    return v.tensor_copy(xn[:, i, c * 128:(c + 1) * 128],
                                         pt[0:128, 0:128])

                t_tr = pingpong(8, PTS, _pe_tr, _ev_tr, "dve",
                                [], list(prev_tr))
                prev_tr = [t_tr[4][1], t_tr[5][1], t_tr[6][1], t_tr[7][1]]
                t_a1_last[c] = t_tr[7][1]
                prev_xnt_pe = t_tr[7][0]

                if c + 2 < NCHUNK:
                    def _(g, c=c, sq_tok=sq_tok, d_tok=d_tok, t_ap=t_ap):
                        w8(g, t_ap[0], sq_tok, d_tok)
                        t_xld[c + 2] = fin(g.dma_start(
                            xtiles[c % 2][:],
                            d_x[(c + 2) * 128:(c + 3) * 128, :]), "gp")
                    GPS(_)

            tok_xn = now()

            last_super = now()
            prev_ep = list(tok_xn)
            prev_pv_evac = []
            prev_rb_acts = []
            prev_v_evac = []
            t_pv = []
            cd_users = []
            t_bdw2 = [None, None]
            t_shr = [None, None]
            t_rdb2 = [None, None]
            pv_s1 = [None, None]
            pv_tanh = [None]
            t_bd_h = [None, None]
            pv_qk_pe = [None]
            pv_wp_ev = []
            for si, (sub0, nsub) in enumerate(SUPER):
                S = sub0 * SUBT
                T = nsub * SUBT
                sh = S + HALO

                def _pe_q(t, i, ps, S=S, T=T, sh=sh):
                    if i == 0:
                        w8(t, *tok_xn)
                        w8(t, tok_const["wq"], tok_const["wk"], t_qs[-1],
                           pv_qk_pe[0])
                    for dc in range(8):
                        last = t.matmul(ps[0:128, 0:T],
                                        wq_s[:, dc, i * 128:(i + 1) * 128],
                                        xn[:, dc, sh:sh + T],
                                        start=(dc == 0), stop=(dc == 7))
                    return last

                def _ev_q(sc, i, ps, T=T):
                    return sc.activation(q_st[:, i, 0:T], ps[0:128, 0:T],
                                         AF.Identity, scale=qsc[:, 0:1])

                t_projq = pingpong(8, [PSA, PSB, PSC, PSD], _pe_q, _ev_q,
                                   "act", [],
                                   list(prev_rb_acts) + list(prev_pv_evac)
                                   + list(pv_wp_ev))

                def _pe_k(t, i, ps, T=T, sh=sh):
                    for dc in range(8):
                        last = t.matmul(ps[0:128, 0:T + 12],
                                        wk_s[:, dc, i * 128:(i + 1) * 128],
                                        xn[:, dc, sh - 12:sh + T],
                                        start=(dc == 0), stop=(dc == 7))
                    return last

                def _ev_k(v, i, ps, T=T):
                    return v.tensor_copy(k_st[:, i, 0:T + 12],
                                         ps[0:128, 0:T + 12])

                t_projk = pingpong(8, [PSA, PSB, PSC, PSD], _pe_k, _ev_k,
                                   "dve", [],
                                   [t_projq[j][1] for j in range(4, 8)])
                t_proj = t_projq + t_projk

                cd_users[:] = [t_proj[14][1], t_proj[15][1]]
                prev_pv_evac = []

                def emit_bd(su):
                    p = su % 2
                    s0 = su * SUBT

                    def _pe_bd(t, i, ps, s0=s0, p=p):
                        if i == 0:
                            w8(t, t_proj[7][1], tok_rel, t_bdw2[p])
                        return t.matmul(ps[0:SUBT, 0:PP],
                                        q_st[:, i, s0:s0 + SUBT],
                                        rel_s[:, i, :], start=True, stop=True)

                    def _ev_bd(v, i, ps, p=p):
                        return v.tensor_copy(bdst2[p][0:SUBT, i, :],
                                             ps[0:SUBT, 0:PP])

                    t_bd = pingpong(H, [PSC, PSD], _pe_bd, _ev_bd, "dve",
                                    [], list(cd_users))
                    cd_users[:] = [t_bd[H - 2][1], t_bd[H - 1][1]]
                    t_bd_h[p] = t_bd

                    def _(y, p=p):
                        w8(y, t_bd[H - 1][1], t_shr[p])
                        t_bdw2[p] = fin(y.dma_start(
                            bass.AP(d_bdc.tensor, p * SUB_C,
                                    [[PP, SUBT], [CSTRIDE, H], [1, PP]]),
                            bass.AP(bdst2[p], 0,
                                    [[H * PP, SUBT], [PP, H], [1, PP]])), "hw")
                        w8(y, t_bdw2[p], t_rdb2[p])
                        t_shr[p] = fin(y.dma_start(
                            bass.AP(d_bdb.tensor, p * SUB_B,
                                    [[CHUNK * KTW + CHUNK, H * SUBB],
                                     [KTW, CHUNK], [1, CTX]]),
                            bass.AP(d_bdc.tensor, p * SUB_C,
                                    [[CHUNK * PP, H * SUBB],
                                     [CTX, CHUNK], [1, CTX]])), "hw")
                        w8(y, t_shr[p], pv_s1[p])
                        t_rdb2[p] = fin(y.dma_start(
                            bass.AP(bdbs2[p], 0,
                                    [[H * KTW, SUBT], [KTW, H], [1, KTW]]),
                            bass.AP(d_bdb.tensor, p * SUB_B,
                                    [[KTW, SUBT], [BSTRIDE, H], [1, KTW]])),
                            "hw")
                    SYN(_)

                emit_bd(0)
                for su in range(nsub):
                    wpump(4)
                    p_su = su % 2
                    s0 = su * SUBT
                    w0 = sh + s0 - 12
                    if su + 1 < nsub:
                        emit_bd(su + 1)
                    t_bd = t_bd_h[p_su]

                    vbuf = v_s0 if su % 2 == 0 else v_s1

                    def _pe_v(t, i, ps, w0=w0):
                        if i == 0:
                            w8(t, tok_const["wv"], t_proj[12][1], t_proj[13][1])
                            w8(t, *prev_v_evac)
                        for dc in range(8):
                            last = t.matmul(ps[0:KTW, 0:512],
                                            xn[:, dc, w0:w0 + KTW],
                                            wv_s[:, dc, i * 512:(i + 1) * 512],
                                            start=(dc == 0), stop=(dc == 7))
                        return last

                    def _ev_v(v, i, ps, vbuf=vbuf):
                        return v.tensor_copy(
                            vbuf[0:KTW, i * 512:(i + 1) * 512], ps[0:KTW, 0:512])

                    t_vp = pingpong(2, [PSA, PSB], _pe_v, _ev_v, "dve",
                                    [], [t_proj[12][1], t_proj[13][1]]
                                    + list(prev_rb_acts))
                    prev_v_evac = [t_vp[0][1], t_vp[1][1]]

                    def _pe_s(t, i, ps, s0=s0, t_bd=t_bd):
                        if i < 2:
                            w8(t, t_bd[H - 2 + i][1])
                        return t.matmul(ps[0:SUBT, 0:KTW],
                                        q_st[:, i, s0:s0 + SUBT],
                                        k_st[:, i, s0:s0 + KTW],
                                        start=True, stop=True)

                    def _ev_s(v, i, ps):
                        return v.tensor_copy(scst[0:SUBT, i, :],
                                             ps[0:SUBT, 0:KTW])

                    t_sc = pingpong(H, [PSC, PSD], _pe_s, _ev_s, "dve",
                                    [], list(cd_users))
                    cd_users[:] = [t_sc[H - 2][1], t_sc[H - 1][1]]
                    pv_qk_pe[0] = t_sc[H - 1][0]

                    t_s1 = []

                    def _(v, p_su=p_su):
                        w8(v, t_sc[H - 1][1], t_rdb2[p_su], pv_tanh[0])
                        t_s1.append(fin(v.tensor_add(
                            s1st[0:SUBT, :, :].rearrange("p a b -> p (a b)"),
                            scst[0:SUBT, :, :].rearrange("p a b -> p (a b)"),
                            bdbs2[p_su][0:SUBT, :, :].rearrange(
                                "p a b -> p (a b)")), "dve"))
                    VEC(_)
                    pv_s1[p_su] = t_s1[0]

                    t_tanh = []

                    def _(sc):
                        w8(sc, t_s1[0])
                        t_tanh.append(fin(sc.activation(
                            ths[0:SUBT, :, :].rearrange("p a b -> p (a b)"),
                            s1st[0:SUBT, :, :].rearrange("p a b -> p (a b)"),
                            AF.Tanh, scale=1.0 / SOFTCAP), "act"))
                    SCL(_)
                    pv_tanh[0] = t_tanh[0]

                    def _pe_te(t, i, pt):
                        if i == 0:
                            w8(t, t_tanh[0])
                        return t.transpose(pt[0:KTW, 0:SUBT],
                                           ths[0:SUBT, i, :],
                                           idb[0:SUBT, 0:SUBT])

                    def _ev_te(sc, i, pt):
                        return sc.activation(emst[:, i, :], pt[0:KTW, 0:SUBT],
                                             AF.Exp, scale=SOFTCAP)

                    t_te = pingpong(H, PTS, _pe_te, _ev_te, "act",
                                    [], [t_tanh[0]])

                    t_mask = []

                    def _(v):
                        w8(v, t_te[H - 1][1], tok_const["mask"])
                        t_mask.append(fin(v.tensor_mul(
                            emst[:].rearrange("p a b -> p (a b)"),
                            emst[:].rearrange("p a b -> p (a b)"),
                            maskb[:]), "dve"))
                    VEC(_)

                    t_den = []

                    def _(t):
                        w8(t, t_mask[0], tok_const["o120b"])
                        w8(t, *cd_users)
                        t.matmul(PSC[0:1, 0:432], o120b[0:KTW, 0:1],
                                 emst[:].rearrange("p a b -> p (a b)")[:, 0:432],
                                 start=True, stop=True)
                        _m2 = t.matmul(
                            PSD[0:1, 0:432], o120b[0:KTW, 0:1],
                            emst[:].rearrange("p a b -> p (a b)")[:, 432:864],
                            start=True, stop=True)
                        t_den.append(fin(_m2, "pe"))
                    blk.tensor(_)

                    def _(v):
                        w8(v, t_den[0])
                        v.reciprocal(rcr[0:1, 0:432], PSC[0:1, 0:432])
                        t_den.append(fin(v.reciprocal(rcr[0:1, 432:864],
                                                      PSD[0:1, 0:432]), "dve"))
                    VEC(_)
                    cd_users[:] = [t_den[1]]

                    def _pe_rb(t, i, ps):
                        if i == 0:
                            w8(t, t_den[1], tok_const["oc_b"])
                        return t.matmul(ps[0:128, 0:432], oc_b[:],
                                        rcr[0:1, i * 432:(i + 1) * 432],
                                        start=True, stop=True)

                    def _ev_rb(v, i, ps):
                        return v.tensor_copy(rbst[:, i * 432:(i + 1) * 432],
                                             ps[0:128, 0:432])

                    wpump(4)
                    t_rb = pingpong(2, [PSA, PSB], _pe_rb, _ev_rb, "dve",
                                    [], [t_vp[0][1], t_vp[1][1]])
                    prev_rb_acts = [t_rb[0][1], t_rb[1][1]]

                    def _pe_pv(t, i, ps, vbuf=vbuf):
                        if i == 0:
                            w8(t, t_rb[1][1])
                        return t.matmul(ps[0:128, 0:SUBT],
                                        vbuf[0:KTW, i * 128:(i + 1) * 128],
                                        emst[:, i, :], start=True, stop=True)

                    def _ev_pv(v, i, ps, s0=s0):
                        return v.tensor_mul(
                            a_st[:, i, s0:s0 + SUBT], ps[0:128, 0:SUBT],
                            rbst[:, i * SUBT:(i + 1) * SUBT])

                    t_pv = pingpong(H, [PSC, PSD], _pe_pv, _ev_pv, "dve",
                                    [], list(cd_users))
                    cd_users[:] = [t_pv[H - 2][1], t_pv[H - 1][1]]
                    prev_pv_evac = [t_pv[H - 2][1], t_pv[H - 1][1]]

                def _pe_wp(t, i, ps, T=T):
                    if i == 0:
                        w8(t, t_pv[H - 1][1], tok_const["wp"])
                        w8(t, *prev_rb_acts)
                    for dci in range(8):
                        last = t.matmul(ps[0:128, 0:T],
                                        wp_s[:, dci, i * 128:(i + 1) * 128],
                                        a_st[:, dci, 0:T], start=(dci == 0),
                                        stop=(dci == 7))
                    return last

                def _ev_wp(v, i, ps, T=T):
                    return v.tensor_copy(y_fm[:, i, 0:T], ps[0:128, 0:T])

                t_wp = pingpong(8, [PSA, PSB], _pe_wp, _ev_wp, "dve",
                                [], list(prev_rb_acts))
                prev_rb_acts = []
                pv_wp_ev[:] = [t_wp[6][1], t_wp[7][1]]

                a = 0
                while a < T:
                    wpump(4)
                    cnt = min(128, T - a)

                    def _pe_yt(t, i, pt, a=a, cnt=cnt, prev_ep=list(prev_ep)):
                        if i == 0:
                            w8(t, t_wp[7][1], *prev_ep)
                        return t.transpose(pt[0:cnt, 0:128],
                                           y_fm[:, i, a:a + cnt], idb[:])

                    def _ev_yt(v, i, pt, cnt=cnt):
                        return v.tensor_copy(
                            y_tm[0:cnt, i * 128:(i + 1) * 128], pt[0:cnt, 0:128])

                    t_yt = pingpong(8, PTS, _pe_yt, _ev_yt, "dve",
                                    [], list(prev_tr))
                    prev_tr = [t_yt[4][1], t_yt[5][1], t_yt[6][1], t_yt[7][1]]

                    t_xr = []

                    def _(g, S=S, a=a, cnt=cnt, prev_ep=list(prev_ep)):
                        w8(g, *prev_ep)
                        t_xr.append(fin(g.dma_start(
                            xt0[0:cnt, :],
                            d_x[HALO + S + a:HALO + S + a + cnt, :]), "gp"))
                    GPS(_)

                    sq_tok, a_tok, d_tok = emit_ln_stats(
                        y_tm[0:cnt, :], cnt, xnt, [t_yt[7][1]],
                        apply_out=lny[0:cnt, :])
                    t_ep = [a_tok]

                    def _(v, cnt=cnt):
                        w8(v, t_ep[0], t_xr[0])
                        t_ep.append(fin(v.tensor_add(
                            xpt[0:cnt, :], lny[0:cnt, :], xt0[0:cnt, :]), "dve"))
                    VEC(_)

                    def _(g, S=S, a=a, cnt=cnt):
                        w8(g, t_ep[1])
                        t_ep.append(fin(g.dma_start(
                            bass.AP(d_xp.tensor, (S + a) * HS,
                                    [[HS, cnt], [1, HS]]), xpt[0:cnt, :]), "gp"))
                    GPS(_)
                    prev_ep = [t_ep[1], t_ep[2], sq_tok]
                    a += cnt

                last_super = now()

            while w_ci[0] < len(WUNITS):
                wpump(6)
            tok_A = now()

        tok_W = now()

        with contextlib.ExitStack() as esB:
            w1q = esB.enter_context(nc.sbuf_tensor("w1q_s", [128, 8, FF], BF16))
            xpb, xnrb, hscb, xqtmb, xqfmb, svrb, svcb = \
                [], [], [], [], [], [], []
            htmb, hqtmb, hqfmb = [], [], []
            for p in range(2):
                xpb.append(esB.enter_context(
                    nc.sbuf_tensor("xp%d_s" % p, [128, HS], F32)))
                xnrb.append(esB.enter_context(
                    nc.sbuf_tensor("xnr%d_s" % p, [128, HS], F32)))
                hscb.append(esB.enter_context(
                    nc.sbuf_tensor("hsc%d_s" % p, [128, 16], F32)))
                xqtmb.append(esB.enter_context(
                    nc.sbuf_tensor("xqtm%d_s" % p, [128, HS], BF16)))
                xqfmb.append(esB.enter_context(
                    nc.sbuf_tensor("xqfm%d_s" % p, [128, 8, 128], BF16)))
                svrb.append(esB.enter_context(
                    nc.sbuf_tensor("svr%d_s" % p, [1, 128], BF16)))
                svcb.append(esB.enter_context(
                    nc.sbuf_tensor("svc%d_s" % p, [128, 1], BF16)))
                htmb.append(esB.enter_context(
                    nc.sbuf_tensor("htm%d_s" % p, [128, FF], F32)))
                hqtmb.append(esB.enter_context(
                    nc.sbuf_tensor("hqtm%d_s" % p, [128, FF], BF16)))
                hqfmb.append(esB.enter_context(
                    nc.sbuf_tensor("hqfm%d_s" % p, [128, 32, 128], BF16)))
            yxs = esB.enter_context(nc.sbuf_tensor("yxs_s", [128, HS], F32))
            q0xs = esB.enter_context(nc.sbuf_tensor("q0xs_s", [128, HS], I32))
            yhfs = esB.enter_context(nc.sbuf_tensor("yhfs_s", [128, FF], F32))
            q0hs = esB.enter_context(nc.sbuf_tensor("q0hs_s", [128, FF], I32))

            t_w1l = []

            def _(g):
                w8(g, *w_done[0])
                t_w1l.append(fin(g.dma_start(
                    bass.AP(w1q, 0, [[8 * FF, 128], [FF, 8], [1, FF]]),
                    bass.AP(d_wq1.tensor, 0, [[FF, 128], [128 * FF, 8], [1, FF]])),
                    "gp"))
            GPS(_)

            t_xpl = [None] * NT
            t_store = []

            def _(g):
                w8(g, *tok_W)
                for c2 in range(2):
                    t0, cnt0 = BTILES[c2]
                    t_xpl[c2] = fin(g.dma_start(
                        xpb[c2][0:cnt0, :],
                        bass.AP(d_xp.tensor, t0 * HS, [[HS, cnt0], [1, HS]])),
                        "gp")
            GPS(_)

            prev_tr = []
            t_tapyh = [None]
            pv_stats_free = [[], []]
            pv_hfree = [[], []]
            pv_xtr_pe = [None, None]
            pv_htr_pe = [None, None]
            pv_w1_tail = []
            pv_sv_pe = [None, None]
            pv_xnr_dve = [[], []]
            pv_rnd_x = [None]
            pv_rnd_h = [None]
            pv_w1_pe = [None, None]
            pv_hq_store = [None, None]
            S_tq = [None] * NT
            S_vec = [None] * NT
            X_sv = [None] * NT
            X_tr = [None] * NT
            W_ev = [None] * NT
            H_tq = [None] * NT

            S_dt = [None] * NT
            S_bv = [None] * NT

            def emit_S_a(k):
                p = k % 2
                co = 32 * p
                t0, cnt = BTILES[k]
                S_dt[k] = emit_ln_sq(xpb[p][0:cnt, :], cnt, xnrb[p],
                                     [t_xpl[k]], co=co, c_extra=True,
                                     minmax=True)

            def emit_S_b(k):
                p = k % 2
                co = 32 * p
                t0, cnt = BTILES[k]
                a_tok = emit_ln_fin(None, cnt, S_dt[k][1], co=co,
                                    c_extra=True)
                t_x1 = [a_tok]

                def _(v, cnt=cnt, k=k, p=p, co=co):
                    w8(v, t_x1[0], *tok_W)
                    w8(v, pv_sv_pe[p])
                    v.tensor_sub(st[0:cnt, co + 19:co + 20],
                                 st[0:cnt, co + 17:co + 18],
                                 st[0:cnt, co + 2:co + 3])
                    v.tensor_sub(st[0:cnt, co + 20:co + 21],
                                 st[0:cnt, co + 2:co + 3],
                                 st[0:cnt, co + 18:co + 19])
                    v.tensor_max(st[0:cnt, co + 20:co + 21],
                                 st[0:cnt, co + 20:co + 21],
                                 st[0:cnt, co + 19:co + 20])
                    v.tensor_mul(st[0:cnt, co + 20:co + 21],
                                 st[0:cnt, co + 20:co + 21],
                                 st[0:cnt, co + 15:co + 16])
                    v.tensor_scalar_max(absx[0:cnt, k:k + 1],
                                        st[0:cnt, co + 20:co + 21], 1e-5)
                    v.reciprocal(st[0:cnt, co + 21:co + 22],
                                 absx[0:cnt, k:k + 1])
                    v.tensor_mul(st[0:cnt, co + 22:co + 23],
                                 absx[0:cnt, k:k + 1],
                                 st[0:cnt, co + 21:co + 22])
                    v.tensor_scalar(st[0:cnt, co + 22:co + 23],
                                    st[0:cnt, co + 22:co + 23], -1.0, 2.0,
                                    ALU.mult, ALU.add)
                    v.tensor_mul(st[0:cnt, co + 21:co + 22],
                                 st[0:cnt, co + 21:co + 22],
                                 st[0:cnt, co + 22:co + 23])
                    v.tensor_scalar_mul(st[0:cnt, co + 23:co + 24],
                                        st[0:cnt, co + 21:co + 22], 127.0)
                    v.tensor_mul(st[0:cnt, co + 28:co + 29],
                                 st[0:cnt, co + 15:co + 16],
                                 st[0:cnt, co + 23:co + 24])
                    v.tensor_mul(st[0:cnt, co + 29:co + 30],
                                 st[0:cnt, co + 14:co + 15],
                                 st[0:cnt, co + 23:co + 24])
                    v.reciprocal(st[0:cnt, co + 24:co + 25], mu1b[0:cnt, 0:1])
                    v.tensor_mul(st[0:cnt, co + 25:co + 26],
                                 st[0:cnt, co + 21:co + 22],
                                 st[0:cnt, co + 24:co + 25])
                    v.tensor_scalar_mul(st[0:cnt, co + 25:co + 26],
                                        st[0:cnt, co + 25:co + 26], 254.0)
                    v.tensor_copy(svcb[p][0:cnt, 0:1],
                                  st[0:cnt, co + 25:co + 26])
                    v.tensor_mul(st[0:cnt, co + 26:co + 27],
                                 absx[0:cnt, k:k + 1], mu1b[0:cnt, 0:1])
                    t_x1.append(fin(v.tensor_scalar_mul(
                        st[0:cnt, co + 26:co + 27],
                        st[0:cnt, co + 26:co + 27], 1.0 / 254.0), "dve"))
                VEC(_)
                S_bv[k] = t_x1[1]

            def emit_S_c(k):
                p = k % 2
                co = 32 * p
                t0, cnt = BTILES[k]
                t_yx = []

                def _(sc, cnt=cnt, p=p, co=co):
                    w8(sc, S_bv[k], pv_rnd_x[0])
                    t_yx.append(fin(sc.activation(
                        yxs[0:cnt, :], xpb[p][0:cnt, :], AF.Identity,
                        scale=st[0:cnt, co + 28:co + 29],
                        bias=st[0:cnt, co + 29:co + 30]), "act"))
                SCL(_)
                pv_stats_free[p] = [S_dt[k][0], S_dt[k][1], S_bv[k], t_yx[0]]

                t_q = emit_round(yxs[0:cnt, :], q0xs[0:cnt, :],
                                 xqtmb[p][0:cnt, :], [t_yx[0], pv_xtr_pe[p]])
                pv_rnd_x[0] = t_q
                S_tq[k] = t_q
                S_vec[k] = S_bv[k]

                if k + 2 < NT:
                    def _(g, k=k, p=p):
                        w8(g, *pv_stats_free[p])
                        t2, cnt2 = BTILES[k + 2]
                        t_xpl[k + 2] = fin(g.dma_start(
                            xpb[p][0:cnt2, :],
                            bass.AP(d_xp.tensor, t2 * HS,
                                    [[HS, cnt2], [1, HS]])), "gp")
                    GPS(_)

            def emit_X(i):
                nonlocal prev_tr
                p = i % 2
                t0, cnt = BTILES[i]

                def _pe_xt(t, j, pt, cnt=cnt, i=i, p=p):
                    if j == 0:
                        w8(t, S_tq[i])
                    return t.transpose(pt[0:128, 0:cnt],
                                       xqtmb[p][0:cnt, j * 128:(j + 1) * 128],
                                       idb[0:cnt, 0:cnt])

                def _ev_xt(v, j, pt, cnt=cnt, p=p):
                    return v.tensor_copy(xqfmb[p][:, j, 0:cnt],
                                         pt[0:128, 0:cnt])

                t_xtr = pingpong(8, PTS, _pe_xt, _ev_xt, "dve",
                                 [pv_w1_pe[p]], list(prev_tr))
                prev_tr = [t_xtr[j][1] for j in range(4, 8)]
                pv_xtr_pe[p] = t_xtr[7][0]
                X_tr[i] = t_xtr

                t_sv = []

                def _(t, cnt=cnt, i=i, p=p):
                    w8(t, S_vec[i], t_xtr[6][1], t_xtr[7][1])
                    _m = t.transpose(PT1[0:1, 0:cnt], svcb[p][0:cnt, 0:1],
                                     idb[0:cnt, 0:cnt])
                    t_sv.append(fin(_m, "pe"))
                blk.tensor(_)
                pv_sv_pe[p] = t_sv[0]

                def _(v, cnt=cnt, p=p):
                    w8(v, t_sv[0], pv_w1_pe[p])
                    t_sv.append(fin(v.tensor_copy(svrb[p][0:1, 0:cnt],
                                                  PT1[0:1, 0:cnt]), "dve"))
                VEC(_)
                prev_tr = prev_tr[1:] + [t_sv[1]]
                X_sv[i] = t_sv[1]

            def _mk_w1(i, off):
                p = i % 2
                co = 32 * p
                t0, cnt = BTILES[i]

                def _pe(t, j, ps, cnt=cnt, i=i, p=p, off=off):
                    fs = off + j
                    if j == 0 and off == 0:
                        w8(t, X_tr[i][7][1], X_sv[i], t_w1l[0],
                           tok_const["b1"])
                    for dc in range(8):
                        t.matmul(ps[0:cnt, 0:512], xqfmb[p][:, dc, 0:cnt],
                                 w1q[:, dc, fs * 512:(fs + 1) * 512],
                                 start=(dc == 0), stop=False)
                    return t.matmul(ps[0:cnt, 0:512], svrb[p][0:1, 0:cnt],
                                    b1r[0:1, fs * 512:(fs + 1) * 512],
                                    start=False, stop=True)

                def _ev(sc, j, ps, cnt=cnt, p=p, co=co, off=off):
                    fs = off + j
                    return sc.activation(
                        htmb[p][0:cnt, fs * 512:(fs + 1) * 512],
                        ps[0:cnt, 0:512], AF.Gelu,
                        scale=st[0:cnt, co + 26:co + 27])
                return _pe, _ev

            def emit_W1(i):
                p = i % 2
                pe, ev = _mk_w1(i, 0)
                t_h = pingpong(4, [PSA, PSB, PSC, PSD], pe, ev, "act",
                               [S_vec[i]] + list(pv_hfree[p]),
                               list(pv_w1_tail))
                W_ev[i] = [t_h[j][1] for j in range(4)]

            def emit_W2(i):
                p = i % 2
                pe, ev = _mk_w1(i, 4)
                t_h = pingpong(4, [PSA, PSB, PSC, PSD], pe, ev, "act",
                               [], list(W_ev[i]))
                W_ev[i] = W_ev[i] + [t_h[j][1] for j in range(4)]
                pv_w1_tail[:] = [t_h[j][1] for j in range(4)]
                pv_w1_pe[p] = t_h[3][0]

            def emit_H(i):
                p = i % 2
                co = 32 * p
                t0, cnt = BTILES[i]
                t_hst = []

                def _(sc, cnt=cnt, i=i, p=p, co=co):
                    w8(sc, *W_ev[i])
                    w8(sc, t_tapyh[0])
                    for fs in range(8):
                        sc.activation(xnrb[p][0:cnt, 0:512],
                                      htmb[p][0:cnt, fs * 512:(fs + 1) * 512],
                                      AF.Square,
                                      accum_out=hscb[p][0:cnt, fs:fs + 1])
                    t_hst.append(fin(sc.activation(
                        xnrb[p][0:cnt, 0:1], hscb[p][0:cnt, 0:1],
                        AF.Identity), "act"))
                SCL(_)

                def _(v, cnt=cnt, p=p, co=co):
                    w8(v, t_hst[0])
                    v.reduce_sum(st[0:cnt, co + 1:co + 2],
                                 hscb[p][0:cnt, 0:8], axis=AX)
                    v.reduce_max(st[0:cnt, co + 20:co + 21],
                                 htmb[p][0:cnt, :], axis=AX,
                                 apply_absolute_value=True)
                    t_hst.append(fin(v.tensor_scalar_mul(
                        st[0:cnt, co + 4:co + 5], st[0:cnt, co + 1:co + 2],
                        1.0 / FF), "dve"))
                VEC(_)

                def _(sc, cnt=cnt, co=co):
                    w8(sc, t_hst[1])
                    sc.activation(st[0:cnt, co + 5:co + 6],
                                  st[0:cnt, co + 4:co + 5], AF.Ln,
                                  bias=cst[0:cnt, 3:4])
                    t_hst.append(fin(sc.activation(
                        st[0:cnt, co + 6:co + 7], st[0:cnt, co + 5:co + 6],
                        AF.Exp, scale=-0.5), "act"))
                SCL(_)

                def _(v, cnt=cnt, i=i, p=p, co=co):
                    w8(v, t_hst[2])
                    v.tensor_mul(st[0:cnt, co + 9:co + 10],
                                 st[0:cnt, co + 6:co + 7],
                                 st[0:cnt, co + 20:co + 21])
                    v.tensor_scalar_max(absh[0:cnt, i:i + 1],
                                        st[0:cnt, co + 9:co + 10], 1e-5)
                    v.reciprocal(st[0:cnt, co + 21:co + 22],
                                 absh[0:cnt, i:i + 1])
                    v.tensor_mul(st[0:cnt, co + 22:co + 23],
                                 absh[0:cnt, i:i + 1],
                                 st[0:cnt, co + 21:co + 22])
                    v.tensor_scalar(st[0:cnt, co + 22:co + 23],
                                    st[0:cnt, co + 22:co + 23], -1.0, 2.0,
                                    ALU.mult, ALU.add)
                    v.tensor_mul(st[0:cnt, co + 21:co + 22],
                                 st[0:cnt, co + 21:co + 22],
                                 st[0:cnt, co + 22:co + 23])
                    v.tensor_scalar_mul(st[0:cnt, co + 23:co + 24],
                                        st[0:cnt, co + 21:co + 22], 127.0)
                    t_hst.append(fin(v.tensor_mul(
                        st[0:cnt, co + 27:co + 28], st[0:cnt, co + 6:co + 7],
                        st[0:cnt, co + 23:co + 24]), "dve"))
                VEC(_)

                t_yh = []

                def _(sc, cnt=cnt, p=p, co=co):
                    w8(sc, t_hst[3], pv_htr_pe[p], t_tapyh[0], pv_rnd_h[0])
                    t_yh.append(fin(sc.activation(
                        yhfs[0:cnt, :], htmb[p][0:cnt, :], AF.Identity,
                        scale=st[0:cnt, co + 27:co + 28]), "act"))
                SCL(_)
                if DEBUG_TAPS:
                    def _(g, i=i, cnt=cnt, t_yh=t_yh):
                        w8(g, t_yh[0])
                        t_tapyh[0] = fin(g.dma_start(
                            bass.AP(d_tyh.tensor, i * 128 * FF,
                                    [[FF, cnt], [1, FF]]),
                            yhfs[0:cnt, :]), "gp")
                    GPS(_)
                t_hq_last = emit_round(yhfs[0:cnt, :], q0hs[0:cnt, :],
                                       hqtmb[p][0:cnt, :], [t_yh[0]])
                pv_rnd_h[0] = t_hq_last
                pv_hfree[p] = [t_hst[1], t_hq_last]
                H_tq[i] = t_hq_last

            def emit_T(i):
                nonlocal prev_tr
                p = i % 2
                t0, cnt = BTILES[i]

                def _pe_ht(t, j, pt, cnt=cnt, i=i, p=p):
                    if j == 0:
                        w8(t, H_tq[i])
                    return t.transpose(pt[0:128, 0:cnt],
                                       hqtmb[p][0:cnt, j * 128:(j + 1) * 128],
                                       idb[0:cnt, 0:cnt])

                def _ev_ht(v, j, pt, cnt=cnt, p=p):
                    return v.tensor_copy(hqfmb[p][:, j, 0:cnt],
                                         pt[0:128, 0:cnt])

                t_htr = pingpong(32, PTS, _pe_ht, _ev_ht, "dve",
                                 [pv_hq_store[p]], list(prev_tr))
                prev_tr = [t_htr[j][1] for j in range(28, 32)]
                pv_htr_pe[p] = t_htr[31][0]

                def _(g, i=i, p=p):
                    w8(g, t_htr[31][1])
                    t_store.append(fin(g.dma_start(
                        bass.AP(d_hq.tensor, i * 128 * FF,
                                [[FF, 128], [1, FF]]),
                        hqfmb[p][:].rearrange("p a b -> p (a b)")), "gp"))
                    pv_hq_store[p] = t_store[-1]
                GPS(_)

            emit_S_a(0)
            emit_S_b(0)
            emit_S_c(0)
            for i in range(NT):
                if i + 1 < NT:
                    emit_S_a(i + 1)
                emit_X(i)
                emit_W1(i)
                if i + 1 < NT:
                    emit_S_b(i + 1)
                emit_W2(i)
                if i + 1 < NT:
                    emit_S_c(i + 1)
                if i >= 1:
                    emit_T(i - 1)
                emit_H(i)
            emit_T(NT - 1)

            tok_B1 = now()

        with contextlib.ExitStack() as esB2:
            w2q = esB2.enter_context(nc.sbuf_tensor("w2q_s", [128, 32, HS], BF16))
            hqb, xprb, x2tb, ln2b, outtb, sv2rb, sv2cb = \
                [], [], [], [], [], [], []
            for p in range(2):
                hqb.append(esB2.enter_context(
                    nc.sbuf_tensor("hq%d_s" % p, [128, 32, 128], BF16)))
                xprb.append(esB2.enter_context(
                    nc.sbuf_tensor("xpr%d_s" % p, [128, HS], F32)))
                x2tb.append(esB2.enter_context(
                    nc.sbuf_tensor("x2t%d_s" % p, [128, HS], F32)))
                ln2b.append(esB2.enter_context(
                    nc.sbuf_tensor("ln2%d_s" % p, [128, HS], F32)))
                outtb.append(esB2.enter_context(
                    nc.sbuf_tensor("outt%d_s" % p, [128, HS], F32)))
                sv2rb.append(esB2.enter_context(
                    nc.sbuf_tensor("sv2r%d_s" % p, [1, 128], BF16)))
                sv2cb.append(esB2.enter_context(
                    nc.sbuf_tensor("sv2c%d_s" % p, [128, 1], BF16)))

            t_w2l = []

            def _(g):
                w8(g, *w_done[1])
                w8(g, *tok_B1)
                t_w2l.append(fin(g.dma_start(
                    bass.AP(w2q, 0, [[32 * HS, 128], [HS, 32], [1, HS]]),
                    bass.AP(d_wq2.tensor, 0,
                            [[HS, 128], [128 * HS, 32], [1, HS]])), "gp"))
            GPS(_)

            t_hql = [None] * NT
            t_xpl2 = [None] * NT
            t_out = []
            prev_tr2 = []
            pv_sv2_pe = [None, None]
            pv_x2_free = [[], []]
            pv_out_dve = [None, None]
            pv_w2_tail = []
            pv_w2_pe = [None, None]
            pv_out_store = [None, None]

            def _(g):
                w8(g, *tok_B1)
                for c2 in range(2):
                    t0, cnt0 = BTILES[c2]
                    t_hql[c2] = fin(g.dma_start(
                        hqb[c2][:].rearrange("p a b -> p (a b)"),
                        bass.AP(d_hq.tensor, c2 * 128 * FF,
                                [[FF, 128], [1, FF]])), "gp")
                    t_xpl2[c2] = fin(g.dma_start(
                        xprb[c2][0:cnt0, :],
                        bass.AP(d_xp.tensor, t0 * HS, [[HS, cnt0], [1, HS]])),
                        "gp")
            GPS(_)

            S2_tok = [None] * NT

            def emit_P2(k):
                nonlocal prev_tr2
                p = k % 2
                co = 32 * p
                t0, cnt = BTILES[k]
                t_s2 = []

                def _(v, cnt=cnt, k=k, p=p, co=co):
                    w8(v, *tok_B1)
                    w8(v, pv_sv2_pe[p], pv_out_dve[p])
                    v.reciprocal(st[0:cnt, co + 21:co + 22],
                                 absh[0:cnt, k:k + 1])
                    v.tensor_mul(st[0:cnt, co + 22:co + 23],
                                 absh[0:cnt, k:k + 1],
                                 st[0:cnt, co + 21:co + 22])
                    v.tensor_scalar(st[0:cnt, co + 22:co + 23],
                                    st[0:cnt, co + 22:co + 23], -1.0, 2.0,
                                    ALU.mult, ALU.add)
                    v.tensor_mul(st[0:cnt, co + 21:co + 22],
                                 st[0:cnt, co + 21:co + 22],
                                 st[0:cnt, co + 22:co + 23])
                    v.reciprocal(st[0:cnt, co + 24:co + 25], mu2b[0:cnt, 0:1])
                    v.tensor_mul(st[0:cnt, co + 25:co + 26],
                                 st[0:cnt, co + 21:co + 22],
                                 st[0:cnt, co + 24:co + 25])
                    v.tensor_scalar_mul(st[0:cnt, co + 25:co + 26],
                                        st[0:cnt, co + 25:co + 26], 254.0)
                    v.tensor_copy(sv2cb[p][0:cnt, 0:1],
                                  st[0:cnt, co + 25:co + 26])
                    v.tensor_mul(st[0:cnt, co + 26:co + 27],
                                 absh[0:cnt, k:k + 1], mu2b[0:cnt, 0:1])
                    t_s2.append(fin(v.tensor_scalar_mul(
                        st[0:cnt, co + 26:co + 27],
                        st[0:cnt, co + 26:co + 27], 1.0 / 254.0), "dve"))
                VEC(_)

                def _(t, cnt=cnt, p=p):
                    w8(t, t_s2[0], *prev_tr2)
                    _m = t.transpose(PT1[0:1, 0:cnt], sv2cb[p][0:cnt, 0:1],
                                     idb[0:cnt, 0:cnt])
                    t_s2.append(fin(_m, "pe"))
                blk.tensor(_)
                pv_sv2_pe[p] = t_s2[1]

                def _(v, cnt=cnt, p=p):
                    w8(v, t_s2[1], pv_w2_pe[p])
                    t_s2.append(fin(v.tensor_copy(sv2rb[p][0:1, 0:cnt],
                                                  PT1[0:1, 0:cnt]), "dve"))
                VEC(_)
                prev_tr2 = [t_s2[2]]
                S2_tok[k] = t_s2

            T_x2 = [None] * NT

            def emit_M2(i):
                p = i % 2
                co = 32 * p
                t0, cnt = BTILES[i]

                def _pe_w2(t, j, ps, cnt=cnt, i=i, p=p):
                    if j == 0:
                        w8(t, t_hql[i], t_w2l[0], S2_tok[i][2],
                           tok_const["b2"])
                    for fc in range(32):
                        t.matmul(ps[0:cnt, 0:512], hqb[p][:, fc, 0:cnt],
                                 w2q[:, fc, j * 512:(j + 1) * 512],
                                 start=(fc == 0), stop=False)
                    return t.matmul(ps[0:cnt, 0:512], sv2rb[p][0:1, 0:cnt],
                                    b2r[0:1, j * 512:(j + 1) * 512],
                                    start=False, stop=True)

                def _ev_w2(sc, j, ps, cnt=cnt, p=p, co=co):
                    return sc.activation(
                        x2tb[p][0:cnt, j * 512:(j + 1) * 512],
                        ps[0:cnt, 0:512], AF.Identity,
                        scale=st[0:cnt, co + 26:co + 27])

                t_x2 = pingpong(2, [PSA, PSB], _pe_w2, _ev_w2, "act",
                                [S2_tok[i][0]] + list(pv_x2_free[p]),
                                list(pv_w2_tail))
                pv_w2_tail[:] = [t_x2[0][1], t_x2[1][1]]
                pv_w2_pe[p] = t_x2[1][0]
                T_x2[i] = t_x2

            def emit_Q2(i):
                p = i % 2
                co = 32 * p
                t0, cnt = BTILES[i]
                t_x2 = T_x2[i]
                sq_tok, a_tok, d_tok = emit_ln_stats(
                    x2tb[p][0:cnt, :], cnt, ln2b[p],
                    [t_x2[0][1], t_x2[1][1], pv_out_dve[p]], co=co,
                    apply_out=ln2b[p][0:cnt, :])
                t_fin = [a_tok]

                def _(v, cnt=cnt, i=i, p=p):
                    w8(v, t_fin[0], t_xpl2[i], pv_out_store[p])
                    t_fin.append(fin(v.tensor_add(
                        outtb[p][0:cnt, :], ln2b[p][0:cnt, :],
                        xprb[p][0:cnt, :]), "dve"))
                VEC(_)
                pv_x2_free[p] = [sq_tok, d_tok, t_fin[0]]
                pv_out_dve[p] = t_fin[1]

                def _(g, t0=t0, cnt=cnt, p=p):
                    w8(g, t_fin[1])
                    t_out.append(fin(g.dma_start(
                        d_out[t0:t0 + cnt, :], outtb[p][0:cnt, :]), "gp"))
                    pv_out_store[p] = t_out[-1]
                GPS(_)

                if i + 2 < NT:
                    def _(g, i=i, sq_tok=sq_tok, t_fin=t_fin, t_x2=t_x2, p=p):
                        w8(g, t_x2[1][1], sq_tok, t_fin[1])
                        t2, cnt2 = BTILES[i + 2]
                        t_hql[i + 2] = fin(g.dma_start(
                            hqb[p][:].rearrange("p a b -> p (a b)"),
                            bass.AP(d_hq.tensor, (i + 2) * 128 * FF,
                                    [[FF, 128], [1, FF]])), "gp")
                        t_xpl2[i + 2] = fin(g.dma_start(
                            xprb[p][0:cnt2, :],
                            bass.AP(d_xp.tensor, t2 * HS,
                                    [[HS, cnt2], [1, HS]])), "gp")
                    GPS(_)

            emit_P2(0)
            for i in range(NT):
                emit_M2(i)
                if i + 1 < NT:
                    emit_P2(i + 1)
                emit_Q2(i)

            def _(g):
                for _q in ("gp", "hw"):
                    for _i in range(4):
                        _k = _q + str(_i)
                        g.wait_ge(SEMS[_k], C[_k])
            GPS(_)

    return nc


def _host_inputs(L0_attn_x, L0_attn_pos_emb, L0_attn_Wq, L0_attn_Wk, L0_attn_Wv,
                 L0_attn_Wpost, L0_attn_Wrel, L0_attn_per_dim_scale,
                 L0_ffn_w1, L0_ffn_b1, L0_ffn_w2, L0_ffn_b2):
    f = lambda a: np.ascontiguousarray(np.asarray(a), dtype=np.float32)
    x = f(L0_attn_x)
    B, S, _ = x.shape
    shards = []
    for b in range(B):
        for half in range(2):
            s = half * TQ
            sh = np.zeros((SHARD, HS), np.float32)
            lo = max(0, s - HALO)
            hi = min(S, s + SHARD - HALO)
            sh[lo - (s - HALO):hi - (s - HALO)] = x[b, lo:hi]
            shards.append(sh)

    mask = np.zeros((KTW, H, SUBT), np.float32)
    for j in range(SUBB):
        for qq in range(CHUNK):
            mask[12 * j:12 * j + CTX, :, 12 * j + qq] = 1.0
    mask8 = np.ascontiguousarray(mask.reshape(KTW, H * SUBT)).astype(bf)

    cstv = np.zeros((128, 8), np.float32)
    cstv[:, 0] = 0.5
    cstv[:, 1] = -0.5
    cstv[:, 2] = 1.0
    cstv[:, 3] = EPS

    common = dict(
        wq=f(L0_attn_Wq), wk=f(L0_attn_Wk), wv=f(L0_attn_Wv),
        wpost=f(L0_attn_Wpost), wrel=f(L0_attn_Wrel),
        posT=f(np.asarray(L0_attn_pos_emb).T),
        pds=f(L0_attn_per_dim_scale),
        w1T=f(np.asarray(L0_ffn_w1).T), w2T=f(np.asarray(L0_ffn_w2).T),
        b1=f(L0_ffn_b1), b2=f(L0_ffn_b2),
        cst=cstv,
        identb=np.eye(128, dtype=np.float32).astype(bf),
        onescol_b=np.ones((1, 128), np.float32).astype(bf),
        onescol_f=np.ones((1, 128), np.float32),
        ones128_f=np.ones((128, 1), np.float32),
        ones120_b=np.ones((128, 1), np.float32).astype(bf),
        mask8=mask8,
        bdz=np.zeros(2 * SUB_B, np.float32),
    )
    return [dict(common, xsh=shards[c]) for c in range(8)]


def kernel(**inputs):
    in_maps = _host_inputs(**inputs)
    nc = build_nc()
    res = run_bass_kernel_spmd(nc, in_maps, core_ids=list(range(8)))
    x = np.asarray(inputs["L0_attn_x"])
    B, S, _ = x.shape
    out = np.empty((B, S, HS), np.float32)
    for b in range(B):
        for half in range(2):
            s = half * TQ
            own = min(TQ, S - s)
            out[b, s:s + own] = np.asarray(
                res.results[2 * b + half]["out"])[0:own]
    return out
